# version 29
# baseline (speedup 1.0000x reference)
import numpy as np
from collections import deque

import concourse.bass as bass
import concourse.mybir as mybir
from concourse.bass_utils import run_bass_kernel_spmd
from contextlib import ExitStack

F32 = mybir.dt.float32
BF16 = mybir.dt.bfloat16
AF = mybir.ActivationFunctionType
ALU = mybir.AluOpType
AX = mybir.AxisListType

D = 1024
HD = 64
DL = 512
NFC = D // 128
LN_EPS = 1e-5
RMS_EPS = 1.1920929e-07
NEG_BIG = -1e30


class Sem:
    __slots__ = ("h", "n", "name")

    def __init__(self, nc, name):
        self.h = nc.alloc_semaphore(name)
        self.n = 0
        self.name = name


class Rel:
    __slots__ = ("w",)

    def __init__(self):
        self.w = []

    def set(self, *sems):
        self.w = [(s, s.n) for s in sems]

    def add(self, *sems):
        self.w += [(s, s.n) for s in sems]

    def waits(self):
        return list(self.w)


class Prog:
    ENGS = ("pe", "dv", "ac", "po", "sy")

    def __init__(self, nc):
        self.nc = nc
        self.ops = {e: [] for e in self.ENGS}

    def step(self, eng, fn, wait=(), inc=None, inck=1):
        ws = []
        for w in wait:
            if isinstance(w, Sem):
                ws.append((w, w.n))
            else:
                ws.append((w[0], w[1]))
        self.ops[eng].append((fn, ws, inc, inck))
        if inc is not None:
            inc.n += inck

    def emit(self, block):
        mapping = dict(pe=block.tensor, dv=block.vector, ac=block.scalar,
                       po=block.gpsimd, sy=block.sync)
        for eng in self.ENGS:
            ops = self.ops[eng]

            def body(E, ops=ops):
                last = {}
                for fn, ws, inc, inck in ops:
                    for s, cnt in ws:
                        if cnt > 0 and last.get(s.name, 0) < cnt:
                            E.wait_ge(s.h, cnt)
                            last[s.name] = cnt
                    instr = fn(E)
                    if inc is not None:
                        instr.then_inc(inc.h, inck)

            mapping[eng](body)


def build_program(B=4, T=2048, FF=4096, H=16, skip_attn=False, skip_ffn=False, skip_proj=False):
    assert T % 512 == 0 and FF % 256 == 0 and H % 2 == 0 and H <= 16
    NB = T // 512
    TC = T // 128
    HHALF = min(8, H)
    NGRP = (H + HHALF - 1) // HHALF
    GW = HHALF * HD
    NQT = 4
    NHC = FF // 128
    NOC = D // 128

    nc = bass.Bass("TRN2", target_bir_lowering=False)

    x_in = nc.dram_tensor("x", [B, T, D], F32, kind="ExternalInput").ap()
    wdq_d = nc.dram_tensor("WDQ", [DL, HD], F32, kind="ExternalInput").ap()
    wuq_d = nc.dram_tensor("WUQ", [HD, DL], F32, kind="ExternalInput").ap()
    wdkv_d = nc.dram_tensor("WDKV", [2 * DL, HD], F32, kind="ExternalInput").ap()
    wukv_d = nc.dram_tensor("WUKV", [2 * HD, 2 * DL], F32, kind="ExternalInput").ap()
    wqr_d = nc.dram_tensor("WQR", [HD, HD], F32, kind="ExternalInput").ap()
    wkr_d = nc.dram_tensor("WKR", [HD, HD], F32, kind="ExternalInput").ap()
    wo_d = nc.dram_tensor("WO", [HD, HD], F32, kind="ExternalInput").ap()
    qw_d = nc.dram_tensor("QNW", [HD], F32, kind="ExternalInput").ap()
    kw_d = nc.dram_tensor("KNW", [HD], F32, kind="ExternalInput").ap()
    w1_d = nc.dram_tensor("W1", [D, FF], F32, kind="ExternalInput").ap()
    b1_d = nc.dram_tensor("B1", [FF], F32, kind="ExternalInput").ap()
    w2_d = nc.dram_tensor("W2", [FF, D], F32, kind="ExternalInput").ap()
    b2_d = nc.dram_tensor("B2", [D], F32, kind="ExternalInput").ap()
    y_out = nc.dram_tensor("y", [B, T, D], F32, kind="ExternalOutput").ap()
    x2s = nc.dram_tensor("x2s", [B * T, D], F32, kind="Internal").ap()

    es = ExitStack()
    with es:
        P = Prog(nc)

        def sb(name, shape, dt=F32):
            return es.enter_context(nc.sbuf_tensor(name, shape, dt))

        id_bf = sb("id_bf", [128, 128], BF16)
        negmask = sb("negmask", [128, 128], F32)
        c0125 = sb("c0125", [128, 1], F32)
        c_lneps = sb("c_lneps", [128, 1], F32)
        c_rmseps = sb("c_rmseps", [128, 1], F32)
        c_nhalf = sb("c_nhalf", [128, 1], F32)
        c_inv64 = sb("c_inv64", [128, 1], F32)

        wqT2 = sb("wqT2", [128, HD], BF16)
        wkT2 = sb("wkT2", [128, HD], BF16)
        wvT2 = sb("wvT2", [128, HD], BF16)
        wqrT = sb("wqrT", [128, HD], BF16)
        wkrT = sb("wkrT", [128, HD], BF16)
        woT_pair = sb("woT_pair", [128, 128], BF16)
        qnw_col = sb("qnw_col", [HD, 1], F32)
        knw_col = sb("knw_col", [HD, 1], F32)
        b1col = sb("b1col", [128, NHC], F32)
        b2col = sb("b2col", [128, NOC], F32)

        x2_buf = sb("x2_buf", [128, 4 * D], F32)
        xln = [sb(f"xln{i}", [128, D], BF16) for i in range(2)]
        xn_bf2 = [sb(f"xn_bf{i}", [128, D], BF16) for i in range(2)]
        musum = sb("musum", [128, 1], F32)
        negmu = sb("negmu", [128, 1], F32)
        sqsum = sb("sqsum", [128, 1], F32)
        mu2 = sb("mu2", [128, 1], F32)
        varb = sb("varb", [128, 1], F32)
        lnv = sb("lnv", [128, 1], F32)
        rstd = sb("rstd", [128, 1], F32)

        ps_s0 = es.enter_context(nc.psum_tensor("ps_s0", [128, 512], F32))
        ps_s1 = es.enter_context(nc.psum_tensor("ps_s1", [128, 512], F32))
        ps_a = es.enter_context(nc.psum_tensor("ps_a", [128, 512], F32))
        ps_b = es.enter_context(nc.psum_tensor("ps_b", [128, 512], F32))
        ps_c = es.enter_context(nc.psum_tensor("ps_c", [128, 512], F32))
        ps_d = es.enter_context(nc.psum_tensor("ps_d", [128, 512], F32))
        pt_x = es.enter_context(nc.psum_tensor("pt_x", [128, 1024], BF16))
        pt_y = es.enter_context(nc.psum_tensor("pt_y", [128, 1024], BF16))

        d_w = Sem(nc, "d_w")
        d_x2 = Sem(nc, "d_x2")
        d_xl = [Sem(nc, f"d_xl{i}") for i in range(2)]
        d_g = Sem(nc, "d_g")
        po_set = Sem(nc, "po_set")
        po_chain = Sem(nc, "po_chain")
        pe_su = Sem(nc, "pe_su")
        dv_su = Sem(nc, "dv_su")
        pe_ln = Sem(nc, "pe_ln")
        dv_ln = Sem(nc, "dv_ln")
        ac_ln = Sem(nc, "ac_ln")
        pe_pr = Sem(nc, "pe_pr")
        ac_pr = Sem(nc, "ac_pr")
        dv_pr = Sem(nc, "dv_pr")
        pe_s = Sem(nc, "pe_s")
        dv_st = Sem(nc, "dv_st")
        ac_st = Sem(nc, "ac_st")
        po_st = Sem(nc, "po_st")
        ac_ex = Sem(nc, "ac_ex")
        pe_pt = Sem(nc, "pe_pt")
        dv_pt = Sem(nc, "dv_pt")
        ac_pt = Sem(nc, "ac_pt")
        pe_cx = Sem(nc, "pe_cx")
        dv_ao = Sem(nc, "dv_ao")
        pe_ao = Sem(nc, "pe_ao")
        dv_res = Sem(nc, "dv_res")
        f_x2 = Sem(nc, "f_x2")
        f_x2e = Sem(nc, "f_x2e")
        f_xl = [Sem(nc, f"f_xl{i}") for i in range(2)]
        pe_f = Sem(nc, "pe_f")
        ac_f = Sem(nc, "ac_f")
        dv_f = Sem(nc, "dv_f")
        pe_ft = Sem(nc, "pe_ft")
        st_out = Sem(nc, "st_out")

        r_ptx, r_pty = Rel(), Rel()
        r_xnbf2 = [Rel(), Rel()]
        r_psa, r_psb, r_psc, r_psd = Rel(), Rel(), Rel(), Rel()
        r_ps0, r_ps1 = Rel(), Rel()
        r_xnbf = Rel()
        r_S = [Rel(), Rel()]
        r_PA, r_PB = Rel(), Rel()
        r_PT = [Rel(), Rel()]

        r_win = Rel()
        r_kv = Rel()
        r_lbuf = Rel()

        P.step("po", lambda g: g.memset(id_bf[:], 0.0), inc=po_chain)
        P.step("po", lambda g: g.affine_select(
            out=id_bf[:], in_=id_bf[:], compare_op=ALU.not_equal, fill=1.0,
            base=0, pattern=[[-1, 128]], channel_multiplier=1),
            wait=[po_chain], inc=po_set)
        P.step("po", lambda g: g.memset(negmask[:], 0.0), inc=po_chain)
        P.step("po", lambda g: g.affine_select(
            out=negmask[:], in_=negmask[:], compare_op=ALU.is_ge, fill=NEG_BIG,
            base=0, pattern=[[-1, 128]], channel_multiplier=1),
            wait=[po_chain], inc=po_set)
        for buf, val in ((c0125, 0.125), (c_lneps, LN_EPS), (c_rmseps, RMS_EPS),
                         (c_nhalf, -0.5), (c_inv64, 1.0 / 64.0)):
            P.step("po", lambda g, buf=buf, val=val: g.memset(buf[:], val),
                   inc=po_set)
        P.step("po", lambda g: g.memset(woT_pair[:], 0.0), inc=po_set)
        POSET = po_set.n

        with ExitStack() as su:
            def tb(name, shape, dt=F32):
                return su.enter_context(nc.sbuf_tensor(name, shape, dt))

            id_f = tb("id_f", [128, 128])
            P.step("po", lambda g: g.memset(id_f[:], 0.0), inc=po_chain)
            P.step("po", lambda g: g.affine_select(
                out=id_f[:], in_=id_f[:], compare_op=ALU.not_equal, fill=1.0,
                base=0, pattern=[[-1, 128]], channel_multiplier=1),
                wait=[po_chain], inc=po_set)
            wdq_s = tb("wdq_s", [128, 4 * HD])
            wuq_s = tb("wuq_s", [HD, DL])
            wdkv_s = tb("wdkv_s", [128, 8 * HD])
            wukv_s = tb("wukv_s", [2 * HD, 2 * DL])
            wqr_s = tb("wqr_s", [HD, HD])
            wkr_s = tb("wkr_s", [HD, HD])
            wo_s = tb("wo_s", [HD, HD])
            wuqT_s = tb("wuqT_s", [128, 4 * HD])
            wukvT_s = tb("wukvT_s", [128, 8 * 2 * HD])
            b1_raw = tb("b1_raw", [NHC, 128])
            b2_raw = tb("b2_raw", [NOC, 128])

            def wload(dst, src):
                P.step("sy", lambda s, d=dst, sr=src: s.dma_start(out=d, in_=sr),
                       inc=d_w, inck=16)

            wload(wdq_s[:].rearrange("p (c f) -> p c f", f=HD),
                  wdq_d.rearrange("(c p) f -> p c f", p=128))
            wload(wuq_s[:], wuq_d)
            wload(wdkv_s[:].rearrange("p (c f) -> p c f", f=HD),
                  wdkv_d.rearrange("(c p) f -> p c f", p=128))
            wload(wukv_s[:], wukv_d)
            wload(wqr_s[:], wqr_d)
            wload(wkr_s[:], wkr_d)
            wload(wo_s[:], wo_d)
            wload(qnw_col[:], qw_d.rearrange("(f o) -> f o", o=1))
            wload(knw_col[:], kw_d.rearrange("(f o) -> f o", o=1))
            wload(b1_raw[:], b1_d.rearrange("(c p) -> c p", p=128))
            wload(b2_raw[:], b2_d.rearrange("(c p) -> c p", p=128))
            WALL = d_w.n

            for c in range(4):
                P.step("pe", lambda t, c=c: t.transpose(
                    ps_a[:, c * HD:(c + 1) * HD],
                    wuq_s[:, c * 128:(c + 1) * 128], id_f[0:HD, 0:HD]),
                    wait=[(d_w, WALL), po_set],
                    inc=pe_su if c == 3 else None)
            P.step("dv", lambda v: v.tensor_copy(wuqT_s[:], ps_a[:, 0:4 * HD]),
                   wait=[pe_su], inc=dv_su)
            r_psa.set(dv_su)
            for rnd in range(2):
                for c in range(4):
                    cc = rnd * 4 + c
                    P.step("pe", lambda t, cc=cc, c=c: t.transpose(
                        ps_b[:, c * 128:(c + 1) * 128],
                        wukv_s[:, cc * 128:(cc + 1) * 128], id_f[:]),
                        wait=r_psb.waits() if c == 0 else [],
                        inc=pe_su if c == 3 else None)
                P.step("dv", lambda v, rnd=rnd: v.tensor_copy(
                    wukvT_s[:, rnd * 512:(rnd + 1) * 512], ps_b[:, :]),
                    wait=[pe_su], inc=dv_su)
                r_psb.set(dv_su)
            P.step("pe", lambda t: t.transpose(ps_a[0:HD, 0:HD], wqr_s[:],
                                               id_f[0:HD, 0:HD]), wait=r_psa.waits())
            P.step("pe", lambda t: t.transpose(ps_a[0:HD, HD:2 * HD], wkr_s[:],
                                               id_f[0:HD, 0:HD]))
            P.step("pe", lambda t: t.transpose(ps_a[0:HD, 2 * HD:3 * HD],
                                               wo_s[:], id_f[0:HD, 0:HD]), inc=pe_su)
            P.step("dv", lambda v: v.tensor_copy(wqrT[0:HD, :], ps_a[0:HD, 0:HD]),
                   wait=[pe_su])
            P.step("dv", lambda v: v.tensor_copy(wqrT[HD:128, :], ps_a[0:HD, 0:HD]))
            P.step("dv", lambda v: v.tensor_copy(wkrT[0:HD, :], ps_a[0:HD, HD:2 * HD]))
            P.step("dv", lambda v: v.tensor_copy(wkrT[HD:128, :], ps_a[0:HD, HD:2 * HD]))
            P.step("dv", lambda v: v.tensor_copy(
                woT_pair[0:HD, 0:HD], ps_a[0:HD, 2 * HD:3 * HD]))
            P.step("dv", lambda v: v.tensor_copy(
                woT_pair[HD:128, HD:128], ps_a[0:HD, 2 * HD:3 * HD]), inc=dv_su)
            r_psa.set(dv_su)
            P.step("pe", lambda t: t.transpose(ps_b[:, 0:NHC], b1_raw[:],
                                               id_f[0:NHC, 0:NHC]), wait=r_psb.waits())
            P.step("pe", lambda t: t.transpose(ps_b[:, 256:256 + NOC], b2_raw[:],
                                               id_f[0:NOC, 0:NOC]), inc=pe_su)
            P.step("dv", lambda v: v.tensor_copy(b1col[:], ps_b[:, 0:NHC]),
                   wait=[pe_su])
            P.step("dv", lambda v: v.tensor_copy(b2col[:], ps_b[:, 256:256 + NOC]),
                   inc=dv_su)
            r_psb.set(dv_su)

            for c in range(4):
                P.step("pe", lambda t, c=c: t.matmul(
                    ps_c[0:HD, 0:HD],
                    wdq_s[:].rearrange("p (c f) -> p c f", f=HD)[:, c, :],
                    wuqT_s[:, c * HD:(c + 1) * HD],
                    start=(c == 0), stop=(c == 3)),
                    wait=[dv_su] if c == 0 else [],
                    inc=pe_su if c == 3 else None)
            for c in range(8):
                P.step("pe", lambda t, c=c: t.matmul(
                    ps_d[0:HD, 0:2 * HD],
                    wdkv_s[:].rearrange("p (c f) -> p c f", f=HD)[:, c, :],
                    wukvT_s[:, c * 2 * HD:(c + 1) * 2 * HD],
                    start=(c == 0), stop=(c == 7)),
                    inc=pe_su if c == 7 else None)
            P.step("dv", lambda v: v.tensor_copy(wqT2[0:HD, :], ps_c[0:HD, 0:HD]),
                   wait=[pe_su])
            P.step("dv", lambda v: v.tensor_copy(wqT2[HD:128, :],
                                                 ps_c[0:HD, 0:HD]))
            P.step("dv", lambda v: v.tensor_copy(wkT2[0:HD, :],
                                                 ps_d[0:HD, 0:HD]))
            P.step("dv", lambda v: v.tensor_copy(wkT2[HD:128, :],
                                                 ps_d[0:HD, 0:HD]))
            P.step("dv", lambda v: v.tensor_copy(wvT2[0:HD, :],
                                                 ps_d[0:HD, HD:2 * HD]))
            P.step("dv", lambda v: v.tensor_copy(wvT2[HD:128, :],
                                                 ps_d[0:HD, HD:2 * HD]),
                   inc=dv_su)
            r_psc.set(dv_su)
            r_psd.set(dv_su)
            SETUP = dv_su.n

        with ExitStack() as att:
            def asb(name, shape, dt=BF16):
                return att.enter_context(nc.sbuf_tensor(name, shape, dt))

            PCAP = 128 * sum(4 * (NB - 1) + qt + 1 for qt in range(NQT))
            Xn1T = asb("Xn1T", [64, H * 512])
            kstack = [asb(f"kst{h}", [128, T]) for h in range(H)]
            qstack = [asb(f"qst{h}", [128, 512]) for h in range(H)]
            vstore = [asb(f"vst{g}", [128, TC * GW]) for g in range(NGRP)]
            qn_tok = asb("qn_tok", [128, GW])
            msq = asb("msq", [128, HHALF], F32)
            ln8 = asb("ln8", [128, HHALF], F32)
            rstd8 = asb("rstd8", [128, HHALF], F32)
            rstdq_win = asb("rstdq_win", [128, 4 * H], F32)
            S2 = [asb(f"S_sb{i}", [128, T], F32) for i in range(2)]
            mcol = asb("mcol", [128, 6], F32)
            mrow = asb("mrow", [128, 1], F32)
            negm8 = asb("negm8", [128, 8], F32)
            P_A = asb("P_A", [128, PCAP])
            P_B = asb("P_B", [128, PCAP])
            PT2 = [asb(f"PT2{par}", [128, 1024]) for par in range(2)]
            l_buf = asb("l_buf", [128, 8], F32)
            rl_buf = asb("rl_buf", [128, 8], F32)
            ctx_bf = asb("ctx_bf", [128, 512])
            attn_bf = ctx_bf
            sqp_bf = asb("sqp_bf", [128, GW])
            r_ctxbf = Rel()
            r_sqp = Rel()
            r_attnbf = r_ctxbf
            r_qntok = Rel()
            r_rstd8, r_msq = Rel(), Rel()

            xnT_marks = {}

            def emit_W1(b_, qb_, thunks=None):
                win0_ = qb_ * 512

                def mk_chain(tc):
                    def chain():
                        t0 = win0_ + tc * 128
                        par = tc % 2
                        P.step("po", lambda g, b=b_, t0=t0, par=par: g.dma_start(
                            out=xln[par][:], in_=x_in[b, t0:t0 + 128, :]),
                            wait=[f_xl[par]], inc=d_xl[par], inck=16)
                        ld = d_xl[par].n
                        P.step("dv", lambda v, par=par: v.reduce_sum(
                            musum[:], xln[par][:], axis=AX.X),
                            wait=[(d_xl[par], ld)], inc=dv_ln)
                        P.step("ac", lambda a, par=par: a.activation(
                            out=xn_bf2[tc % 2][:], in_=xln[par][:], func=AF.Square,
                            accum_out=sqsum[:]),
                            wait=[(d_xl[par], ld), dv_ln]
                            + r_xnbf2[tc % 2].waits(),
                            inc=ac_ln)
                        P.step("dv", lambda v: v.tensor_scalar(
                            out=negmu[:], in0=musum[:], scalar1=-1.0 / D,
                            scalar2=None, op0=ALU.mult), wait=[dv_ln],
                            inc=dv_ln)
                        P.step("dv", lambda v: v.tensor_mul(
                            mu2[:], negmu[:], negmu[:]), wait=[dv_ln],
                            inc=dv_ln)
                        P.step("dv", lambda v: v.tensor_scalar(
                            out=varb[:], in0=sqsum[:], scalar1=1.0 / D,
                            scalar2=None, op0=ALU.mult),
                            wait=[ac_ln, dv_ln], inc=dv_ln)
                        P.step("dv", lambda v: v.tensor_sub(
                            varb[:], varb[:], mu2[:]), wait=[dv_ln],
                            inc=dv_ln)
                        P.step("ac", lambda a: a.activation(
                            out=lnv[:], in_=varb[:], func=AF.Ln,
                            bias=c_lneps[:]),
                            wait=[dv_ln, ac_ln, (po_set, POSET)], inc=ac_ln)
                        P.step("ac", lambda a: a.activation(
                            out=rstd[:], in_=lnv[:], func=AF.Exp,
                            scale=c_nhalf[:]),
                            wait=[ac_ln, f_xl[0], f_xl[1]], inc=ac_ln)
                        P.step("dv", lambda v, par=par, tc=tc: v.tensor_scalar(
                            out=xn_bf2[tc % 2][:], in0=xln[par][:],
                            scalar1=negmu[:],
                            scalar2=rstd[:], op0=ALU.add, op1=ALU.mult),
                            wait=[ac_ln] + r_xnbf2[tc % 2].waits(),
                            inc=f_xl[par])
                    return chain

                def mk_tr(tc):
                    def tr():
                        par = tc % 2
                        xn_done = f_xl[par].n
                        for fc in range(NFC):
                            P.step("pe", lambda t, fc=fc: t.transpose(
                                pt_x[:, fc * 128:(fc + 1) * 128],
                                xn_bf2[tc % 2][:, fc * 128:(fc + 1) * 128],
                                id_bf[:]),
                                wait=([(f_xl[par], xn_done), (po_set, POSET)]
                                      + r_ptx.waits() + [(pe_pr, pe_pr.n)])
                                if fc == 0 else [],
                                inc=pe_ln if fc == NFC - 1 else None)
                        r_xnbf2[tc % 2].set(pe_ln)
                        P.step("dv", lambda v, tc=tc: v.tensor_copy(
                            Xn1T[:].rearrange("p (c f) -> p c f", f=1024)
                            [:, :, tc * 128:(tc + 1) * 128],
                            pt_x[0:64, :].rearrange("p (c f) -> p c f", f=128)
                            [:, 0:H // 2, :]),
                            wait=[pe_ln], inc=dv_ln)
                        P.step("dv", lambda v, tc=tc: v.tensor_copy(
                            Xn1T[:].rearrange("p (c f) -> p c f", f=1024)
                            [:, :, 512 + tc * 128:512 + (tc + 1) * 128],
                            pt_x[64:128, :].rearrange("p (c f) -> p c f",
                                                      f=128)
                            [:, 0:H // 2, :]),
                            inc=dv_ln)
                        r_ptx.set(dv_ln)
                        if tc == 3:
                            xnT_marks[(b_, qb_)] = dv_ln.n
                    return tr

                seq = []
                for tc in range(4):
                    seq.append(mk_chain(tc))
                    seq.append(mk_tr(tc))
                if thunks is None:
                    for f in seq:
                        f()
                else:
                    thunks.extend(seq)

            for b in range(B):
                for qb in range(NB):
                    win0 = qb * 512

                    for tc in range(4):
                        t0 = win0 + tc * 128
                        P.step("sy", lambda s, b=b, t0=t0, tc=tc: s.dma_start(
                            out=x2_buf[:, tc * D:(tc + 1) * D],
                            in_=x_in[b, t0:t0 + 128, :]),
                            wait=[f_x2] if tc == 0 else [], inc=d_x2, inck=16)
                    x2_ready = d_x2.n

                    rot4 = [(ps_a, r_psa), (ps_b, r_psb),
                            (ps_c, r_psc), (ps_d, r_psd)]
                    uidx = [0]
                    pendu = deque()

                    def w2_stage2(tc, tcg, grp, kind, ps, r_ps, mm_done):
                        if kind == "v":
                            P.step("ac", lambda a, grp=grp, tcg=tcg, ps=ps: a.activation(
                                out=vstore[grp][:, tcg * GW:(tcg + 1) * GW],
                                in_=ps[:, 0:GW], func=AF.Copy),
                                wait=[(pe_pr, mm_done)] + r_kv.waits(),
                                inc=ac_pr)
                            r_ps.set(ac_pr)
                            return

                        if kind == "q":
                            P.step("dv", lambda v, ps=ps: v.tensor_copy(
                                qn_tok[:], ps[:, 0:GW]),
                                wait=[(pe_pr, mm_done)] + r_qntok.waits(),
                                inc=dv_pr)
                            ev_mark = dv_pr.n
                            for hh in range(HHALF):
                                P.step("pe", lambda t, hh=hh: t.transpose(
                                    pt_y[0:HD, hh * 128:(hh + 1) * 128],
                                    qn_tok[:, hh * HD:(hh + 1) * HD],
                                    id_bf[:]),
                                    wait=([(dv_pr, ev_mark)] + r_pty.waits())
                                    if hh == 0 else [],
                                    inc=pe_pr if hh == HHALF - 1 else None)
                            r_qntok.set(pe_pr)
                            tr_done = pe_pr.n
                            for hh in range(HHALF):
                                h = grp * HHALF + hh
                                dst = qstack[h][0:HD, tc * 128:(tc + 1) * 128]
                                P.step("dv", lambda v, hh=hh, dst=dst: v.tensor_scalar(
                                    out=dst,
                                    in0=pt_y[0:HD, hh * 128:(hh + 1) * 128],
                                    scalar1=qnw_col[:], scalar2=None,
                                    op0=ALU.mult),
                                    wait=([(pe_pr, tr_done)] + r_win.waits())
                                    if hh == 0 else [],
                                    inc=dv_pr if hh == HHALF - 1 else None)
                            r_pty.set(dv_pr)
                            P.step("ac", lambda a, ps=ps: a.activation(
                                out=sqp_bf[:], in_=ps[:, 0:GW],
                                func=AF.Square),
                                wait=[(pe_pr, mm_done), (dv_pr, ev_mark)]
                                + r_sqp.waits(),
                                inc=ac_pr)
                            sq_mark = ac_pr.n
                            P.step("dv", lambda v: v.tensor_reduce(
                                out=msq[:],
                                in_=sqp_bf[:].rearrange("p (h f) -> p h f",
                                                        f=HD),
                                axis=AX.X, op=ALU.add),
                                wait=[(ac_pr, sq_mark)] + r_msq.waits(),
                                inc=dv_pr)
                            r_sqp.set(dv_pr)
                            red_mark = dv_pr.n
                            r_ps.set(dv_pr)
                            P.step("ac", lambda a: a.activation(
                                out=ln8[:], in_=msq[:], func=AF.Ln,
                                bias=c_rmseps[:], scale=c_inv64[:]),
                                wait=[(dv_pr, red_mark), ac_pr], inc=ac_pr)
                            r_msq.set(ac_pr)
                            P.step("ac", lambda a: a.activation(
                                out=rstd8[:], in_=ln8[:], func=AF.Exp,
                                scale=c_nhalf[:]),
                                wait=[ac_pr] + r_rstd8.waits(), inc=ac_pr)
                            rstd_done = ac_pr.n
                            c0 = tc * H + grp * HHALF
                            P.step("dv", lambda v, c0=c0: v.tensor_scalar(
                                out=rstdq_win[:, c0:c0 + HHALF],
                                in0=rstd8[:], scalar1=0.125, scalar2=None,
                                op0=ALU.mult),
                                wait=[(ac_pr, rstd_done)], inc=dv_pr)
                            r_rstd8.set(dv_pr)
                            return

                        for hh in range(HHALF):
                            P.step("ac", lambda a, hh=hh, ps=ps: a.activation(
                                out=sqp_bf[:, hh * HD:(hh + 1) * HD],
                                in_=ps[:, hh * HD:(hh + 1) * HD],
                                func=AF.Square,
                                accum_out=msq[:, hh:hh + 1]),
                                wait=([(pe_pr, mm_done)] + r_sqp.waits()
                                      + r_msq.waits()) if hh == 0 else [],
                                inc=ac_pr if hh == HHALF - 1 else None)
                        r_sqp.set(ac_pr)
                        sq_mark = ac_pr.n
                        P.step("ac", lambda a: a.activation(
                            out=ln8[:], in_=msq[:], func=AF.Ln,
                            bias=c_rmseps[:], scale=c_inv64[:]),
                            wait=[(ac_pr, sq_mark)], inc=ac_pr)
                        r_msq.set(ac_pr)
                        P.step("ac", lambda a: a.activation(
                            out=rstd8[:], in_=ln8[:], func=AF.Exp,
                            scale=c_nhalf[:]),
                            wait=[ac_pr] + r_rstd8.waits(), inc=ac_pr)
                        rstd_done = ac_pr.n
                        for hh in range(HHALF):
                            P.step("dv", lambda v, hh=hh, ps=ps: v.tensor_scalar(
                                out=qn_tok[:, hh * HD:(hh + 1) * HD],
                                in0=ps[:, hh * HD:(hh + 1) * HD],
                                scalar1=rstd8[:, hh:hh + 1],
                                scalar2=None, op0=ALU.mult),
                                wait=([(ac_pr, rstd_done)]
                                      + r_qntok.waits())
                                if hh == 0 else [],
                                inc=dv_pr if hh == HHALF - 1 else None)
                        r_rstd8.set(dv_pr)
                        r_ps.set(dv_pr)
                        r_ps.add(ac_pr)
                        appl = dv_pr.n
                        for hh in range(HHALF):
                            P.step("pe", lambda t, hh=hh: t.transpose(
                                pt_y[0:HD, hh * 128:(hh + 1) * 128],
                                qn_tok[:, hh * HD:(hh + 1) * HD],
                                id_bf[:]),
                                wait=([(dv_pr, appl)] + r_pty.waits())
                                if hh == 0 else [],
                                inc=pe_pr if hh == HHALF - 1 else None)
                        r_qntok.set(pe_pr)
                        tr_done = pe_pr.n
                        for hh in range(HHALF):
                            h = grp * HHALF + hh
                            dst = kstack[h][0:HD, tcg * 128:(tcg + 1) * 128]
                            P.step("dv", lambda v, hh=hh, dst=dst: v.tensor_scalar(
                                out=dst,
                                in0=pt_y[0:HD, hh * 128:(hh + 1) * 128],
                                scalar1=knw_col[:], scalar2=None,
                                op0=ALU.mult),
                                wait=([(pe_pr, tr_done)] + r_kv.waits())
                                if hh == 0 else [],
                                inc=dv_pr if hh == HHALF - 1 else None)
                        r_pty.set(dv_pr)
                    if (b, qb) not in xnT_marks:
                        emit_W1(b, qb)
                    xnT_done = xnT_marks[(b, qb)]
                    for tc in range(4):
                        t0 = win0 + tc * 128
                        tcg = t0 // 128
                        for grp in range(0 if skip_proj else NGRP):
                            for kind in ("q", "k", "v"):
                                ps, r_ps = rot4[uidx[0] % 4]
                                uidx[0] += 1
                                wT = {"q": wqT2, "k": wkT2, "v": wvT2}[kind]
                                for hh in range(HHALF):
                                    h = grp * HHALF + hh
                                    P.step("pe", lambda t, h=h, hh=hh, wT=wT, tc=tc, ps=ps: t.matmul(
                                        ps[:, hh * HD:(hh + 1) * HD],
                                        Xn1T[0:HD, h * 512 + tc * 128:
                                             h * 512 + (tc + 1) * 128],
                                        wT[0:HD, :], start=True,
                                        stop=True),
                                        wait=([(dv_ln, xnT_done),
                                               (dv_su, SETUP)] + r_ps.waits())
                                        if hh == 0 else [],
                                        inc=pe_pr if hh == HHALF - 1 else None)
                                pendu.append((tc, tcg, grp, kind, ps, r_ps,
                                              pe_pr.n))
                                if len(pendu) >= 3:
                                    w2_stage2(*pendu.popleft())

                    while pendu:
                        w2_stage2(*pendu.popleft())
                    stacks_done = dv_pr.n
                    pingpong = [(ps_a, r_psa), (ps_b, r_psb)]
                    pidx = 0
                    for kind in (() if skip_proj else ("qr", "kr")):
                        wT = wqrT if kind == "qr" else wkrT
                        for h in range(H):
                            ps, r_ps = pingpong[pidx]
                            pidx ^= 1
                            if kind == "qr":
                                rhs = qstack[h][0:HD, 0:512]
                            else:
                                rhs = Xn1T[0:HD, h * 512:(h + 1) * 512]
                            P.step("pe", lambda t, wT=wT, rhs=rhs, ps=ps: t.matmul(
                                ps[0:HD, :], wT[0:HD, :], rhs,
                                start=True, stop=True),
                                wait=[(dv_pr, stacks_done)] + r_ps.waits(),
                                inc=pe_pr)
                            mmn = pe_pr.n
                            if kind == "qr":
                                dst = qstack[h][HD:128, 0:512]
                            else:
                                dst = kstack[h][HD:128, win0:win0 + 512]
                            P.step("ac", lambda a, dst=dst, ps=ps: a.activation(
                                out=dst, in_=ps[0:HD, :], func=AF.Copy),
                                wait=[(pe_pr, mmn)], inc=ac_pr)
                            r_ps.set(ac_pr)
                    win_ready = dv_pr.n
                    win_ready_ac = ac_pr.n

                    if skip_attn:
                        P.step("dv", lambda v: v.tensor_copy(
                            mrow[:], musum[:]), wait=[(d_x2, x2_ready)],
                            inc=dv_res)
                        sp = dv_res.n
                        for tc in range(4):
                            row0 = b * T + win0 + tc * 128
                            P.step("po", lambda g, row0=row0, tc=tc: g.dma_start(
                                out=x2s[row0:row0 + 128, :],
                                in_=x2_buf[:, tc * D:(tc + 1) * D]),
                                wait=[(dv_res, sp)] if tc == 0 else [],
                                inc=f_x2, inck=16)
                        r_win.set(pe_s, pe_pr)
                        continue
                    poffs = [0]
                    for qt in range(NQT):
                        poffs.append(poffs[-1] + 128 * (4 * qb + qt + 1))

                    stats_mark = {}
                    negm_free = {}
                    sring = [0]

                    def emit_slot_stats(p, hi, qt):
                        h = 2 * p + hi
                        slot = hi * 4 + qt
                        spar = slot % 2
                        S_sb = S2[spar]
                        Tq = 4 * qb + qt
                        klen = 128 * (Tq + 1)
                        offs = list(range(0, klen, 512))
                        nchunks = 0
                        sping = [(ps_s0, r_ps0), (ps_s1, r_ps1),
                                 (ps_a, r_psa), (ps_b, r_psb)]
                        first_stat = True
                        for ci, off in enumerate(offs):
                            w = min(512, klen - off)
                            ps, r_ps = sping[(sring[0] + ci) % 4]
                            has_diag = (off + w == klen)
                            P.step("pe", lambda t, h=h, qt=qt, off=off, w=w, ps=ps: t.matmul(
                                ps[:, 0:w],
                                qstack[h][:, qt * 128:(qt + 1) * 128],
                                kstack[h][:, off:off + w],
                                start=True, stop=True),
                                wait=[(dv_pr, win_ready),
                                      (ac_pr, win_ready_ac)] + r_ps.waits(),
                                inc=pe_s)
                            mmn = pe_s.n
                            wmain = w - 128 if has_diag else w
                            sg = r_S[spar].waits()
                            first_stat = False
                            if wmain > 0:
                                P.step("dv", lambda v, off=off, wmain=wmain, ps=ps, nchunks=nchunks, S_sb=S_sb: v.tensor_scalar(
                                    out=S_sb[:, off:off + wmain],
                                    in0=ps[:, 0:wmain], scalar1=1.0,
                                    scalar2=None, op0=ALU.mult,
                                    op1=ALU.max,
                                    accum_out=mcol[:, nchunks:nchunks + 1]),
                                    wait=[(pe_s, mmn)] + sg, inc=dv_st)
                                r_ps.set(dv_st)
                                nchunks += 1
                            if has_diag:
                                P.step("dv", lambda v, off=off, w=w, wmain=wmain, ps=ps, S_sb=S_sb: v.scalar_tensor_tensor(
                                    out=S_sb[:, off + wmain:off + w],
                                    in0=ps[:, wmain:w], scalar=1.0,
                                    in1=negmask[:], op0=ALU.mult,
                                    op1=ALU.add),
                                    wait=[(pe_s, mmn), (po_set, POSET)] + sg,
                                    inc=dv_st)
                                P.step("dv", lambda v, off=off, w=w, wmain=wmain, nchunks=nchunks, S_sb=S_sb: v.reduce_max(
                                    out=mcol[:, nchunks:nchunks + 1],
                                    in_=S_sb[:, off + wmain:off + w],
                                    axis=AX.X),
                                    wait=[dv_st], inc=dv_st)
                                nchunks += 1
                                r_ps.set(dv_st)
                                r_ps.add(ac_st)
                        sring[0] = (sring[0] + len(offs)) % 4
                        P.step("dv", lambda v, nchunks=nchunks: v.reduce_max(
                            out=mrow[:], in_=mcol[:, 0:nchunks], axis=AX.X),
                            wait=[dv_st], inc=dv_st)
                        rc = qt * H + h
                        P.step("dv", lambda v, slot=slot, rc=rc: v.tensor_scalar(
                            out=negm8[:, slot:slot + 1], in0=mrow[:],
                            scalar1=rstdq_win[:, rc:rc + 1], scalar2=-1.0,
                            op0=ALU.mult, op1=ALU.mult),
                            wait=[dv_st, (dv_pr, win_ready)]
                            + negm_free.get(slot, []),
                            inc=dv_st)
                        stats_mark[(p, slot)] = dv_st.n

                    def emit_slot_exp(p, hi, qt):
                        slot = hi * 4 + qt
                        spar = slot % 2
                        P_sb = (P_A, P_B)[hi]
                        r_P = (r_PA, r_PB)[hi]
                        klen = 128 * (4 * qb + qt + 1)
                        po0 = poffs[qt]
                        rc = qt * H + 2 * p + hi
                        P.step("ac", lambda a, P_sb=P_sb, po0=po0, klen=klen, slot=slot, spar=spar, rc=rc: a.activation(
                            out=P_sb[:, po0:po0 + klen],
                            in_=S2[spar][:, 0:klen], func=AF.Exp,
                            scale=rstdq_win[:, rc:rc + 1],
                            bias=negm8[:, slot:slot + 1],
                            accum_out=l_buf[:, slot:slot + 1]),
                            wait=[(dv_st, stats_mark[(p, slot)])]
                            + r_P.waits()
                            + (r_lbuf.waits() if slot == 0 else []),
                            inc=ac_ex)
                        negm_free[slot] = [(ac_ex, ac_ex.n)]
                        r_S[spar].set(ac_ex)

                    def emit_pair_ctx(p):
                        P.step("dv", lambda v: v.reciprocal(rl_buf[:],
                                                            l_buf[:]),
                               wait=[ac_ex], inc=dv_res)
                        r_lbuf.set(dv_res)
                        rl_done = dv_res.n
                        nj = 4 * qb + 4
                        ptb = (pt_x, pt_y)
                        rptb = (r_ptx, r_pty)
                        for j in range(nj):
                            par = j % 2
                            qt_min = max(0, j - 4 * qb)
                            for hi in range(2):
                                P_sb = (P_A, P_B)[hi]
                                for qt in range(qt_min, NQT):
                                    src0 = poffs[qt] + j * 128
                                    P.step("pe", lambda t, P_sb=P_sb, hi=hi, qt=qt, src0=src0, par=par: t.transpose(
                                        ptb[par][:, hi * 512 + qt * 128:
                                                 hi * 512 + (qt + 1) * 128],
                                        P_sb[:, src0:src0 + 128], id_bf[:]),
                                        wait=([ac_ex] + rptb[par].waits())
                                        if (hi == 0 and qt == qt_min) else [],
                                        inc=pe_pt
                                        if (hi == 1 and qt == NQT - 1)
                                        else None)
                            trn = pe_pt.n
                            P.step("dv", lambda v, par=par, qt_min=qt_min: v.tensor_copy(
                                PT2[par][:].rearrange("p (h f) -> p h f",
                                                      f=512)
                                [:, :, qt_min * 128:512],
                                ptb[par][:].rearrange("p (h f) -> p h f",
                                                      f=512)
                                [:, :, qt_min * 128:512]),
                                wait=[(pe_pt, trn)] + r_PT[par].waits(),
                                inc=dv_pt)
                            ev_w = (dv_pt, dv_pt.n)
                            rptb[par].set(dv_pt)
                            for hi in range(2):
                                h = 2 * p + hi
                                grp = h // HHALF
                                hh = h % HHALF
                                v0 = j * GW + hh * HD
                                P.step("pe", lambda t, hi=hi, grp=grp, v0=v0, par=par, qt_min=qt_min, j=j, nj=nj: t.matmul(
                                    ps_c[hi * HD:(hi + 1) * HD,
                                         qt_min * 128:512],
                                    vstore[grp][:, v0:v0 + HD],
                                    PT2[par][:, hi * 512 + qt_min * 128:
                                             hi * 512 + 512],
                                    start=(j == 0), stop=(j == nj - 1),
                                    tile_position=(0, hi * HD),
                                    skip_group_check=True),
                                    wait=([ev_w,
                                           (dv_pr, win_ready),
                                           (ac_pr, win_ready_ac)]
                                          + (r_psc.waits() if j == 0 else []))
                                    if hi == 0 else [],
                                    inc=pe_cx if hi == 1 else None)
                            r_PT[par].set(pe_cx)
                        r_PA.set(pe_pt)
                        r_PB.set(pe_pt)

                        P.step("dv", lambda v: v.tensor_copy(ctx_bf[:],
                                                             ps_c[:, :]),
                               wait=[pe_cx] + r_ctxbf.waits(), inc=dv_ao)
                        r_psc.set(dv_ao)
                        P.step("pe", lambda t: t.matmul(
                            ps_d[:, :], woT_pair[:], ctx_bf[:], start=True,
                            stop=True),
                            wait=[dv_ao, (dv_su, SETUP)] + r_psd.waits(),
                            inc=pe_ao)
                        r_ctxbf.set(pe_ao)
                        P.step("dv", lambda v: v.tensor_copy(attn_bf[:],
                                                             ps_d[:, :]),
                               wait=[pe_ao] + r_attnbf.waits(), inc=dv_ao)
                        r_psd.set(dv_ao)
                        ab_done = dv_ao.n
                        for qt in range(NQT):
                            P.step("pe", lambda t, qt=qt: t.transpose(
                                pt_x[:, qt * 128:(qt + 1) * 128],
                                attn_bf[:, qt * 128:(qt + 1) * 128],
                                id_bf[:]),
                                wait=([(dv_ao, ab_done)] + r_ptx.waits())
                                if qt == 0 else [],
                                inc=pe_ao if qt == NQT - 1 else None)
                        r_attnbf.set(pe_ao)
                        trn = pe_ao.n
                        for qt in range(NQT):
                            for hi in range(2):
                                slot = hi * 4 + qt
                                col0 = qt * D + p * 128 + hi * HD
                                P.step("dv", lambda v, qt=qt, hi=hi, slot=slot, col0=col0: v.scalar_tensor_tensor(
                                    out=x2_buf[:, col0:col0 + HD],
                                    in0=pt_x[:, qt * 128 + hi * HD:
                                             qt * 128 + (hi + 1) * HD],
                                    scalar=rl_buf[:, slot:slot + 1],
                                    in1=x2_buf[:, col0:col0 + HD],
                                    op0=ALU.mult, op1=ALU.add),
                                    wait=([(pe_ao, trn), (dv_res, rl_done),
                                           (d_x2, x2_ready)]
                                          if (qt == 0 and hi == 0) else []),
                                    inc=(dv_res
                                         if (qt == NQT - 1 and hi == 1)
                                         else None))
                        r_ptx.set(dv_res)

                    pend = deque()
                    w1q = []
                    slot_ctr = [0]
                    for p in range(H // 2):
                        if p == H // 2 - 3:
                            nb, nqb = (b, qb + 1) if qb + 1 < NB else \
                                      (b + 1, 0)
                            if nb < B:
                                emit_W1(nb, nqb, thunks=w1q)
                        for hi in range(2):
                            for qt in range(NQT):
                                slot_ctr[0] += 1
                                if w1q and slot_ctr[0] % 3 == 0:
                                    w1q.pop(0)()
                                if len(pend) >= 2:
                                    g = pend.popleft()
                                    emit_slot_exp(*g)
                                    if g[1] == 1 and g[2] == NQT - 1:
                                        emit_pair_ctx(g[0])
                                emit_slot_stats(p, hi, qt)
                                pend.append((p, hi, qt))
                    while pend:
                        g = pend.popleft()
                        emit_slot_exp(*g)
                        if g[1] == 1 and g[2] == NQT - 1:
                            emit_pair_ctx(g[0])
                    while w1q:
                        w1q.pop(0)()
                    sp = dv_res.n
                    for tc in range(4):
                        row0 = b * T + win0 + tc * 128
                        P.step("po", lambda g, row0=row0, tc=tc: g.dma_start(
                            out=x2s[row0:row0 + 128, :],
                            in_=x2_buf[:, tc * D:(tc + 1) * D]),
                            wait=[(dv_res, sp)] if tc == 0 else [],
                            inc=f_x2, inck=16)
                    r_win.set(pe_s, pe_pr)
                r_kv.set(pe_s, pe_cx)

        with ExitStack() as ffn:
            def fsb(name, shape, dt=BF16):
                return ffn.enter_context(nc.sbuf_tensor(name, shape, dt))

            w1b = fsb("w1b", [128, NFC * FF])
            w2b = fsb("w2b", [128, NHC * D])
            xn2 = fsb("xn2", [128, D])
            Xn2T2 = [fsb(f"Xn2T{i}", [128, NFC * 512]) for i in range(2)]
            r_Xn2Tb = [Rel(), Rel()]
            hT = fsb("hT", [128, NHC * 512])
            y_bf = [fsb(f"ybf{i}", [128, 512]) for i in range(2)]
            r_xn2, r_Xn2T = Rel(), Rel()
            r_ybf = [Rel(), Rel()]

            SPILLS = f_x2.n

            prev_w2_end = 0
            ln2_marks = {}
            sq_guard = [0]

            def emit_ln2(s_):
                b_, qb_ = divmod(s_, NB)
                win0_ = qb_ * 512
                for tc in range(4):
                    row0 = b_ * T + win0_ + tc * 128
                    par = tc % 2
                    P.step("po", lambda g, row0=row0, par=par: g.dma_start(
                        out=xln[par][:], in_=x2s[row0:row0 + 128, :]),
                        wait=[f_xl[par], (f_x2, SPILLS)],
                        inc=d_xl[par], inck=16)
                    ld = d_xl[par].n
                    P.step("dv", lambda v, par=par: v.reduce_sum(
                        musum[:], xln[par][:], axis=AX.X),
                        wait=[(d_xl[par], ld)], inc=dv_f)
                    P.step("ac", lambda a, par=par: a.activation(
                        out=xn2[:], in_=xln[par][:],
                        func=AF.Square, accum_out=sqsum[:]),
                        wait=[(d_xl[par], ld), (dv_f, sq_guard[0])]
                        + r_xn2.waits(),
                        inc=ac_f)
                    P.step("dv", lambda v: v.tensor_scalar(
                        out=negmu[:], in0=musum[:], scalar1=-1.0 / D,
                        scalar2=None, op0=ALU.mult), wait=[dv_f], inc=dv_f)
                    P.step("dv", lambda v: v.tensor_mul(mu2[:], negmu[:],
                                                        negmu[:]),
                           wait=[dv_f], inc=dv_f)
                    P.step("dv", lambda v: v.tensor_scalar(
                        out=varb[:], in0=sqsum[:], scalar1=1.0 / D,
                        scalar2=None, op0=ALU.mult),
                        wait=[ac_f, dv_f], inc=dv_f)
                    sq_guard[0] = dv_f.n
                    P.step("dv", lambda v: v.tensor_sub(varb[:], varb[:],
                                                        mu2[:]),
                           wait=[dv_f], inc=dv_f)
                    P.step("ac", lambda a: a.activation(
                        out=lnv[:], in_=varb[:], func=AF.Ln, bias=c_lneps[:]),
                        wait=[dv_f, ac_f], inc=ac_f)
                    P.step("ac", lambda a: a.activation(
                        out=rstd[:], in_=lnv[:], func=AF.Exp,
                        scale=c_nhalf[:]), wait=[ac_f, dv_f], inc=ac_f)
                    P.step("dv", lambda v, par=par: v.tensor_scalar(
                        out=xn2[:], in0=xln[par][:],
                        scalar1=negmu[:], scalar2=rstd[:], op0=ALU.add,
                        op1=ALU.mult),
                        wait=[ac_f] + r_xn2.waits(), inc=f_xl[par])
                    xnd = f_xl[par].n
                    for fc in range(NFC):
                        P.step("pe", lambda t, fc=fc: t.transpose(
                            pt_x[:, fc * 128:(fc + 1) * 128],
                            xn2[:, fc * 128:(fc + 1) * 128], id_bf[:]),
                            wait=([(f_xl[par], xnd)] + r_ptx.waits())
                            if fc == 0 else [],
                            inc=pe_ft if fc == NFC - 1 else None)
                    r_xn2.set(pe_ft)
                    P.step("dv", lambda v, tc=tc, s_=s_: v.tensor_copy(
                        Xn2T2[s_ % 2][:].rearrange("p (c f) -> p c f", f=512)
                        [:, :, tc * 128:(tc + 1) * 128],
                        pt_x[:].rearrange("p (c f) -> p c f", f=128)),
                        wait=[pe_ft]
                        + (r_Xn2Tb[s_ % 2].waits() if tc == 0 else []),
                        inc=dv_f)
                    r_ptx.set(dv_f)
                ln2_marks[s_] = dv_f.n

            emit_ln2(0)
            if B * NB > 1:
                emit_ln2(1)
            w1_marks = []
            HCG = FF // 4
            for hg in range(4):
                for fc in range(NFC):
                    P.step("po", lambda g, fc=fc, hg=hg: g.dma_start(
                        out=w1b[:, fc * FF + hg * HCG:
                                fc * FF + (hg + 1) * HCG],
                        in_=w1_d[fc * 128:(fc + 1) * 128,
                                 hg * HCG:(hg + 1) * HCG]),
                        wait=[(d_g, w1_marks[-1])]
                        if (fc == 0 and hg > 0) else [],
                        inc=d_g, inck=16)
                w1_marks.append(d_g.n)
            w2_marks = []
            for kg in range(4):
                for kc in range(kg * (NHC // 4), (kg + 1) * (NHC // 4)):
                    P.step("po", lambda g, kc=kc, kg=kg: g.dma_start(
                        out=w2b[:, kc * D:(kc + 1) * D],
                        in_=w2_d[kc * 128:(kc + 1) * 128, :]),
                        wait=[(d_g, (w2_marks[-1] if kg > 0
                                     else w1_marks[-1]))]
                        if kc == kg * (NHC // 4) else [],
                        inc=d_g, inck=16)
                w2_marks.append(d_g.n)

            for s in range(B * NB):
                b, qb = divmod(s, NB)
                win0 = qb * 512
                if s not in ln2_marks:
                    emit_ln2(s)
                if s + 1 < B * NB and s + 1 not in ln2_marks:
                    emit_ln2(s + 1)
                xn2T_ready = ln2_marks[s]
                for tc in range(4):
                    row0 = b * T + win0 + tc * 128
                    P.step("sy", lambda sy, row0=row0, tc=tc: sy.dma_start(
                        out=x2_buf[:, tc * D:(tc + 1) * D],
                        in_=x2s[row0:row0 + 128, :]),
                        wait=[f_x2, st_out] if tc == 0 else [], inc=d_x2,
                        inck=16)
                x2_here = d_x2.n

                pp1 = [(ps_c, r_psc), (ps_d, r_psd)]
                gelu_marks2 = {}
                for hc in range(0 if skip_ffn else NHC):
                    ps, r_ps = pp1[hc % 2]
                    for fc in range(NFC):
                        P.step("pe", lambda t, fc=fc, hc=hc, ps=ps, s=s: t.matmul(
                            ps[:, :],
                            w1b[:, fc * FF + hc * 128:
                                fc * FF + (hc + 1) * 128],
                            Xn2T2[s % 2][:, fc * 512:(fc + 1) * 512],
                            start=(fc == 0), stop=(fc == NFC - 1)),
                            wait=([(dv_f, xn2T_ready),
                                   (d_g, w1_marks[(hc * 128) // HCG])]
                                  + r_ps.waits()) if fc == 0 else [],
                            inc=pe_f if fc == NFC - 1 else None)
                    mmn = pe_f.n
                    P.step("ac", lambda a, hc=hc, ps=ps: a.activation(
                        out=hT[:, hc * 512:(hc + 1) * 512], in_=ps[:, :],
                        func=AF.Gelu, bias=b1col[:, hc:hc + 1]),
                        wait=[(pe_f, mmn), (pe_f, prev_w2_end),
                              (dv_su, SETUP)], inc=ac_f)
                    gelu_marks2[hc] = ac_f.n
                    r_ps.set(ac_f)
                r_Xn2Tb[s % 2].set(pe_f)
                pp2 = [(ps_a, r_psa), (ps_b, r_psb)]
                for oc in range(0 if skip_ffn else NOC):
                    ps, r_ps = pp2[oc % 2]
                    for kc in range(NHC):
                        P.step("pe", lambda t, oc=oc, kc=kc, ps=ps: t.matmul(
                            ps[:, :],
                            w2b[:, kc * D + oc * 128:
                                kc * D + (oc + 1) * 128],
                            hT[:, kc * 512:(kc + 1) * 512],
                            start=(kc == 0), stop=(kc == NHC - 1)),
                            wait=([(ac_f, gelu_marks2[kc]),
                                   (d_g, w2_marks[kc // (NHC // 4)])]
                                  + (r_ps.waits() if kc == 0 else [])),
                            inc=pe_f if kc == NHC - 1 else None)
                    mm2 = pe_f.n
                    ypar = oc % 2
                    P.step("dv", lambda v, oc=oc, ps=ps, ypar=ypar: v.tensor_scalar(
                        out=y_bf[ypar][:], in0=ps[:, :],
                        scalar1=b2col[:, oc:oc + 1], scalar2=None, op0=ALU.add),
                        wait=[(pe_f, mm2), (dv_su, SETUP)]
                        + r_ybf[ypar].waits(), inc=dv_f)
                    yn = dv_f.n
                    r_ps.set(dv_f)
                    for tc in range(4):
                        P.step("pe", lambda t, tc=tc, ypar=ypar: t.transpose(
                            pt_y[:, tc * 128:(tc + 1) * 128],
                            y_bf[ypar][:, tc * 128:(tc + 1) * 128], id_bf[:]),
                            wait=([(dv_f, yn)] + r_pty.waits())
                            if tc == 0 else [],
                            inc=pe_ft if tc == 3 else None)
                    r_ybf[ypar].set(pe_ft)
                    trn = pe_ft.n
                    for tc in range(4):
                        P.step("dv", lambda v, tc=tc, oc=oc: v.tensor_add(
                            x2_buf[:, tc * D + oc * 128:
                                   tc * D + (oc + 1) * 128],
                            x2_buf[:, tc * D + oc * 128:
                                   tc * D + (oc + 1) * 128],
                            pt_y[:, tc * 128:(tc + 1) * 128]),
                            wait=[(pe_ft, trn), (d_x2, x2_here)]
                            if tc == 0 else [],
                            inc=dv_f if tc == 3 else None)
                    r_pty.set(dv_f)
                prev_w2_end = pe_f.n
                add_done = dv_f.n
                for tc in range(4):
                    t0 = win0 + tc * 128
                    P.step("sy", lambda sy, b=b, t0=t0, tc=tc: sy.dma_start(
                        out=y_out[b, t0:t0 + 128, :],
                        in_=x2_buf[:, tc * D:(tc + 1) * D]),
                        wait=[(dv_f, add_done)] if tc == 0 else [],
                        inc=st_out, inck=16)

            P.step("po", lambda g: g.nop(), wait=[st_out])

        with nc.Block() as block:
            P.emit(block)

    return nc


_CACHE = {}


def _get_program():
    if "nc" not in _CACHE:
        _CACHE["nc"] = build_program()
    return _CACHE["nc"]


def kernel(L0_attn_x, L0_attn_W_DQ, L0_attn_W_UQ, L0_attn_W_DKV, L0_attn_W_UKV,
           L0_attn_W_QR, L0_attn_W_KR, L0_attn_W_O, L0_attn_q_norm_w,
           L0_attn_k_norm_w, L0_ffn_w1, L0_ffn_b1, L0_ffn_w2, L0_ffn_b2):
    x = np.ascontiguousarray(np.asarray(L0_attn_x, dtype=np.float32))
    n_cores = 8
    Bs = x.shape[0] // n_cores
    shards = [np.ascontiguousarray(x[i * Bs:(i + 1) * Bs])
              for i in range(n_cores)]
    w = {
        "WDQ": L0_attn_W_DQ, "WUQ": L0_attn_W_UQ, "WDKV": L0_attn_W_DKV,
        "WUKV": L0_attn_W_UKV, "WQR": L0_attn_W_QR, "WKR": L0_attn_W_KR,
        "WO": L0_attn_W_O, "QNW": L0_attn_q_norm_w, "KNW": L0_attn_k_norm_w,
        "W1": L0_ffn_w1, "B1": L0_ffn_b1, "W2": L0_ffn_w2, "B2": L0_ffn_b2,
    }
    w = {k: np.ascontiguousarray(np.asarray(v, dtype=np.float32))
         for k, v in w.items()}
    nc = _get_program()
    in_maps = [dict(x=shards[i], **w) for i in range(n_cores)]
    res = run_bass_kernel_spmd(nc, in_maps, core_ids=list(range(n_cores)))
    return np.concatenate([r["y"] for r in res.results], axis=0)


# version 30
# speedup vs baseline: 1.0123x; 1.0123x over previous
import numpy as np
from collections import deque

import concourse.bass as bass
import concourse.mybir as mybir
from concourse.bass_utils import run_bass_kernel_spmd
from contextlib import ExitStack

F32 = mybir.dt.float32
BF16 = mybir.dt.bfloat16
AF = mybir.ActivationFunctionType
ALU = mybir.AluOpType
AX = mybir.AxisListType

D = 1024
HD = 64
DL = 512
NFC = D // 128
LN_EPS = 1e-5
RMS_EPS = 1.1920929e-07
NEG_BIG = -1e30


class Sem:
    __slots__ = ("h", "n", "name")

    def __init__(self, nc, name):
        self.h = nc.alloc_semaphore(name)
        self.n = 0
        self.name = name


class Rel:
    __slots__ = ("w",)

    def __init__(self):
        self.w = []

    def set(self, *sems):
        self.w = [(s, s.n) for s in sems]

    def add(self, *sems):
        self.w += [(s, s.n) for s in sems]

    def waits(self):
        return list(self.w)


class Prog:
    ENGS = ("pe", "dv", "ac", "po", "sy")

    def __init__(self, nc):
        self.nc = nc
        self.ops = {e: [] for e in self.ENGS}

    def step(self, eng, fn, wait=(), inc=None, inck=1):
        ws = []
        for w in wait:
            if isinstance(w, Sem):
                ws.append((w, w.n))
            else:
                ws.append((w[0], w[1]))
        self.ops[eng].append((fn, ws, inc, inck))
        if inc is not None:
            inc.n += inck

    def emit(self, block):
        mapping = dict(pe=block.tensor, dv=block.vector, ac=block.scalar,
                       po=block.gpsimd, sy=block.sync)
        for eng in self.ENGS:
            ops = self.ops[eng]

            def body(E, ops=ops):
                last = {}
                for fn, ws, inc, inck in ops:
                    for s, cnt in ws:
                        if cnt > 0 and last.get(s.name, 0) < cnt:
                            E.wait_ge(s.h, cnt)
                            last[s.name] = cnt
                    instr = fn(E)
                    if inc is not None:
                        instr.then_inc(inc.h, inck)

            mapping[eng](body)


def build_program(B=4, T=2048, FF=4096, H=16, skip_attn=False, skip_ffn=False, skip_proj=False):
    assert T % 512 == 0 and FF % 256 == 0 and H % 2 == 0 and H <= 16
    NB = T // 512
    TC = T // 128
    HHALF = min(8, H)
    NGRP = (H + HHALF - 1) // HHALF
    GW = HHALF * HD
    NQT = 4
    NHC = FF // 128
    NOC = D // 128

    nc = bass.Bass("TRN2", target_bir_lowering=False)

    x_in = nc.dram_tensor("x", [B, T, D], F32, kind="ExternalInput").ap()
    wdq_d = nc.dram_tensor("WDQ", [DL, HD], F32, kind="ExternalInput").ap()
    wuq_d = nc.dram_tensor("WUQ", [HD, DL], F32, kind="ExternalInput").ap()
    wdkv_d = nc.dram_tensor("WDKV", [2 * DL, HD], F32, kind="ExternalInput").ap()
    wukv_d = nc.dram_tensor("WUKV", [2 * HD, 2 * DL], F32, kind="ExternalInput").ap()
    wqr_d = nc.dram_tensor("WQR", [HD, HD], F32, kind="ExternalInput").ap()
    wkr_d = nc.dram_tensor("WKR", [HD, HD], F32, kind="ExternalInput").ap()
    wo_d = nc.dram_tensor("WO", [HD, HD], F32, kind="ExternalInput").ap()
    qw_d = nc.dram_tensor("QNW", [HD], F32, kind="ExternalInput").ap()
    kw_d = nc.dram_tensor("KNW", [HD], F32, kind="ExternalInput").ap()
    w1_d = nc.dram_tensor("W1", [D, FF], F32, kind="ExternalInput").ap()
    b1_d = nc.dram_tensor("B1", [FF], F32, kind="ExternalInput").ap()
    w2_d = nc.dram_tensor("W2", [FF, D], F32, kind="ExternalInput").ap()
    b2_d = nc.dram_tensor("B2", [D], F32, kind="ExternalInput").ap()
    y_out = nc.dram_tensor("y", [B, T, D], F32, kind="ExternalOutput").ap()
    x2s = nc.dram_tensor("x2s", [B * T, D], F32, kind="Internal").ap()

    es = ExitStack()
    with es:
        P = Prog(nc)

        def sb(name, shape, dt=F32):
            return es.enter_context(nc.sbuf_tensor(name, shape, dt))

        id_bf = sb("id_bf", [128, 128], BF16)
        negmask = sb("negmask", [128, 128], F32)
        c0125 = sb("c0125", [128, 1], F32)
        c_lneps = sb("c_lneps", [128, 1], F32)
        c_rmseps = sb("c_rmseps", [128, 1], F32)
        c_nhalf = sb("c_nhalf", [128, 1], F32)
        c_inv64 = sb("c_inv64", [128, 1], F32)

        wqT2 = sb("wqT2", [128, HD], BF16)
        wkT2 = sb("wkT2", [128, HD], BF16)
        wvT2 = sb("wvT2", [128, HD], BF16)
        wqrT = sb("wqrT", [128, HD], BF16)
        wkrT = sb("wkrT", [128, HD], BF16)
        woT_pair = sb("woT_pair", [128, 128], BF16)
        qnw_col = sb("qnw_col", [HD, 1], F32)
        knw_col = sb("knw_col", [HD, 1], F32)
        b1col = sb("b1col", [128, NHC], F32)
        b2col = sb("b2col", [128, NOC], F32)

        x2_buf = sb("x2_buf", [128, 4 * D], F32)
        xln = [sb(f"xln{i}", [128, D], BF16) for i in range(2)]
        xn_bf2 = [sb(f"xn_bf{i}", [128, D], BF16) for i in range(2)]
        musum = sb("musum", [128, 1], F32)
        negmu = sb("negmu", [128, 1], F32)
        sqsum = sb("sqsum", [128, 1], F32)
        mu2 = sb("mu2", [128, 1], F32)
        varb = sb("varb", [128, 1], F32)
        lnv = sb("lnv", [128, 1], F32)
        rstd = sb("rstd", [128, 1], F32)

        ps_s0 = es.enter_context(nc.psum_tensor("ps_s0", [128, 512], F32))
        ps_s1 = es.enter_context(nc.psum_tensor("ps_s1", [128, 512], F32))
        ps_a = es.enter_context(nc.psum_tensor("ps_a", [128, 512], F32))
        ps_b = es.enter_context(nc.psum_tensor("ps_b", [128, 512], F32))
        ps_c = es.enter_context(nc.psum_tensor("ps_c", [128, 512], F32))
        ps_d = es.enter_context(nc.psum_tensor("ps_d", [128, 512], F32))
        pt_x = es.enter_context(nc.psum_tensor("pt_x", [128, 1024], BF16))
        pt_y = es.enter_context(nc.psum_tensor("pt_y", [128, 1024], BF16))

        d_w = Sem(nc, "d_w")
        d_x2 = Sem(nc, "d_x2")
        d_xl = [Sem(nc, f"d_xl{i}") for i in range(2)]
        d_g = Sem(nc, "d_g")
        po_set = Sem(nc, "po_set")
        po_chain = Sem(nc, "po_chain")
        pe_su = Sem(nc, "pe_su")
        dv_su = Sem(nc, "dv_su")
        pe_ln = Sem(nc, "pe_ln")
        dv_ln = Sem(nc, "dv_ln")
        ac_ln = Sem(nc, "ac_ln")
        pe_pr = Sem(nc, "pe_pr")
        ac_pr = Sem(nc, "ac_pr")
        dv_pr = Sem(nc, "dv_pr")
        pe_s = Sem(nc, "pe_s")
        dv_st = Sem(nc, "dv_st")
        ac_st = Sem(nc, "ac_st")
        po_st = Sem(nc, "po_st")
        ac_ex = Sem(nc, "ac_ex")
        pe_pt = Sem(nc, "pe_pt")
        dv_pt = Sem(nc, "dv_pt")
        ac_pt = Sem(nc, "ac_pt")
        pe_cx = Sem(nc, "pe_cx")
        dv_ao = Sem(nc, "dv_ao")
        pe_ao = Sem(nc, "pe_ao")
        dv_res = Sem(nc, "dv_res")
        f_x2 = Sem(nc, "f_x2")
        f_x2e = Sem(nc, "f_x2e")
        f_xl = [Sem(nc, f"f_xl{i}") for i in range(2)]
        pe_f = Sem(nc, "pe_f")
        ac_f = Sem(nc, "ac_f")
        dv_f = Sem(nc, "dv_f")
        pe_ft = Sem(nc, "pe_ft")
        st_out = Sem(nc, "st_out")

        r_ptx, r_pty = Rel(), Rel()
        r_xnbf2 = [Rel(), Rel()]
        r_psa, r_psb, r_psc, r_psd = Rel(), Rel(), Rel(), Rel()
        r_ps0, r_ps1 = Rel(), Rel()
        r_xnbf = Rel()
        r_S = [Rel(), Rel()]
        r_PA, r_PB = Rel(), Rel()
        r_PT = [Rel(), Rel()]

        r_win = Rel()
        r_kv = Rel()
        r_lbuf = Rel()

        P.step("po", lambda g: g.memset(id_bf[:], 0.0), inc=po_chain)
        P.step("po", lambda g: g.affine_select(
            out=id_bf[:], in_=id_bf[:], compare_op=ALU.not_equal, fill=1.0,
            base=0, pattern=[[-1, 128]], channel_multiplier=1),
            wait=[po_chain], inc=po_set)
        P.step("po", lambda g: g.memset(negmask[:], 0.0), inc=po_chain)
        P.step("po", lambda g: g.affine_select(
            out=negmask[:], in_=negmask[:], compare_op=ALU.is_ge, fill=NEG_BIG,
            base=0, pattern=[[-1, 128]], channel_multiplier=1),
            wait=[po_chain], inc=po_set)
        for buf, val in ((c0125, 0.125), (c_lneps, LN_EPS), (c_rmseps, RMS_EPS),
                         (c_nhalf, -0.5), (c_inv64, 1.0 / 64.0)):
            P.step("po", lambda g, buf=buf, val=val: g.memset(buf[:], val),
                   inc=po_set)
        P.step("po", lambda g: g.memset(woT_pair[:], 0.0), inc=po_set)
        POSET = po_set.n

        with ExitStack() as su:
            def tb(name, shape, dt=F32):
                return su.enter_context(nc.sbuf_tensor(name, shape, dt))

            id_f = tb("id_f", [128, 128])
            P.step("po", lambda g: g.memset(id_f[:], 0.0), inc=po_chain)
            P.step("po", lambda g: g.affine_select(
                out=id_f[:], in_=id_f[:], compare_op=ALU.not_equal, fill=1.0,
                base=0, pattern=[[-1, 128]], channel_multiplier=1),
                wait=[po_chain], inc=po_set)
            wdq_s = tb("wdq_s", [128, 4 * HD])
            wuq_s = tb("wuq_s", [HD, DL])
            wdkv_s = tb("wdkv_s", [128, 8 * HD])
            wukv_s = tb("wukv_s", [2 * HD, 2 * DL])
            wqr_s = tb("wqr_s", [HD, HD])
            wkr_s = tb("wkr_s", [HD, HD])
            wo_s = tb("wo_s", [HD, HD])
            wuqT_s = tb("wuqT_s", [128, 4 * HD])
            wukvT_s = tb("wukvT_s", [128, 8 * 2 * HD])
            b1_raw = tb("b1_raw", [NHC, 128])
            b2_raw = tb("b2_raw", [NOC, 128])

            def wload(dst, src):
                P.step("sy", lambda s, d=dst, sr=src: s.dma_start(out=d, in_=sr),
                       inc=d_w, inck=16)

            wload(wdq_s[:].rearrange("p (c f) -> p c f", f=HD),
                  wdq_d.rearrange("(c p) f -> p c f", p=128))
            wload(wuq_s[:], wuq_d)
            wload(wdkv_s[:].rearrange("p (c f) -> p c f", f=HD),
                  wdkv_d.rearrange("(c p) f -> p c f", p=128))
            wload(wukv_s[:], wukv_d)
            wload(wqr_s[:], wqr_d)
            wload(wkr_s[:], wkr_d)
            wload(wo_s[:], wo_d)
            wload(qnw_col[:], qw_d.rearrange("(f o) -> f o", o=1))
            wload(knw_col[:], kw_d.rearrange("(f o) -> f o", o=1))
            wload(b1_raw[:], b1_d.rearrange("(c p) -> c p", p=128))
            wload(b2_raw[:], b2_d.rearrange("(c p) -> c p", p=128))
            WALL = d_w.n

            for c in range(4):
                P.step("pe", lambda t, c=c: t.transpose(
                    ps_a[:, c * HD:(c + 1) * HD],
                    wuq_s[:, c * 128:(c + 1) * 128], id_f[0:HD, 0:HD]),
                    wait=[(d_w, WALL), po_set],
                    inc=pe_su if c == 3 else None)
            P.step("dv", lambda v: v.tensor_copy(wuqT_s[:], ps_a[:, 0:4 * HD]),
                   wait=[pe_su], inc=dv_su)
            r_psa.set(dv_su)
            for rnd in range(2):
                for c in range(4):
                    cc = rnd * 4 + c
                    P.step("pe", lambda t, cc=cc, c=c: t.transpose(
                        ps_b[:, c * 128:(c + 1) * 128],
                        wukv_s[:, cc * 128:(cc + 1) * 128], id_f[:]),
                        wait=r_psb.waits() if c == 0 else [],
                        inc=pe_su if c == 3 else None)
                P.step("dv", lambda v, rnd=rnd: v.tensor_copy(
                    wukvT_s[:, rnd * 512:(rnd + 1) * 512], ps_b[:, :]),
                    wait=[pe_su], inc=dv_su)
                r_psb.set(dv_su)
            P.step("pe", lambda t: t.transpose(ps_a[0:HD, 0:HD], wqr_s[:],
                                               id_f[0:HD, 0:HD]), wait=r_psa.waits())
            P.step("pe", lambda t: t.transpose(ps_a[0:HD, HD:2 * HD], wkr_s[:],
                                               id_f[0:HD, 0:HD]))
            P.step("pe", lambda t: t.transpose(ps_a[0:HD, 2 * HD:3 * HD],
                                               wo_s[:], id_f[0:HD, 0:HD]), inc=pe_su)
            P.step("dv", lambda v: v.tensor_copy(wqrT[0:HD, :], ps_a[0:HD, 0:HD]),
                   wait=[pe_su])
            P.step("dv", lambda v: v.tensor_copy(wqrT[HD:128, :], ps_a[0:HD, 0:HD]))
            P.step("dv", lambda v: v.tensor_copy(wkrT[0:HD, :], ps_a[0:HD, HD:2 * HD]))
            P.step("dv", lambda v: v.tensor_copy(wkrT[HD:128, :], ps_a[0:HD, HD:2 * HD]))
            P.step("dv", lambda v: v.tensor_copy(
                woT_pair[0:HD, 0:HD], ps_a[0:HD, 2 * HD:3 * HD]))
            P.step("dv", lambda v: v.tensor_copy(
                woT_pair[HD:128, HD:128], ps_a[0:HD, 2 * HD:3 * HD]), inc=dv_su)
            r_psa.set(dv_su)
            P.step("pe", lambda t: t.transpose(ps_b[:, 0:NHC], b1_raw[:],
                                               id_f[0:NHC, 0:NHC]), wait=r_psb.waits())
            P.step("pe", lambda t: t.transpose(ps_b[:, 256:256 + NOC], b2_raw[:],
                                               id_f[0:NOC, 0:NOC]), inc=pe_su)
            P.step("dv", lambda v: v.tensor_copy(b1col[:], ps_b[:, 0:NHC]),
                   wait=[pe_su])
            P.step("dv", lambda v: v.tensor_copy(b2col[:], ps_b[:, 256:256 + NOC]),
                   inc=dv_su)
            r_psb.set(dv_su)

            for c in range(4):
                P.step("pe", lambda t, c=c: t.matmul(
                    ps_c[0:HD, 0:HD],
                    wdq_s[:].rearrange("p (c f) -> p c f", f=HD)[:, c, :],
                    wuqT_s[:, c * HD:(c + 1) * HD],
                    start=(c == 0), stop=(c == 3)),
                    wait=[dv_su] if c == 0 else [],
                    inc=pe_su if c == 3 else None)
            for c in range(8):
                P.step("pe", lambda t, c=c: t.matmul(
                    ps_d[0:HD, 0:2 * HD],
                    wdkv_s[:].rearrange("p (c f) -> p c f", f=HD)[:, c, :],
                    wukvT_s[:, c * 2 * HD:(c + 1) * 2 * HD],
                    start=(c == 0), stop=(c == 7)),
                    inc=pe_su if c == 7 else None)
            P.step("dv", lambda v: v.tensor_copy(wqT2[0:HD, :], ps_c[0:HD, 0:HD]),
                   wait=[pe_su])
            P.step("dv", lambda v: v.tensor_copy(wqT2[HD:128, :],
                                                 ps_c[0:HD, 0:HD]))
            P.step("dv", lambda v: v.tensor_copy(wkT2[0:HD, :],
                                                 ps_d[0:HD, 0:HD]))
            P.step("dv", lambda v: v.tensor_copy(wkT2[HD:128, :],
                                                 ps_d[0:HD, 0:HD]))
            P.step("dv", lambda v: v.tensor_copy(wvT2[0:HD, :],
                                                 ps_d[0:HD, HD:2 * HD]))
            P.step("dv", lambda v: v.tensor_copy(wvT2[HD:128, :],
                                                 ps_d[0:HD, HD:2 * HD]),
                   inc=dv_su)
            r_psc.set(dv_su)
            r_psd.set(dv_su)
            SETUP = dv_su.n

        with ExitStack() as att:
            def asb(name, shape, dt=BF16):
                return att.enter_context(nc.sbuf_tensor(name, shape, dt))

            PCAP = 128 * sum(4 * (NB - 1) + qt + 1 for qt in range(NQT))
            Xn1T = asb("Xn1T", [64, H * 512])
            kstack = [asb(f"kst{h}", [128, T]) for h in range(H)]
            qstack = [asb(f"qst{h}", [128, 512]) for h in range(H)]
            vstore = [asb(f"vst{g}", [128, TC * GW]) for g in range(NGRP)]
            qn_tok = asb("qn_tok", [128, GW])
            msq = asb("msq", [128, HHALF], F32)
            ln8 = asb("ln8", [128, HHALF], F32)
            rstd8 = asb("rstd8", [128, HHALF], F32)
            rstdq_win = asb("rstdq_win", [128, 4 * H], F32)
            S2 = [asb(f"S_sb{i}", [128, T], F32) for i in range(2)]
            mcol = asb("mcol", [128, 6], F32)
            mrow = asb("mrow", [128, 1], F32)
            negm8 = asb("negm8", [128, 8], F32)
            P_A = asb("P_A", [128, PCAP])
            P_B = asb("P_B", [128, PCAP])
            PT2 = [asb(f"PT2{par}", [128, 1024]) for par in range(2)]
            l_buf = asb("l_buf", [128, 8], F32)
            rl_buf = asb("rl_buf", [128, 8], F32)
            ctx_bf = asb("ctx_bf", [128, 512])
            attn_bf = ctx_bf
            sqp_bf = asb("sqp_bf", [128, GW])
            r_ctxbf = Rel()
            r_sqp = Rel()
            r_attnbf = r_ctxbf
            r_qntok = Rel()
            r_rstd8, r_msq = Rel(), Rel()

            xnT_marks = {}

            def emit_W1(b_, qb_, thunks=None):
                win0_ = qb_ * 512

                def mk_chain(tc):
                    def chain():
                        t0 = win0_ + tc * 128
                        par = tc % 2
                        P.step("po", lambda g, b=b_, t0=t0, par=par: g.dma_start(
                            out=xln[par][:], in_=x_in[b, t0:t0 + 128, :]),
                            wait=[f_xl[par]], inc=d_xl[par], inck=16)
                        ld = d_xl[par].n
                        P.step("dv", lambda v, par=par: v.reduce_sum(
                            musum[:], xln[par][:], axis=AX.X),
                            wait=[(d_xl[par], ld)], inc=dv_ln)
                        P.step("ac", lambda a, par=par: a.activation(
                            out=xn_bf2[tc % 2][:], in_=xln[par][:], func=AF.Square,
                            accum_out=sqsum[:]),
                            wait=[(d_xl[par], ld), dv_ln]
                            + r_xnbf2[tc % 2].waits(),
                            inc=ac_ln)
                        P.step("dv", lambda v: v.tensor_scalar(
                            out=negmu[:], in0=musum[:], scalar1=-1.0 / D,
                            scalar2=None, op0=ALU.mult), wait=[dv_ln],
                            inc=dv_ln)
                        P.step("dv", lambda v: v.tensor_mul(
                            mu2[:], negmu[:], negmu[:]), wait=[dv_ln],
                            inc=dv_ln)
                        P.step("dv", lambda v: v.tensor_scalar(
                            out=varb[:], in0=sqsum[:], scalar1=1.0 / D,
                            scalar2=None, op0=ALU.mult),
                            wait=[ac_ln, dv_ln], inc=dv_ln)
                        P.step("dv", lambda v: v.tensor_sub(
                            varb[:], varb[:], mu2[:]), wait=[dv_ln],
                            inc=dv_ln)
                        P.step("ac", lambda a: a.activation(
                            out=lnv[:], in_=varb[:], func=AF.Ln,
                            bias=c_lneps[:]),
                            wait=[dv_ln, ac_ln, (po_set, POSET)], inc=ac_ln)
                        P.step("ac", lambda a: a.activation(
                            out=rstd[:], in_=lnv[:], func=AF.Exp,
                            scale=c_nhalf[:]),
                            wait=[ac_ln, f_xl[0], f_xl[1]], inc=ac_ln)
                        P.step("dv", lambda v, par=par, tc=tc: v.tensor_scalar(
                            out=xn_bf2[tc % 2][:], in0=xln[par][:],
                            scalar1=negmu[:],
                            scalar2=rstd[:], op0=ALU.add, op1=ALU.mult),
                            wait=[ac_ln] + r_xnbf2[tc % 2].waits(),
                            inc=f_xl[par])
                    return chain

                def mk_tr(tc):
                    def tr():
                        par = tc % 2
                        xn_done = f_xl[par].n
                        for fc in range(NFC):
                            P.step("pe", lambda t, fc=fc: t.transpose(
                                pt_x[:, fc * 128:(fc + 1) * 128],
                                xn_bf2[tc % 2][:, fc * 128:(fc + 1) * 128],
                                id_bf[:]),
                                wait=([(f_xl[par], xn_done), (po_set, POSET)]
                                      + r_ptx.waits() + [(pe_pr, pe_pr.n)])
                                if fc == 0 else [],
                                inc=pe_ln if fc == NFC - 1 else None)
                        r_xnbf2[tc % 2].set(pe_ln)
                        P.step("dv", lambda v, tc=tc: v.tensor_copy(
                            Xn1T[:].rearrange("p (c f) -> p c f", f=1024)
                            [:, :, tc * 128:(tc + 1) * 128],
                            pt_x[0:64, :].rearrange("p (c f) -> p c f", f=128)
                            [:, 0:H // 2, :]),
                            wait=[pe_ln], inc=dv_ln)
                        P.step("dv", lambda v, tc=tc: v.tensor_copy(
                            Xn1T[:].rearrange("p (c f) -> p c f", f=1024)
                            [:, :, 512 + tc * 128:512 + (tc + 1) * 128],
                            pt_x[64:128, :].rearrange("p (c f) -> p c f",
                                                      f=128)
                            [:, 0:H // 2, :]),
                            inc=dv_ln)
                        r_ptx.set(dv_ln)
                        if tc == 3:
                            xnT_marks[(b_, qb_)] = dv_ln.n
                    return tr

                seq = []
                for tc in range(4):
                    seq.append(mk_chain(tc))
                    seq.append(mk_tr(tc))
                if thunks is None:
                    for f in seq:
                        f()
                else:
                    thunks.extend(seq)

            wo_q = []
            wo_mark = [0]
            for b in range(B):
                for qb in range(NB):
                    win0 = qb * 512

                    for tc in range(4):
                        t0 = win0 + tc * 128
                        P.step("sy", lambda s, b=b, t0=t0, tc=tc: s.dma_start(
                            out=x2_buf[:, tc * D:(tc + 1) * D],
                            in_=x_in[b, t0:t0 + 128, :]),
                            wait=[f_x2] if tc == 0 else [], inc=d_x2, inck=16)
                    x2_ready = d_x2.n

                    rot4 = [(ps_a, r_psa), (ps_b, r_psb),
                            (ps_c, r_psc), (ps_d, r_psd)]
                    uidx = [0]
                    pendu = deque()

                    def w2_stage2(tc, tcg, grp, kind, ps, r_ps, mm_done):
                        if kind == "v":
                            P.step("ac", lambda a, grp=grp, tcg=tcg, ps=ps: a.activation(
                                out=vstore[grp][:, tcg * GW:(tcg + 1) * GW],
                                in_=ps[:, 0:GW], func=AF.Copy),
                                wait=[(pe_pr, mm_done)] + r_kv.waits(),
                                inc=ac_pr)
                            r_ps.set(ac_pr)
                            return

                        if kind == "q":
                            P.step("dv", lambda v, ps=ps: v.tensor_copy(
                                qn_tok[:], ps[:, 0:GW]),
                                wait=[(pe_pr, mm_done)] + r_qntok.waits(),
                                inc=dv_pr)
                            ev_mark = dv_pr.n
                            for hh in range(HHALF):
                                P.step("pe", lambda t, hh=hh: t.transpose(
                                    pt_y[0:HD, hh * 128:(hh + 1) * 128],
                                    qn_tok[:, hh * HD:(hh + 1) * HD],
                                    id_bf[:]),
                                    wait=([(dv_pr, ev_mark)] + r_pty.waits())
                                    if hh == 0 else [],
                                    inc=pe_pr if hh == HHALF - 1 else None)
                            r_qntok.set(pe_pr)
                            tr_done = pe_pr.n
                            for hh in range(HHALF):
                                h = grp * HHALF + hh
                                dst = qstack[h][0:HD, tc * 128:(tc + 1) * 128]
                                P.step("dv", lambda v, hh=hh, dst=dst: v.tensor_scalar(
                                    out=dst,
                                    in0=pt_y[0:HD, hh * 128:(hh + 1) * 128],
                                    scalar1=qnw_col[:], scalar2=None,
                                    op0=ALU.mult),
                                    wait=([(pe_pr, tr_done)] + r_win.waits())
                                    if hh == 0 else [],
                                    inc=dv_pr if hh == HHALF - 1 else None)
                            r_pty.set(dv_pr)
                            P.step("ac", lambda a, ps=ps: a.activation(
                                out=sqp_bf[:], in_=ps[:, 0:GW],
                                func=AF.Square),
                                wait=[(pe_pr, mm_done), (dv_pr, ev_mark)]
                                + r_sqp.waits(),
                                inc=ac_pr)
                            sq_mark = ac_pr.n
                            P.step("dv", lambda v: v.tensor_reduce(
                                out=msq[:],
                                in_=sqp_bf[:].rearrange("p (h f) -> p h f",
                                                        f=HD),
                                axis=AX.X, op=ALU.add),
                                wait=[(ac_pr, sq_mark)] + r_msq.waits(),
                                inc=dv_pr)
                            r_sqp.set(dv_pr)
                            red_mark = dv_pr.n
                            r_ps.set(dv_pr)
                            P.step("ac", lambda a: a.activation(
                                out=ln8[:], in_=msq[:], func=AF.Ln,
                                bias=c_rmseps[:], scale=c_inv64[:]),
                                wait=[(dv_pr, red_mark), ac_pr], inc=ac_pr)
                            r_msq.set(ac_pr)
                            P.step("ac", lambda a: a.activation(
                                out=rstd8[:], in_=ln8[:], func=AF.Exp,
                                scale=c_nhalf[:]),
                                wait=[ac_pr] + r_rstd8.waits(), inc=ac_pr)
                            rstd_done = ac_pr.n
                            c0 = tc * H + grp * HHALF
                            P.step("dv", lambda v, c0=c0: v.tensor_scalar(
                                out=rstdq_win[:, c0:c0 + HHALF],
                                in0=rstd8[:], scalar1=0.125, scalar2=None,
                                op0=ALU.mult),
                                wait=[(ac_pr, rstd_done)], inc=dv_pr)
                            r_rstd8.set(dv_pr)
                            return

                        P.step("ac", lambda a, ps=ps: a.activation(
                            out=sqp_bf[:], in_=ps[:, 0:GW],
                            func=AF.Square),
                            wait=[(pe_pr, mm_done)] + r_sqp.waits(),
                            inc=ac_pr)
                        sq_mark = ac_pr.n
                        P.step("dv", lambda v: v.tensor_reduce(
                            out=msq[:],
                            in_=sqp_bf[:].rearrange(
                                "p (h f) -> p h f", f=HD),
                            axis=AX.X, op=ALU.add),
                            wait=[(ac_pr, sq_mark)] + r_msq.waits(),
                            inc=dv_pr)
                        r_sqp.set(dv_pr)
                        red_mark = dv_pr.n
                        P.step("ac", lambda a: a.activation(
                            out=ln8[:], in_=msq[:], func=AF.Ln,
                            bias=c_rmseps[:], scale=c_inv64[:]),
                            wait=[(dv_pr, red_mark), ac_pr], inc=ac_pr)
                        r_msq.set(ac_pr)
                        P.step("ac", lambda a: a.activation(
                            out=rstd8[:], in_=ln8[:], func=AF.Exp,
                            scale=c_nhalf[:]),
                            wait=[ac_pr] + r_rstd8.waits(), inc=ac_pr)
                        rstd_done = ac_pr.n
                        for hh in range(HHALF):
                            P.step("dv", lambda v, hh=hh, ps=ps: v.tensor_scalar(
                                out=qn_tok[:, hh * HD:(hh + 1) * HD],
                                in0=ps[:, hh * HD:(hh + 1) * HD],
                                scalar1=rstd8[:, hh:hh + 1],
                                scalar2=None, op0=ALU.mult),
                                wait=([(ac_pr, rstd_done)]
                                      + r_qntok.waits())
                                if hh == 0 else [],
                                inc=dv_pr if hh == HHALF - 1 else None)
                        r_rstd8.set(dv_pr)
                        r_ps.set(dv_pr)
                        r_ps.add(ac_pr)
                        appl = dv_pr.n
                        for hh in range(HHALF):
                            P.step("pe", lambda t, hh=hh: t.transpose(
                                pt_y[0:HD, hh * 128:(hh + 1) * 128],
                                qn_tok[:, hh * HD:(hh + 1) * HD],
                                id_bf[:]),
                                wait=([(dv_pr, appl)] + r_pty.waits())
                                if hh == 0 else [],
                                inc=pe_pr if hh == HHALF - 1 else None)
                        r_qntok.set(pe_pr)
                        tr_done = pe_pr.n
                        for hh in range(HHALF):
                            h = grp * HHALF + hh
                            dst = kstack[h][0:HD, tcg * 128:(tcg + 1) * 128]
                            P.step("dv", lambda v, hh=hh, dst=dst: v.tensor_scalar(
                                out=dst,
                                in0=pt_y[0:HD, hh * 128:(hh + 1) * 128],
                                scalar1=knw_col[:], scalar2=None,
                                op0=ALU.mult),
                                wait=([(pe_pr, tr_done)] + r_kv.waits())
                                if hh == 0 else [],
                                inc=dv_pr if hh == HHALF - 1 else None)
                        r_pty.set(dv_pr)
                    if (b, qb) not in xnT_marks:
                        emit_W1(b, qb)
                    xnT_done = xnT_marks[(b, qb)]
                    for tc in range(4):
                        t0 = win0 + tc * 128
                        tcg = t0 // 128
                        for grp in range(0 if skip_proj else NGRP):
                            for kind in ("q", "k", "v"):
                                ps, r_ps = rot4[uidx[0] % 4]
                                uidx[0] += 1
                                wT = {"q": wqT2, "k": wkT2, "v": wvT2}[kind]
                                for hh in range(HHALF):
                                    h = grp * HHALF + hh
                                    P.step("pe", lambda t, h=h, hh=hh, wT=wT, tc=tc, ps=ps: t.matmul(
                                        ps[:, hh * HD:(hh + 1) * HD],
                                        Xn1T[0:HD, h * 512 + tc * 128:
                                             h * 512 + (tc + 1) * 128],
                                        wT[0:HD, :], start=True,
                                        stop=True),
                                        wait=([(dv_ln, xnT_done),
                                               (dv_su, SETUP)] + r_ps.waits())
                                        if hh == 0 else [],
                                        inc=pe_pr if hh == HHALF - 1 else None)
                                pendu.append((tc, tcg, grp, kind, ps, r_ps,
                                              pe_pr.n))
                                if len(pendu) >= 3:
                                    w2_stage2(*pendu.popleft())

                    while pendu:
                        w2_stage2(*pendu.popleft())
                    stacks_done = dv_pr.n
                    pingpong = [(ps_a, r_psa), (ps_b, r_psb)]
                    pidx = 0
                    for kind in (() if skip_proj else ("qr", "kr")):
                        wT = wqrT if kind == "qr" else wkrT
                        for h in range(H):
                            ps, r_ps = pingpong[pidx]
                            pidx ^= 1
                            if kind == "qr":
                                rhs = qstack[h][0:HD, 0:512]
                            else:
                                rhs = Xn1T[0:HD, h * 512:(h + 1) * 512]
                            P.step("pe", lambda t, wT=wT, rhs=rhs, ps=ps: t.matmul(
                                ps[0:HD, :], wT[0:HD, :], rhs,
                                start=True, stop=True),
                                wait=[(dv_pr, stacks_done)] + r_ps.waits(),
                                inc=pe_pr)
                            mmn = pe_pr.n
                            if kind == "qr":
                                dst = qstack[h][HD:128, 0:512]
                            else:
                                dst = kstack[h][HD:128, win0:win0 + 512]
                            P.step("ac", lambda a, dst=dst, ps=ps: a.activation(
                                out=dst, in_=ps[0:HD, :], func=AF.Copy),
                                wait=[(pe_pr, mmn)], inc=ac_pr)
                            r_ps.set(ac_pr)
                    win_ready = dv_pr.n
                    win_ready_ac = ac_pr.n

                    if skip_attn:
                        P.step("dv", lambda v: v.tensor_copy(
                            mrow[:], musum[:]), wait=[(d_x2, x2_ready)],
                            inc=dv_res)
                        sp = dv_res.n
                        for tc in range(4):
                            row0 = b * T + win0 + tc * 128
                            P.step("po", lambda g, row0=row0, tc=tc: g.dma_start(
                                out=x2s[row0:row0 + 128, :],
                                in_=x2_buf[:, tc * D:(tc + 1) * D]),
                                wait=[(dv_res, sp)] if tc == 0 else [],
                                inc=f_x2, inck=16)
                        r_win.set(pe_s, pe_pr)
                        continue
                    poffs = [0]
                    for qt in range(NQT):
                        poffs.append(poffs[-1] + 128 * (4 * qb + qt + 1))

                    stats_mark = {}
                    negm_free = {}
                    sring = [0]

                    def emit_slot_stats(p, hi, qt):
                        h = 2 * p + hi
                        slot = hi * 4 + qt
                        spar = slot % 2
                        S_sb = S2[spar]
                        Tq = 4 * qb + qt
                        klen = 128 * (Tq + 1)
                        offs = list(range(0, klen, 512))
                        nchunks = 0
                        sping = [(ps_s0, r_ps0), (ps_s1, r_ps1),
                                 (ps_a, r_psa), (ps_b, r_psb)]
                        first_stat = True
                        for ci, off in enumerate(offs):
                            w = min(512, klen - off)
                            ps, r_ps = sping[(sring[0] + ci) % 4]
                            has_diag = (off + w == klen)
                            P.step("pe", lambda t, h=h, qt=qt, off=off, w=w, ps=ps: t.matmul(
                                ps[:, 0:w],
                                qstack[h][:, qt * 128:(qt + 1) * 128],
                                kstack[h][:, off:off + w],
                                start=True, stop=True),
                                wait=[(dv_pr, win_ready),
                                      (ac_pr, win_ready_ac)] + r_ps.waits(),
                                inc=pe_s)
                            mmn = pe_s.n
                            wmain = w - 128 if has_diag else w
                            sg = r_S[spar].waits()
                            first_stat = False
                            if wmain > 0:
                                P.step("dv", lambda v, off=off, wmain=wmain, ps=ps, nchunks=nchunks, S_sb=S_sb: v.tensor_scalar(
                                    out=S_sb[:, off:off + wmain],
                                    in0=ps[:, 0:wmain], scalar1=1.0,
                                    scalar2=None, op0=ALU.mult,
                                    op1=ALU.max,
                                    accum_out=mcol[:, nchunks:nchunks + 1]),
                                    wait=[(pe_s, mmn)] + sg, inc=dv_st)
                                r_ps.set(dv_st)
                                nchunks += 1
                            if has_diag:
                                P.step("dv", lambda v, off=off, w=w, wmain=wmain, ps=ps, S_sb=S_sb: v.scalar_tensor_tensor(
                                    out=S_sb[:, off + wmain:off + w],
                                    in0=ps[:, wmain:w], scalar=1.0,
                                    in1=negmask[:], op0=ALU.mult,
                                    op1=ALU.add),
                                    wait=[(pe_s, mmn), (po_set, POSET)] + sg,
                                    inc=dv_st)
                                P.step("dv", lambda v, off=off, w=w, wmain=wmain, nchunks=nchunks, S_sb=S_sb: v.reduce_max(
                                    out=mcol[:, nchunks:nchunks + 1],
                                    in_=S_sb[:, off + wmain:off + w],
                                    axis=AX.X),
                                    wait=[dv_st], inc=dv_st)
                                nchunks += 1
                                r_ps.set(dv_st)
                                r_ps.add(ac_st)
                        sring[0] = (sring[0] + len(offs)) % 4
                        P.step("dv", lambda v, nchunks=nchunks: v.reduce_max(
                            out=mrow[:], in_=mcol[:, 0:nchunks], axis=AX.X),
                            wait=[dv_st], inc=dv_st)
                        rc = qt * H + h
                        P.step("dv", lambda v, slot=slot, rc=rc: v.tensor_scalar(
                            out=negm8[:, slot:slot + 1], in0=mrow[:],
                            scalar1=rstdq_win[:, rc:rc + 1], scalar2=-1.0,
                            op0=ALU.mult, op1=ALU.mult),
                            wait=[dv_st, (dv_pr, win_ready)]
                            + negm_free.get(slot, []),
                            inc=dv_st)
                        stats_mark[(p, slot)] = dv_st.n

                    def emit_slot_exp(p, hi, qt):
                        slot = hi * 4 + qt
                        spar = slot % 2
                        P_sb = (P_A, P_B)[hi]
                        r_P = (r_PA, r_PB)[hi]
                        klen = 128 * (4 * qb + qt + 1)
                        po0 = poffs[qt]
                        rc = qt * H + 2 * p + hi
                        P.step("ac", lambda a, P_sb=P_sb, po0=po0, klen=klen, slot=slot, spar=spar, rc=rc: a.activation(
                            out=P_sb[:, po0:po0 + klen],
                            in_=S2[spar][:, 0:klen], func=AF.Exp,
                            scale=rstdq_win[:, rc:rc + 1],
                            bias=negm8[:, slot:slot + 1],
                            accum_out=l_buf[:, slot:slot + 1]),
                            wait=[(dv_st, stats_mark[(p, slot)])]
                            + r_P.waits()
                            + (r_lbuf.waits() if slot == 0 else []),
                            inc=ac_ex)
                        negm_free[slot] = [(ac_ex, ac_ex.n)]
                        r_S[spar].set(ac_ex)

                    def emit_wo(p, rl_done):
                        P.step("pe", lambda t: t.matmul(
                            ps_d[:, :], woT_pair[:], ctx_bf[:], start=True,
                            stop=True),
                            wait=[dv_ao, (dv_su, SETUP)] + r_psd.waits(),
                            inc=pe_ao)
                        r_ctxbf.set(pe_ao)
                        P.step("dv", lambda v: v.tensor_copy(attn_bf[:],
                                                             ps_d[:, :]),
                               wait=[pe_ao] + r_attnbf.waits(), inc=dv_ao)
                        r_psd.set(dv_ao)
                        ab_done = dv_ao.n
                        for qt in range(NQT):
                            P.step("pe", lambda t, qt=qt: t.transpose(
                                pt_x[:, qt * 128:(qt + 1) * 128],
                                attn_bf[:, qt * 128:(qt + 1) * 128],
                                id_bf[:]),
                                wait=([(dv_ao, ab_done)] + r_ptx.waits())
                                if qt == 0 else [],
                                inc=pe_ao if qt == NQT - 1 else None)
                        r_attnbf.set(pe_ao)
                        trn = pe_ao.n
                        for qt in range(NQT):
                            for hi in range(2):
                                slot = hi * 4 + qt
                                col0 = qt * D + p * 128 + hi * HD
                                P.step("dv", lambda v, qt=qt, hi=hi, slot=slot, col0=col0: v.scalar_tensor_tensor(
                                    out=x2_buf[:, col0:col0 + HD],
                                    in0=pt_x[:, qt * 128 + hi * HD:
                                             qt * 128 + (hi + 1) * HD],
                                    scalar=rl_buf[:, slot:slot + 1],
                                    in1=x2_buf[:, col0:col0 + HD],
                                    op0=ALU.mult, op1=ALU.add),
                                    wait=([(pe_ao, trn), (dv_res, rl_done),
                                           (d_x2, x2_ready)]
                                          if (qt == 0 and hi == 0) else []),
                                    inc=(dv_res
                                         if (qt == NQT - 1 and hi == 1)
                                         else None))
                        r_ptx.set(dv_res)
                        return dv_res.n

                    def emit_pair_ctx(p):
                        if wo_q:
                            wo_mark[0] = wo_q.pop(0)()
                        P.step("dv", lambda v: v.reciprocal(rl_buf[:],
                                                            l_buf[:]),
                               wait=[ac_ex, (dv_res, wo_mark[0])],
                               inc=dv_res)
                        r_lbuf.set(dv_res)
                        rl_done = dv_res.n
                        nj = 4 * qb + 4
                        ptb = (pt_x, pt_y)
                        rptb = (r_ptx, r_pty)
                        for j in range(nj):
                            par = j % 2
                            qt_min = max(0, j - 4 * qb)
                            for hi in range(2):
                                P_sb = (P_A, P_B)[hi]
                                for qt in range(qt_min, NQT):
                                    src0 = poffs[qt] + j * 128
                                    P.step("pe", lambda t, P_sb=P_sb, hi=hi, qt=qt, src0=src0, par=par: t.transpose(
                                        ptb[par][:, hi * 512 + qt * 128:
                                                 hi * 512 + (qt + 1) * 128],
                                        P_sb[:, src0:src0 + 128], id_bf[:]),
                                        wait=([ac_ex] + rptb[par].waits())
                                        if (hi == 0 and qt == qt_min) else [],
                                        inc=pe_pt
                                        if (hi == 1 and qt == NQT - 1)
                                        else None)
                            trn = pe_pt.n
                            P.step("dv", lambda v, par=par, qt_min=qt_min: v.tensor_copy(
                                PT2[par][:].rearrange("p (h f) -> p h f",
                                                      f=512)
                                [:, :, qt_min * 128:512],
                                ptb[par][:].rearrange("p (h f) -> p h f",
                                                      f=512)
                                [:, :, qt_min * 128:512]),
                                wait=[(pe_pt, trn)] + r_PT[par].waits(),
                                inc=dv_pt)
                            ev_w = (dv_pt, dv_pt.n)
                            rptb[par].set(dv_pt)
                            for hi in range(2):
                                h = 2 * p + hi
                                grp = h // HHALF
                                hh = h % HHALF
                                v0 = j * GW + hh * HD
                                P.step("pe", lambda t, hi=hi, grp=grp, v0=v0, par=par, qt_min=qt_min, j=j, nj=nj: t.matmul(
                                    ps_c[hi * HD:(hi + 1) * HD,
                                         qt_min * 128:512],
                                    vstore[grp][:, v0:v0 + HD],
                                    PT2[par][:, hi * 512 + qt_min * 128:
                                             hi * 512 + 512],
                                    start=(j == 0), stop=(j == nj - 1),
                                    tile_position=(0, hi * HD),
                                    skip_group_check=True),
                                    wait=([ev_w,
                                           (dv_pr, win_ready),
                                           (ac_pr, win_ready_ac)]
                                          + (r_psc.waits() if j == 0 else []))
                                    if hi == 0 else [],
                                    inc=pe_cx if hi == 1 else None)
                            r_PT[par].set(pe_cx)
                        r_PA.set(pe_pt)
                        r_PB.set(pe_pt)

                        P.step("dv", lambda v: v.tensor_copy(ctx_bf[:],
                                                             ps_c[:, :]),
                               wait=[pe_cx] + r_ctxbf.waits(), inc=dv_ao)
                        r_psc.set(dv_ao)
                        wo_q.append(lambda p=p, rl_done=rl_done:
                                    emit_wo(p, rl_done))

                    pend = deque()
                    w1q = []
                    slot_ctr = [0]
                    for p in range(H // 2):
                        if p == H // 2 - 3:
                            nb, nqb = (b, qb + 1) if qb + 1 < NB else \
                                      (b + 1, 0)
                            if nb < B:
                                emit_W1(nb, nqb, thunks=w1q)
                        for hi in range(2):
                            for qt in range(NQT):
                                slot_ctr[0] += 1
                                if w1q and slot_ctr[0] % 3 == 0:
                                    w1q.pop(0)()
                                if len(pend) >= 2:
                                    g = pend.popleft()
                                    emit_slot_exp(*g)
                                    if g[1] == 1 and g[2] == NQT - 1:
                                        emit_pair_ctx(g[0])
                                emit_slot_stats(p, hi, qt)
                                pend.append((p, hi, qt))
                    while pend:
                        g = pend.popleft()
                        emit_slot_exp(*g)
                        if g[1] == 1 and g[2] == NQT - 1:
                            emit_pair_ctx(g[0])
                    while w1q:
                        w1q.pop(0)()
                    while wo_q:
                        wo_mark[0] = wo_q.pop(0)()
                    sp = dv_res.n
                    for tc in range(4):
                        row0 = b * T + win0 + tc * 128
                        P.step("po", lambda g, row0=row0, tc=tc: g.dma_start(
                            out=x2s[row0:row0 + 128, :],
                            in_=x2_buf[:, tc * D:(tc + 1) * D]),
                            wait=[(dv_res, sp)] if tc == 0 else [],
                            inc=f_x2, inck=16)
                    r_win.set(pe_s, pe_pr)
                r_kv.set(pe_s, pe_cx)

        with ExitStack() as ffn:
            def fsb(name, shape, dt=BF16):
                return ffn.enter_context(nc.sbuf_tensor(name, shape, dt))

            w1b = fsb("w1b", [128, NFC * FF])
            w2b = fsb("w2b", [128, NHC * D])
            xn2 = fsb("xn2", [128, D])
            Xn2T2 = [fsb(f"Xn2T{i}", [128, NFC * 512]) for i in range(2)]
            r_Xn2Tb = [Rel(), Rel()]
            hT = fsb("hT", [128, NHC * 512])
            y_bf = [fsb(f"ybf{i}", [128, 512]) for i in range(2)]
            r_xn2, r_Xn2T = Rel(), Rel()
            r_ybf = [Rel(), Rel()]

            SPILLS = f_x2.n

            prev_w2_end = 0
            ln2_marks = {}
            sq_guard = [0]

            def emit_ln2(s_):
                b_, qb_ = divmod(s_, NB)
                win0_ = qb_ * 512
                for tc in range(4):
                    row0 = b_ * T + win0_ + tc * 128
                    par = tc % 2
                    P.step("po", lambda g, row0=row0, par=par: g.dma_start(
                        out=xln[par][:], in_=x2s[row0:row0 + 128, :]),
                        wait=[f_xl[par], (f_x2, SPILLS)],
                        inc=d_xl[par], inck=16)
                    ld = d_xl[par].n
                    P.step("dv", lambda v, par=par: v.reduce_sum(
                        musum[:], xln[par][:], axis=AX.X),
                        wait=[(d_xl[par], ld)], inc=dv_f)
                    P.step("ac", lambda a, par=par: a.activation(
                        out=xn2[:], in_=xln[par][:],
                        func=AF.Square, accum_out=sqsum[:]),
                        wait=[(d_xl[par], ld), (dv_f, sq_guard[0])]
                        + r_xn2.waits(),
                        inc=ac_f)
                    P.step("dv", lambda v: v.tensor_scalar(
                        out=negmu[:], in0=musum[:], scalar1=-1.0 / D,
                        scalar2=None, op0=ALU.mult), wait=[dv_f], inc=dv_f)
                    P.step("dv", lambda v: v.tensor_mul(mu2[:], negmu[:],
                                                        negmu[:]),
                           wait=[dv_f], inc=dv_f)
                    P.step("dv", lambda v: v.tensor_scalar(
                        out=varb[:], in0=sqsum[:], scalar1=1.0 / D,
                        scalar2=None, op0=ALU.mult),
                        wait=[ac_f, dv_f], inc=dv_f)
                    sq_guard[0] = dv_f.n
                    P.step("dv", lambda v: v.tensor_sub(varb[:], varb[:],
                                                        mu2[:]),
                           wait=[dv_f], inc=dv_f)
                    P.step("ac", lambda a: a.activation(
                        out=lnv[:], in_=varb[:], func=AF.Ln, bias=c_lneps[:]),
                        wait=[dv_f, ac_f], inc=ac_f)
                    P.step("ac", lambda a: a.activation(
                        out=rstd[:], in_=lnv[:], func=AF.Exp,
                        scale=c_nhalf[:]), wait=[ac_f, dv_f], inc=ac_f)
                    P.step("dv", lambda v, par=par: v.tensor_scalar(
                        out=xn2[:], in0=xln[par][:],
                        scalar1=negmu[:], scalar2=rstd[:], op0=ALU.add,
                        op1=ALU.mult),
                        wait=[ac_f] + r_xn2.waits(), inc=f_xl[par])
                    xnd = f_xl[par].n
                    for fc in range(NFC):
                        P.step("pe", lambda t, fc=fc: t.transpose(
                            pt_x[:, fc * 128:(fc + 1) * 128],
                            xn2[:, fc * 128:(fc + 1) * 128], id_bf[:]),
                            wait=([(f_xl[par], xnd)] + r_ptx.waits())
                            if fc == 0 else [],
                            inc=pe_ft if fc == NFC - 1 else None)
                    r_xn2.set(pe_ft)
                    P.step("dv", lambda v, tc=tc, s_=s_: v.tensor_copy(
                        Xn2T2[s_ % 2][:].rearrange("p (c f) -> p c f", f=512)
                        [:, :, tc * 128:(tc + 1) * 128],
                        pt_x[:].rearrange("p (c f) -> p c f", f=128)),
                        wait=[pe_ft]
                        + (r_Xn2Tb[s_ % 2].waits() if tc == 0 else []),
                        inc=dv_f)
                    r_ptx.set(dv_f)
                ln2_marks[s_] = dv_f.n

            emit_ln2(0)
            if B * NB > 1:
                emit_ln2(1)
            w1_marks = []
            HCG = FF // 4
            for hg in range(4):
                for fc in range(NFC):
                    P.step("po", lambda g, fc=fc, hg=hg: g.dma_start(
                        out=w1b[:, fc * FF + hg * HCG:
                                fc * FF + (hg + 1) * HCG],
                        in_=w1_d[fc * 128:(fc + 1) * 128,
                                 hg * HCG:(hg + 1) * HCG]),
                        wait=[(d_g, w1_marks[-1])]
                        if (fc == 0 and hg > 0) else [],
                        inc=d_g, inck=16)
                w1_marks.append(d_g.n)
            w2_marks = []
            for kg in range(4):
                for kc in range(kg * (NHC // 4), (kg + 1) * (NHC // 4)):
                    P.step("po", lambda g, kc=kc, kg=kg: g.dma_start(
                        out=w2b[:, kc * D:(kc + 1) * D],
                        in_=w2_d[kc * 128:(kc + 1) * 128, :]),
                        wait=[(d_g, (w2_marks[-1] if kg > 0
                                     else w1_marks[-1]))]
                        if kc == kg * (NHC // 4) else [],
                        inc=d_g, inck=16)
                w2_marks.append(d_g.n)

            for s in range(B * NB):
                b, qb = divmod(s, NB)
                win0 = qb * 512
                if s not in ln2_marks:
                    emit_ln2(s)
                if s + 1 < B * NB and s + 1 not in ln2_marks:
                    emit_ln2(s + 1)
                xn2T_ready = ln2_marks[s]
                for tc in range(4):
                    row0 = b * T + win0 + tc * 128
                    P.step("sy", lambda sy, row0=row0, tc=tc: sy.dma_start(
                        out=x2_buf[:, tc * D:(tc + 1) * D],
                        in_=x2s[row0:row0 + 128, :]),
                        wait=[f_x2, st_out] if tc == 0 else [], inc=d_x2,
                        inck=16)
                x2_here = d_x2.n

                pp1 = [(ps_c, r_psc), (ps_d, r_psd)]
                gelu_marks2 = {}
                for hc in range(0 if skip_ffn else NHC):
                    ps, r_ps = pp1[hc % 2]
                    for fc in range(NFC):
                        P.step("pe", lambda t, fc=fc, hc=hc, ps=ps, s=s: t.matmul(
                            ps[:, :],
                            w1b[:, fc * FF + hc * 128:
                                fc * FF + (hc + 1) * 128],
                            Xn2T2[s % 2][:, fc * 512:(fc + 1) * 512],
                            start=(fc == 0), stop=(fc == NFC - 1)),
                            wait=([(dv_f, xn2T_ready),
                                   (d_g, w1_marks[(hc * 128) // HCG])]
                                  + r_ps.waits()) if fc == 0 else [],
                            inc=pe_f if fc == NFC - 1 else None)
                    mmn = pe_f.n
                    P.step("ac", lambda a, hc=hc, ps=ps: a.activation(
                        out=hT[:, hc * 512:(hc + 1) * 512], in_=ps[:, :],
                        func=AF.Gelu, bias=b1col[:, hc:hc + 1]),
                        wait=[(pe_f, mmn), (pe_f, prev_w2_end),
                              (dv_su, SETUP)], inc=ac_f)
                    gelu_marks2[hc] = ac_f.n
                    r_ps.set(ac_f)
                r_Xn2Tb[s % 2].set(pe_f)
                pp2 = [(ps_a, r_psa), (ps_b, r_psb)]
                for oc in range(0 if skip_ffn else NOC):
                    ps, r_ps = pp2[oc % 2]
                    for kc in range(NHC):
                        P.step("pe", lambda t, oc=oc, kc=kc, ps=ps: t.matmul(
                            ps[:, :],
                            w2b[:, kc * D + oc * 128:
                                kc * D + (oc + 1) * 128],
                            hT[:, kc * 512:(kc + 1) * 512],
                            start=(kc == 0), stop=(kc == NHC - 1)),
                            wait=([(ac_f, gelu_marks2[kc]),
                                   (d_g, w2_marks[kc // (NHC // 4)])]
                                  + (r_ps.waits() if kc == 0 else [])),
                            inc=pe_f if kc == NHC - 1 else None)
                    mm2 = pe_f.n
                    ypar = oc % 2
                    P.step("dv", lambda v, oc=oc, ps=ps, ypar=ypar: v.tensor_scalar(
                        out=y_bf[ypar][:], in0=ps[:, :],
                        scalar1=b2col[:, oc:oc + 1], scalar2=None, op0=ALU.add),
                        wait=[(pe_f, mm2), (dv_su, SETUP)]
                        + r_ybf[ypar].waits(), inc=dv_f)
                    yn = dv_f.n
                    r_ps.set(dv_f)
                    for tc in range(4):
                        P.step("pe", lambda t, tc=tc, ypar=ypar: t.transpose(
                            pt_y[:, tc * 128:(tc + 1) * 128],
                            y_bf[ypar][:, tc * 128:(tc + 1) * 128], id_bf[:]),
                            wait=([(dv_f, yn)] + r_pty.waits())
                            if tc == 0 else [],
                            inc=pe_ft if tc == 3 else None)
                    r_ybf[ypar].set(pe_ft)
                    trn = pe_ft.n
                    for tc in range(4):
                        P.step("dv", lambda v, tc=tc, oc=oc: v.tensor_add(
                            x2_buf[:, tc * D + oc * 128:
                                   tc * D + (oc + 1) * 128],
                            x2_buf[:, tc * D + oc * 128:
                                   tc * D + (oc + 1) * 128],
                            pt_y[:, tc * 128:(tc + 1) * 128]),
                            wait=[(pe_ft, trn), (d_x2, x2_here)]
                            if tc == 0 else [],
                            inc=dv_f if tc == 3 else None)
                    r_pty.set(dv_f)
                prev_w2_end = pe_f.n
                add_done = dv_f.n
                for tc in range(4):
                    t0 = win0 + tc * 128
                    P.step("sy", lambda sy, b=b, t0=t0, tc=tc: sy.dma_start(
                        out=y_out[b, t0:t0 + 128, :],
                        in_=x2_buf[:, tc * D:(tc + 1) * D]),
                        wait=[(dv_f, add_done)] if tc == 0 else [],
                        inc=st_out, inck=16)

            P.step("po", lambda g: g.nop(), wait=[st_out])

        with nc.Block() as block:
            P.emit(block)

    return nc


_CACHE = {}


def _get_program():
    if "nc" not in _CACHE:
        _CACHE["nc"] = build_program()
    return _CACHE["nc"]


def kernel(L0_attn_x, L0_attn_W_DQ, L0_attn_W_UQ, L0_attn_W_DKV, L0_attn_W_UKV,
           L0_attn_W_QR, L0_attn_W_KR, L0_attn_W_O, L0_attn_q_norm_w,
           L0_attn_k_norm_w, L0_ffn_w1, L0_ffn_b1, L0_ffn_w2, L0_ffn_b2):
    x = np.ascontiguousarray(np.asarray(L0_attn_x, dtype=np.float32))
    n_cores = 8
    Bs = x.shape[0] // n_cores
    shards = [np.ascontiguousarray(x[i * Bs:(i + 1) * Bs])
              for i in range(n_cores)]
    w = {
        "WDQ": L0_attn_W_DQ, "WUQ": L0_attn_W_UQ, "WDKV": L0_attn_W_DKV,
        "WUKV": L0_attn_W_UKV, "WQR": L0_attn_W_QR, "WKR": L0_attn_W_KR,
        "WO": L0_attn_W_O, "QNW": L0_attn_q_norm_w, "KNW": L0_attn_k_norm_w,
        "W1": L0_ffn_w1, "B1": L0_ffn_b1, "W2": L0_ffn_w2, "B2": L0_ffn_b2,
    }
    w = {k: np.ascontiguousarray(np.asarray(v, dtype=np.float32))
         for k, v in w.items()}
    nc = _get_program()
    in_maps = [dict(x=shards[i], **w) for i in range(n_cores)]
    res = run_bass_kernel_spmd(nc, in_maps, core_ids=list(range(n_cores)))
    return np.concatenate([r["y"] for r in res.results], axis=0)


# version 31
# speedup vs baseline: 1.0255x; 1.0130x over previous
import numpy as np
from collections import deque

import concourse.bass as bass
import concourse.mybir as mybir
from concourse.bass_utils import run_bass_kernel_spmd
from contextlib import ExitStack

F32 = mybir.dt.float32
BF16 = mybir.dt.bfloat16
AF = mybir.ActivationFunctionType
ALU = mybir.AluOpType
AX = mybir.AxisListType

D = 1024
HD = 64
DL = 512
NFC = D // 128
LN_EPS = 1e-5
RMS_EPS = 1.1920929e-07
NEG_BIG = -1e30


class Sem:
    __slots__ = ("h", "n", "name")

    def __init__(self, nc, name):
        self.h = nc.alloc_semaphore(name)
        self.n = 0
        self.name = name


class Rel:
    __slots__ = ("w",)

    def __init__(self):
        self.w = []

    def set(self, *sems):
        self.w = [(s, s.n) for s in sems]

    def add(self, *sems):
        self.w += [(s, s.n) for s in sems]

    def waits(self):
        return list(self.w)


class Prog:
    ENGS = ("pe", "dv", "ac", "po", "sy")

    def __init__(self, nc):
        self.nc = nc
        self.ops = {e: [] for e in self.ENGS}

    def step(self, eng, fn, wait=(), inc=None, inck=1):
        ws = []
        for w in wait:
            if isinstance(w, Sem):
                ws.append((w, w.n))
            else:
                ws.append((w[0], w[1]))
        self.ops[eng].append((fn, ws, inc, inck))
        if inc is not None:
            inc.n += inck

    def emit(self, block):
        mapping = dict(pe=block.tensor, dv=block.vector, ac=block.scalar,
                       po=block.gpsimd, sy=block.sync)
        for eng in self.ENGS:
            ops = self.ops[eng]

            def body(E, ops=ops):
                last = {}
                for fn, ws, inc, inck in ops:
                    for s, cnt in ws:
                        if cnt > 0 and last.get(s.name, 0) < cnt:
                            E.wait_ge(s.h, cnt)
                            last[s.name] = cnt
                    instr = fn(E)
                    if inc is not None:
                        instr.then_inc(inc.h, inck)

            mapping[eng](body)


def build_program(B=4, T=2048, FF=4096, H=16, skip_attn=False, skip_ffn=False, skip_proj=False):
    assert T % 512 == 0 and FF % 256 == 0 and H % 2 == 0 and H <= 16
    NB = T // 512
    TC = T // 128
    HHALF = min(8, H)
    NGRP = (H + HHALF - 1) // HHALF
    GW = HHALF * HD
    NQT = 4
    NHC = FF // 128
    NOC = D // 128

    nc = bass.Bass("TRN2", target_bir_lowering=False)

    x_in = nc.dram_tensor("x", [B, T, D], F32, kind="ExternalInput").ap()
    wdq_d = nc.dram_tensor("WDQ", [DL, HD], F32, kind="ExternalInput").ap()
    wuq_d = nc.dram_tensor("WUQ", [HD, DL], F32, kind="ExternalInput").ap()
    wdkv_d = nc.dram_tensor("WDKV", [2 * DL, HD], F32, kind="ExternalInput").ap()
    wukv_d = nc.dram_tensor("WUKV", [2 * HD, 2 * DL], F32, kind="ExternalInput").ap()
    wqr_d = nc.dram_tensor("WQR", [HD, HD], F32, kind="ExternalInput").ap()
    wkr_d = nc.dram_tensor("WKR", [HD, HD], F32, kind="ExternalInput").ap()
    wo_d = nc.dram_tensor("WO", [HD, HD], F32, kind="ExternalInput").ap()
    qw_d = nc.dram_tensor("QNW", [HD], F32, kind="ExternalInput").ap()
    kw_d = nc.dram_tensor("KNW", [HD], F32, kind="ExternalInput").ap()
    w1_d = nc.dram_tensor("W1", [D, FF], F32, kind="ExternalInput").ap()
    b1_d = nc.dram_tensor("B1", [FF], F32, kind="ExternalInput").ap()
    w2_d = nc.dram_tensor("W2", [FF, D], F32, kind="ExternalInput").ap()
    b2_d = nc.dram_tensor("B2", [D], F32, kind="ExternalInput").ap()
    y_out = nc.dram_tensor("y", [B, T, D], F32, kind="ExternalOutput").ap()
    x2s = nc.dram_tensor("x2s", [B * T, D], F32, kind="Internal").ap()

    es = ExitStack()
    with es:
        P = Prog(nc)

        def sb(name, shape, dt=F32):
            return es.enter_context(nc.sbuf_tensor(name, shape, dt))

        id_bf = sb("id_bf", [128, 128], BF16)
        negmask = sb("negmask", [128, 128], F32)
        c0125 = sb("c0125", [128, 1], F32)
        c_lneps = sb("c_lneps", [128, 1], F32)
        c_rmseps = sb("c_rmseps", [128, 1], F32)
        c_nhalf = sb("c_nhalf", [128, 1], F32)
        c_inv64 = sb("c_inv64", [128, 1], F32)

        wqT2 = sb("wqT2", [128, HD], BF16)
        wkT2 = sb("wkT2", [128, HD], BF16)
        wvT2 = sb("wvT2", [128, HD], BF16)
        wqrT = sb("wqrT", [128, HD], BF16)
        wkrT = sb("wkrT", [128, HD], BF16)
        woT_pair = sb("woT_pair", [128, 128], BF16)
        qnw_col = sb("qnw_col", [HD, 1], F32)
        knw_col = sb("knw_col", [HD, 1], F32)
        b1col = sb("b1col", [128, NHC], F32)
        b2col = sb("b2col", [128, NOC], F32)

        x2_buf = sb("x2_buf", [128, 4 * D], F32)
        xln = [sb(f"xln{i}", [128, D], BF16) for i in range(2)]
        xn_bf2 = [sb(f"xn_bf{i}", [128, D], BF16) for i in range(2)]
        musum = sb("musum", [128, 1], F32)
        negmu = sb("negmu", [128, 1], F32)
        sqsum = sb("sqsum", [128, 1], F32)
        mu2 = sb("mu2", [128, 1], F32)
        varb = sb("varb", [128, 1], F32)
        lnv = sb("lnv", [128, 1], F32)
        rstd = sb("rstd", [128, 1], F32)

        ps_s0 = es.enter_context(nc.psum_tensor("ps_s0", [128, 512], F32))
        ps_s1 = es.enter_context(nc.psum_tensor("ps_s1", [128, 512], F32))
        ps_a = es.enter_context(nc.psum_tensor("ps_a", [128, 512], F32))
        ps_b = es.enter_context(nc.psum_tensor("ps_b", [128, 512], F32))
        ps_c = es.enter_context(nc.psum_tensor("ps_c", [128, 512], F32))
        ps_d = es.enter_context(nc.psum_tensor("ps_d", [128, 512], F32))
        pt_x = es.enter_context(nc.psum_tensor("pt_x", [128, 1024], BF16))
        pt_y = es.enter_context(nc.psum_tensor("pt_y", [128, 1024], BF16))

        d_w = Sem(nc, "d_w")
        d_x2 = Sem(nc, "d_x2")
        d_xl = [Sem(nc, f"d_xl{i}") for i in range(2)]
        d_g = Sem(nc, "d_g")
        po_set = Sem(nc, "po_set")
        po_chain = Sem(nc, "po_chain")
        pe_su = Sem(nc, "pe_su")
        dv_su = Sem(nc, "dv_su")
        pe_ln = Sem(nc, "pe_ln")
        dv_ln = Sem(nc, "dv_ln")
        ac_ln = Sem(nc, "ac_ln")
        pe_pr = Sem(nc, "pe_pr")
        ac_pr = Sem(nc, "ac_pr")
        dv_pr = Sem(nc, "dv_pr")
        pe_s = Sem(nc, "pe_s")
        dv_st = Sem(nc, "dv_st")
        ac_st = Sem(nc, "ac_st")
        po_st = Sem(nc, "po_st")
        ac_ex = Sem(nc, "ac_ex")
        pe_pt = Sem(nc, "pe_pt")
        dv_pt = Sem(nc, "dv_pt")
        ac_pt = Sem(nc, "ac_pt")
        pe_cx = Sem(nc, "pe_cx")
        dv_ao = Sem(nc, "dv_ao")
        pe_ao = Sem(nc, "pe_ao")
        dv_res = Sem(nc, "dv_res")
        f_x2 = Sem(nc, "f_x2")
        f_x2e = Sem(nc, "f_x2e")
        f_xl = [Sem(nc, f"f_xl{i}") for i in range(2)]
        pe_f = Sem(nc, "pe_f")
        ac_f = Sem(nc, "ac_f")
        dv_f = Sem(nc, "dv_f")
        po_f = Sem(nc, "po_f")
        pe_ft = Sem(nc, "pe_ft")
        st_out = Sem(nc, "st_out")

        r_ptx, r_pty = Rel(), Rel()
        r_xnbf2 = [Rel(), Rel()]
        r_psa, r_psb, r_psc, r_psd = Rel(), Rel(), Rel(), Rel()
        r_ps0, r_ps1 = Rel(), Rel()
        r_xnbf = Rel()
        r_S = [Rel(), Rel()]
        r_PA, r_PB = Rel(), Rel()
        r_PT = [Rel(), Rel()]

        r_win = Rel()
        r_kv = Rel()
        r_lbuf = Rel()

        P.step("po", lambda g: g.memset(id_bf[:], 0.0), inc=po_chain)
        P.step("po", lambda g: g.affine_select(
            out=id_bf[:], in_=id_bf[:], compare_op=ALU.not_equal, fill=1.0,
            base=0, pattern=[[-1, 128]], channel_multiplier=1),
            wait=[po_chain], inc=po_set)
        P.step("po", lambda g: g.memset(negmask[:], 0.0), inc=po_chain)
        P.step("po", lambda g: g.affine_select(
            out=negmask[:], in_=negmask[:], compare_op=ALU.is_ge, fill=NEG_BIG,
            base=0, pattern=[[-1, 128]], channel_multiplier=1),
            wait=[po_chain], inc=po_set)
        for buf, val in ((c0125, 0.125), (c_lneps, LN_EPS), (c_rmseps, RMS_EPS),
                         (c_nhalf, -0.5), (c_inv64, 1.0 / 64.0)):
            P.step("po", lambda g, buf=buf, val=val: g.memset(buf[:], val),
                   inc=po_set)
        P.step("po", lambda g: g.memset(woT_pair[:], 0.0), inc=po_set)
        POSET = po_set.n

        with ExitStack() as su:
            def tb(name, shape, dt=F32):
                return su.enter_context(nc.sbuf_tensor(name, shape, dt))

            id_f = tb("id_f", [128, 128])
            P.step("po", lambda g: g.memset(id_f[:], 0.0), inc=po_chain)
            P.step("po", lambda g: g.affine_select(
                out=id_f[:], in_=id_f[:], compare_op=ALU.not_equal, fill=1.0,
                base=0, pattern=[[-1, 128]], channel_multiplier=1),
                wait=[po_chain], inc=po_set)
            wdq_s = tb("wdq_s", [128, 4 * HD])
            wuq_s = tb("wuq_s", [HD, DL])
            wdkv_s = tb("wdkv_s", [128, 8 * HD])
            wukv_s = tb("wukv_s", [2 * HD, 2 * DL])
            wqr_s = tb("wqr_s", [HD, HD])
            wkr_s = tb("wkr_s", [HD, HD])
            wo_s = tb("wo_s", [HD, HD])
            wuqT_s = tb("wuqT_s", [128, 4 * HD])
            wukvT_s = tb("wukvT_s", [128, 8 * 2 * HD])
            b1_raw = tb("b1_raw", [NHC, 128])
            b2_raw = tb("b2_raw", [NOC, 128])

            def wload(dst, src):
                P.step("sy", lambda s, d=dst, sr=src: s.dma_start(out=d, in_=sr),
                       inc=d_w, inck=16)

            wload(wdq_s[:].rearrange("p (c f) -> p c f", f=HD),
                  wdq_d.rearrange("(c p) f -> p c f", p=128))
            wload(wuq_s[:], wuq_d)
            wload(wdkv_s[:].rearrange("p (c f) -> p c f", f=HD),
                  wdkv_d.rearrange("(c p) f -> p c f", p=128))
            wload(wukv_s[:], wukv_d)
            wload(wqr_s[:], wqr_d)
            wload(wkr_s[:], wkr_d)
            wload(wo_s[:], wo_d)
            wload(qnw_col[:], qw_d.rearrange("(f o) -> f o", o=1))
            wload(knw_col[:], kw_d.rearrange("(f o) -> f o", o=1))
            wload(b1_raw[:], b1_d.rearrange("(c p) -> c p", p=128))
            wload(b2_raw[:], b2_d.rearrange("(c p) -> c p", p=128))
            WALL = d_w.n

            for c in range(4):
                P.step("pe", lambda t, c=c: t.transpose(
                    ps_a[:, c * HD:(c + 1) * HD],
                    wuq_s[:, c * 128:(c + 1) * 128], id_f[0:HD, 0:HD]),
                    wait=[(d_w, WALL), po_set],
                    inc=pe_su if c == 3 else None)
            P.step("dv", lambda v: v.tensor_copy(wuqT_s[:], ps_a[:, 0:4 * HD]),
                   wait=[pe_su], inc=dv_su)
            r_psa.set(dv_su)
            for rnd in range(2):
                for c in range(4):
                    cc = rnd * 4 + c
                    P.step("pe", lambda t, cc=cc, c=c: t.transpose(
                        ps_b[:, c * 128:(c + 1) * 128],
                        wukv_s[:, cc * 128:(cc + 1) * 128], id_f[:]),
                        wait=r_psb.waits() if c == 0 else [],
                        inc=pe_su if c == 3 else None)
                P.step("dv", lambda v, rnd=rnd: v.tensor_copy(
                    wukvT_s[:, rnd * 512:(rnd + 1) * 512], ps_b[:, :]),
                    wait=[pe_su], inc=dv_su)
                r_psb.set(dv_su)
            P.step("pe", lambda t: t.transpose(ps_a[0:HD, 0:HD], wqr_s[:],
                                               id_f[0:HD, 0:HD]), wait=r_psa.waits())
            P.step("pe", lambda t: t.transpose(ps_a[0:HD, HD:2 * HD], wkr_s[:],
                                               id_f[0:HD, 0:HD]))
            P.step("pe", lambda t: t.transpose(ps_a[0:HD, 2 * HD:3 * HD],
                                               wo_s[:], id_f[0:HD, 0:HD]), inc=pe_su)
            P.step("dv", lambda v: v.tensor_copy(wqrT[0:HD, :], ps_a[0:HD, 0:HD]),
                   wait=[pe_su])
            P.step("dv", lambda v: v.tensor_copy(wqrT[HD:128, :], ps_a[0:HD, 0:HD]))
            P.step("dv", lambda v: v.tensor_copy(wkrT[0:HD, :], ps_a[0:HD, HD:2 * HD]))
            P.step("dv", lambda v: v.tensor_copy(wkrT[HD:128, :], ps_a[0:HD, HD:2 * HD]))
            P.step("dv", lambda v: v.tensor_copy(
                woT_pair[0:HD, 0:HD], ps_a[0:HD, 2 * HD:3 * HD]))
            P.step("dv", lambda v: v.tensor_copy(
                woT_pair[HD:128, HD:128], ps_a[0:HD, 2 * HD:3 * HD]), inc=dv_su)
            r_psa.set(dv_su)
            P.step("pe", lambda t: t.transpose(ps_b[:, 0:NHC], b1_raw[:],
                                               id_f[0:NHC, 0:NHC]), wait=r_psb.waits())
            P.step("pe", lambda t: t.transpose(ps_b[:, 256:256 + NOC], b2_raw[:],
                                               id_f[0:NOC, 0:NOC]), inc=pe_su)
            P.step("dv", lambda v: v.tensor_copy(b1col[:], ps_b[:, 0:NHC]),
                   wait=[pe_su])
            P.step("dv", lambda v: v.tensor_copy(b2col[:], ps_b[:, 256:256 + NOC]),
                   inc=dv_su)
            r_psb.set(dv_su)

            for c in range(4):
                P.step("pe", lambda t, c=c: t.matmul(
                    ps_c[0:HD, 0:HD],
                    wdq_s[:].rearrange("p (c f) -> p c f", f=HD)[:, c, :],
                    wuqT_s[:, c * HD:(c + 1) * HD],
                    start=(c == 0), stop=(c == 3)),
                    wait=[dv_su] if c == 0 else [],
                    inc=pe_su if c == 3 else None)
            for c in range(8):
                P.step("pe", lambda t, c=c: t.matmul(
                    ps_d[0:HD, 0:2 * HD],
                    wdkv_s[:].rearrange("p (c f) -> p c f", f=HD)[:, c, :],
                    wukvT_s[:, c * 2 * HD:(c + 1) * 2 * HD],
                    start=(c == 0), stop=(c == 7)),
                    inc=pe_su if c == 7 else None)
            P.step("dv", lambda v: v.tensor_copy(wqT2[0:HD, :], ps_c[0:HD, 0:HD]),
                   wait=[pe_su])
            P.step("dv", lambda v: v.tensor_copy(wqT2[HD:128, :],
                                                 ps_c[0:HD, 0:HD]))
            P.step("dv", lambda v: v.tensor_copy(wkT2[0:HD, :],
                                                 ps_d[0:HD, 0:HD]))
            P.step("dv", lambda v: v.tensor_copy(wkT2[HD:128, :],
                                                 ps_d[0:HD, 0:HD]))
            P.step("dv", lambda v: v.tensor_copy(wvT2[0:HD, :],
                                                 ps_d[0:HD, HD:2 * HD]))
            P.step("dv", lambda v: v.tensor_copy(wvT2[HD:128, :],
                                                 ps_d[0:HD, HD:2 * HD]),
                   inc=dv_su)
            r_psc.set(dv_su)
            r_psd.set(dv_su)
            SETUP = dv_su.n

        with ExitStack() as att:
            def asb(name, shape, dt=BF16):
                return att.enter_context(nc.sbuf_tensor(name, shape, dt))

            PCAP = 128 * sum(4 * (NB - 1) + qt + 1 for qt in range(NQT))
            Xn1T = asb("Xn1T", [64, H * 512])
            kstack = [asb(f"kst{h}", [128, T]) for h in range(H)]
            qstack = [asb(f"qst{h}", [128, 512]) for h in range(H)]
            vstore = [asb(f"vst{g}", [128, TC * GW]) for g in range(NGRP)]
            qn_tok = asb("qn_tok", [128, GW])
            msq = asb("msq", [128, HHALF], F32)
            ln8 = asb("ln8", [128, HHALF], F32)
            rstd8 = asb("rstd8", [128, HHALF], F32)
            rstdq_win = asb("rstdq_win", [128, 4 * H], F32)
            S2 = [asb(f"S_sb{i}", [128, T], F32) for i in range(2)]
            mcol = asb("mcol", [128, 6], F32)
            mrow = asb("mrow", [128, 1], F32)
            negm8 = asb("negm8", [128, 8], F32)
            P_A = asb("P_A", [128, PCAP])
            P_B = asb("P_B", [128, PCAP])
            PT2 = [asb(f"PT2{par}", [128, 1024]) for par in range(2)]
            l_buf = asb("l_buf", [128, 8], F32)
            rl_buf = asb("rl_buf", [128, 8], F32)
            ctx_bf = asb("ctx_bf", [128, 512])
            attn_bf = ctx_bf
            sqp_bf = asb("sqp_bf", [128, GW])
            r_ctxbf = Rel()
            r_sqp = Rel()
            r_attnbf = r_ctxbf
            r_qntok = Rel()
            r_rstd8, r_msq = Rel(), Rel()

            xnT_marks = {}

            def emit_W1(b_, qb_, thunks=None):
                win0_ = qb_ * 512

                def mk_chain(tc):
                    def chain():
                        t0 = win0_ + tc * 128
                        par = tc % 2
                        P.step("po", lambda g, b=b_, t0=t0, par=par: g.dma_start(
                            out=xln[par][:], in_=x_in[b, t0:t0 + 128, :]),
                            wait=[f_xl[par]], inc=d_xl[par], inck=16)
                        ld = d_xl[par].n
                        P.step("dv", lambda v, par=par: v.reduce_sum(
                            musum[:], xln[par][:], axis=AX.X),
                            wait=[(d_xl[par], ld)], inc=dv_ln)
                        P.step("ac", lambda a, par=par: a.activation(
                            out=xn_bf2[tc % 2][:], in_=xln[par][:], func=AF.Square,
                            accum_out=sqsum[:]),
                            wait=[(d_xl[par], ld), dv_ln]
                            + r_xnbf2[tc % 2].waits(),
                            inc=ac_ln)
                        P.step("dv", lambda v: v.tensor_scalar(
                            out=negmu[:], in0=musum[:], scalar1=-1.0 / D,
                            scalar2=None, op0=ALU.mult), wait=[dv_ln],
                            inc=dv_ln)
                        P.step("dv", lambda v: v.tensor_mul(
                            mu2[:], negmu[:], negmu[:]), wait=[dv_ln],
                            inc=dv_ln)
                        P.step("dv", lambda v: v.tensor_scalar(
                            out=varb[:], in0=sqsum[:], scalar1=1.0 / D,
                            scalar2=None, op0=ALU.mult),
                            wait=[ac_ln, dv_ln], inc=dv_ln)
                        P.step("dv", lambda v: v.tensor_sub(
                            varb[:], varb[:], mu2[:]), wait=[dv_ln],
                            inc=dv_ln)
                        P.step("ac", lambda a: a.activation(
                            out=lnv[:], in_=varb[:], func=AF.Ln,
                            bias=c_lneps[:]),
                            wait=[dv_ln, ac_ln, (po_set, POSET)], inc=ac_ln)
                        P.step("ac", lambda a: a.activation(
                            out=rstd[:], in_=lnv[:], func=AF.Exp,
                            scale=c_nhalf[:]),
                            wait=[ac_ln, f_xl[0], f_xl[1]], inc=ac_ln)
                        P.step("dv", lambda v, par=par, tc=tc: v.tensor_scalar(
                            out=xn_bf2[tc % 2][:], in0=xln[par][:],
                            scalar1=negmu[:],
                            scalar2=rstd[:], op0=ALU.add, op1=ALU.mult),
                            wait=[ac_ln] + r_xnbf2[tc % 2].waits(),
                            inc=f_xl[par])
                    return chain

                def mk_tr(tc):
                    def tr():
                        par = tc % 2
                        xn_done = f_xl[par].n
                        for fc in range(NFC):
                            P.step("pe", lambda t, fc=fc: t.transpose(
                                pt_x[:, fc * 128:(fc + 1) * 128],
                                xn_bf2[tc % 2][:, fc * 128:(fc + 1) * 128],
                                id_bf[:]),
                                wait=([(f_xl[par], xn_done), (po_set, POSET)]
                                      + r_ptx.waits() + [(pe_pr, pe_pr.n)])
                                if fc == 0 else [],
                                inc=pe_ln if fc == NFC - 1 else None)
                        r_xnbf2[tc % 2].set(pe_ln)
                        P.step("dv", lambda v, tc=tc: v.tensor_copy(
                            Xn1T[:].rearrange("p (c f) -> p c f", f=1024)
                            [:, :, tc * 128:(tc + 1) * 128],
                            pt_x[0:64, :].rearrange("p (c f) -> p c f", f=128)
                            [:, 0:H // 2, :]),
                            wait=[pe_ln], inc=dv_ln)
                        P.step("dv", lambda v, tc=tc: v.tensor_copy(
                            Xn1T[:].rearrange("p (c f) -> p c f", f=1024)
                            [:, :, 512 + tc * 128:512 + (tc + 1) * 128],
                            pt_x[64:128, :].rearrange("p (c f) -> p c f",
                                                      f=128)
                            [:, 0:H // 2, :]),
                            inc=dv_ln)
                        r_ptx.set(dv_ln)
                        if tc == 3:
                            xnT_marks[(b_, qb_)] = dv_ln.n
                    return tr

                seq = []
                for tc in range(4):
                    seq.append(mk_chain(tc))
                    seq.append(mk_tr(tc))
                if thunks is None:
                    for f in seq:
                        f()
                else:
                    thunks.extend(seq)

            wo_q = []
            wo_mark = [0]
            for b in range(B):
                for qb in range(NB):
                    win0 = qb * 512

                    for tc in range(4):
                        t0 = win0 + tc * 128
                        P.step("sy", lambda s, b=b, t0=t0, tc=tc: s.dma_start(
                            out=x2_buf[:, tc * D:(tc + 1) * D],
                            in_=x_in[b, t0:t0 + 128, :]),
                            wait=[f_x2] if tc == 0 else [], inc=d_x2, inck=16)
                    x2_ready = d_x2.n

                    rot4 = [(ps_a, r_psa), (ps_b, r_psb),
                            (ps_c, r_psc), (ps_d, r_psd)]
                    uidx = [0]
                    pendu = deque()

                    def w2_stage2(tc, tcg, grp, kind, ps, r_ps, mm_done):
                        if kind == "v":
                            P.step("ac", lambda a, grp=grp, tcg=tcg, ps=ps: a.activation(
                                out=vstore[grp][:, tcg * GW:(tcg + 1) * GW],
                                in_=ps[:, 0:GW], func=AF.Copy),
                                wait=[(pe_pr, mm_done)] + r_kv.waits(),
                                inc=ac_pr)
                            r_ps.set(ac_pr)
                            return

                        if kind == "q":
                            P.step("dv", lambda v, ps=ps: v.tensor_copy(
                                qn_tok[:], ps[:, 0:GW]),
                                wait=[(pe_pr, mm_done)] + r_qntok.waits(),
                                inc=dv_pr)
                            ev_mark = dv_pr.n
                            for hh in range(HHALF):
                                P.step("pe", lambda t, hh=hh: t.transpose(
                                    pt_y[0:HD, hh * 128:(hh + 1) * 128],
                                    qn_tok[:, hh * HD:(hh + 1) * HD],
                                    id_bf[:]),
                                    wait=([(dv_pr, ev_mark)] + r_pty.waits())
                                    if hh == 0 else [],
                                    inc=pe_pr if hh == HHALF - 1 else None)
                            r_qntok.set(pe_pr)
                            tr_done = pe_pr.n
                            for hh in range(HHALF):
                                h = grp * HHALF + hh
                                dst = qstack[h][0:HD, tc * 128:(tc + 1) * 128]
                                P.step("dv", lambda v, hh=hh, dst=dst: v.tensor_scalar(
                                    out=dst,
                                    in0=pt_y[0:HD, hh * 128:(hh + 1) * 128],
                                    scalar1=qnw_col[:], scalar2=None,
                                    op0=ALU.mult),
                                    wait=([(pe_pr, tr_done)] + r_win.waits())
                                    if hh == 0 else [],
                                    inc=dv_pr if hh == HHALF - 1 else None)
                            r_pty.set(dv_pr)
                            P.step("ac", lambda a, ps=ps: a.activation(
                                out=sqp_bf[:], in_=ps[:, 0:GW],
                                func=AF.Square),
                                wait=[(pe_pr, mm_done), (dv_pr, ev_mark)]
                                + r_sqp.waits(),
                                inc=ac_pr)
                            sq_mark = ac_pr.n
                            P.step("dv", lambda v: v.tensor_reduce(
                                out=msq[:],
                                in_=sqp_bf[:].rearrange("p (h f) -> p h f",
                                                        f=HD),
                                axis=AX.X, op=ALU.add),
                                wait=[(ac_pr, sq_mark)] + r_msq.waits(),
                                inc=dv_pr)
                            r_sqp.set(dv_pr)
                            red_mark = dv_pr.n
                            r_ps.set(dv_pr)
                            P.step("ac", lambda a: a.activation(
                                out=ln8[:], in_=msq[:], func=AF.Ln,
                                bias=c_rmseps[:], scale=c_inv64[:]),
                                wait=[(dv_pr, red_mark), ac_pr], inc=ac_pr)
                            r_msq.set(ac_pr)
                            P.step("ac", lambda a: a.activation(
                                out=rstd8[:], in_=ln8[:], func=AF.Exp,
                                scale=c_nhalf[:]),
                                wait=[ac_pr] + r_rstd8.waits(), inc=ac_pr)
                            rstd_done = ac_pr.n
                            c0 = tc * H + grp * HHALF
                            P.step("dv", lambda v, c0=c0: v.tensor_scalar(
                                out=rstdq_win[:, c0:c0 + HHALF],
                                in0=rstd8[:], scalar1=0.125, scalar2=None,
                                op0=ALU.mult),
                                wait=[(ac_pr, rstd_done)], inc=dv_pr)
                            r_rstd8.set(dv_pr)
                            return

                        P.step("ac", lambda a, ps=ps: a.activation(
                            out=sqp_bf[:], in_=ps[:, 0:GW],
                            func=AF.Square),
                            wait=[(pe_pr, mm_done)] + r_sqp.waits(),
                            inc=ac_pr)
                        sq_mark = ac_pr.n
                        P.step("dv", lambda v: v.tensor_reduce(
                            out=msq[:],
                            in_=sqp_bf[:].rearrange(
                                "p (h f) -> p h f", f=HD),
                            axis=AX.X, op=ALU.add),
                            wait=[(ac_pr, sq_mark)] + r_msq.waits(),
                            inc=dv_pr)
                        r_sqp.set(dv_pr)
                        red_mark = dv_pr.n
                        P.step("ac", lambda a: a.activation(
                            out=ln8[:], in_=msq[:], func=AF.Ln,
                            bias=c_rmseps[:], scale=c_inv64[:]),
                            wait=[(dv_pr, red_mark), ac_pr], inc=ac_pr)
                        r_msq.set(ac_pr)
                        P.step("ac", lambda a: a.activation(
                            out=rstd8[:], in_=ln8[:], func=AF.Exp,
                            scale=c_nhalf[:]),
                            wait=[ac_pr] + r_rstd8.waits(), inc=ac_pr)
                        rstd_done = ac_pr.n
                        for hh in range(HHALF):
                            P.step("dv", lambda v, hh=hh, ps=ps: v.tensor_scalar(
                                out=qn_tok[:, hh * HD:(hh + 1) * HD],
                                in0=ps[:, hh * HD:(hh + 1) * HD],
                                scalar1=rstd8[:, hh:hh + 1],
                                scalar2=None, op0=ALU.mult),
                                wait=([(ac_pr, rstd_done)]
                                      + r_qntok.waits())
                                if hh == 0 else [],
                                inc=dv_pr if hh == HHALF - 1 else None)
                        r_rstd8.set(dv_pr)
                        r_ps.set(dv_pr)
                        r_ps.add(ac_pr)
                        appl = dv_pr.n
                        for hh in range(HHALF):
                            P.step("pe", lambda t, hh=hh: t.transpose(
                                pt_y[0:HD, hh * 128:(hh + 1) * 128],
                                qn_tok[:, hh * HD:(hh + 1) * HD],
                                id_bf[:]),
                                wait=([(dv_pr, appl)] + r_pty.waits())
                                if hh == 0 else [],
                                inc=pe_pr if hh == HHALF - 1 else None)
                        r_qntok.set(pe_pr)
                        tr_done = pe_pr.n
                        for hh in range(HHALF):
                            h = grp * HHALF + hh
                            dst = kstack[h][0:HD, tcg * 128:(tcg + 1) * 128]
                            P.step("dv", lambda v, hh=hh, dst=dst: v.tensor_scalar(
                                out=dst,
                                in0=pt_y[0:HD, hh * 128:(hh + 1) * 128],
                                scalar1=knw_col[:], scalar2=None,
                                op0=ALU.mult),
                                wait=([(pe_pr, tr_done)] + r_kv.waits())
                                if hh == 0 else [],
                                inc=dv_pr if hh == HHALF - 1 else None)
                        r_pty.set(dv_pr)
                    if (b, qb) not in xnT_marks:
                        emit_W1(b, qb)
                    xnT_done = xnT_marks[(b, qb)]
                    for tc in range(4):
                        t0 = win0 + tc * 128
                        tcg = t0 // 128
                        for grp in range(0 if skip_proj else NGRP):
                            for kind in ("q", "k", "v"):
                                ps, r_ps = rot4[uidx[0] % 4]
                                uidx[0] += 1
                                wT = {"q": wqT2, "k": wkT2, "v": wvT2}[kind]
                                for hh in range(HHALF):
                                    h = grp * HHALF + hh
                                    P.step("pe", lambda t, h=h, hh=hh, wT=wT, tc=tc, ps=ps: t.matmul(
                                        ps[:, hh * HD:(hh + 1) * HD],
                                        Xn1T[0:HD, h * 512 + tc * 128:
                                             h * 512 + (tc + 1) * 128],
                                        wT[0:HD, :], start=True,
                                        stop=True),
                                        wait=([(dv_ln, xnT_done),
                                               (dv_su, SETUP)] + r_ps.waits())
                                        if hh == 0 else [],
                                        inc=pe_pr if hh == HHALF - 1 else None)
                                pendu.append((tc, tcg, grp, kind, ps, r_ps,
                                              pe_pr.n))
                                if len(pendu) >= 3:
                                    w2_stage2(*pendu.popleft())

                    while pendu:
                        w2_stage2(*pendu.popleft())
                    stacks_done = dv_pr.n
                    pingpong = [(ps_a, r_psa), (ps_b, r_psb)]
                    pidx = 0
                    for kind in (() if skip_proj else ("qr", "kr")):
                        wT = wqrT if kind == "qr" else wkrT
                        for h in range(H):
                            ps, r_ps = pingpong[pidx]
                            pidx ^= 1
                            if kind == "qr":
                                rhs = qstack[h][0:HD, 0:512]
                            else:
                                rhs = Xn1T[0:HD, h * 512:(h + 1) * 512]
                            P.step("pe", lambda t, wT=wT, rhs=rhs, ps=ps: t.matmul(
                                ps[0:HD, :], wT[0:HD, :], rhs,
                                start=True, stop=True),
                                wait=[(dv_pr, stacks_done)] + r_ps.waits(),
                                inc=pe_pr)
                            mmn = pe_pr.n
                            if kind == "qr":
                                dst = qstack[h][HD:128, 0:512]
                            else:
                                dst = kstack[h][HD:128, win0:win0 + 512]
                            P.step("ac", lambda a, dst=dst, ps=ps: a.activation(
                                out=dst, in_=ps[0:HD, :], func=AF.Copy),
                                wait=[(pe_pr, mmn)], inc=ac_pr)
                            r_ps.set(ac_pr)
                    win_ready = dv_pr.n
                    win_ready_ac = ac_pr.n

                    if skip_attn:
                        P.step("dv", lambda v: v.tensor_copy(
                            mrow[:], musum[:]), wait=[(d_x2, x2_ready)],
                            inc=dv_res)
                        sp = dv_res.n
                        for tc in range(4):
                            row0 = b * T + win0 + tc * 128
                            P.step("po", lambda g, row0=row0, tc=tc: g.dma_start(
                                out=x2s[row0:row0 + 128, :],
                                in_=x2_buf[:, tc * D:(tc + 1) * D]),
                                wait=[(dv_res, sp)] if tc == 0 else [],
                                inc=f_x2, inck=16)
                        r_win.set(pe_s, pe_pr)
                        continue
                    poffs = [0]
                    for qt in range(NQT):
                        poffs.append(poffs[-1] + 128 * (4 * qb + qt + 1))

                    stats_mark = {}
                    negm_free = {}
                    sring = [0]

                    def emit_slot_stats(p, hi, qt):
                        h = 2 * p + hi
                        slot = hi * 4 + qt
                        spar = slot % 2
                        S_sb = S2[spar]
                        Tq = 4 * qb + qt
                        klen = 128 * (Tq + 1)
                        offs = list(range(0, klen, 512))
                        nchunks = 0
                        sping = [(ps_s0, r_ps0), (ps_s1, r_ps1),
                                 (ps_a, r_psa), (ps_b, r_psb)]
                        first_stat = True
                        for ci, off in enumerate(offs):
                            w = min(512, klen - off)
                            ps, r_ps = sping[(sring[0] + ci) % 4]
                            has_diag = (off + w == klen)
                            P.step("pe", lambda t, h=h, qt=qt, off=off, w=w, ps=ps: t.matmul(
                                ps[:, 0:w],
                                qstack[h][:, qt * 128:(qt + 1) * 128],
                                kstack[h][:, off:off + w],
                                start=True, stop=True),
                                wait=[(dv_pr, win_ready),
                                      (ac_pr, win_ready_ac)] + r_ps.waits(),
                                inc=pe_s)
                            mmn = pe_s.n
                            wmain = w - 128 if has_diag else w
                            sg = r_S[spar].waits()
                            first_stat = False
                            if wmain > 0:
                                P.step("dv", lambda v, off=off, wmain=wmain, ps=ps, nchunks=nchunks, S_sb=S_sb: v.tensor_scalar(
                                    out=S_sb[:, off:off + wmain],
                                    in0=ps[:, 0:wmain], scalar1=1.0,
                                    scalar2=None, op0=ALU.mult,
                                    op1=ALU.max,
                                    accum_out=mcol[:, nchunks:nchunks + 1]),
                                    wait=[(pe_s, mmn)] + sg, inc=dv_st)
                                r_ps.set(dv_st)
                                nchunks += 1
                            if has_diag:
                                P.step("dv", lambda v, off=off, w=w, wmain=wmain, ps=ps, S_sb=S_sb: v.scalar_tensor_tensor(
                                    out=S_sb[:, off + wmain:off + w],
                                    in0=ps[:, wmain:w], scalar=1.0,
                                    in1=negmask[:], op0=ALU.mult,
                                    op1=ALU.add),
                                    wait=[(pe_s, mmn), (po_set, POSET)] + sg,
                                    inc=dv_st)
                                P.step("dv", lambda v, off=off, w=w, wmain=wmain, nchunks=nchunks, S_sb=S_sb: v.reduce_max(
                                    out=mcol[:, nchunks:nchunks + 1],
                                    in_=S_sb[:, off + wmain:off + w],
                                    axis=AX.X),
                                    wait=[dv_st], inc=dv_st)
                                nchunks += 1
                                r_ps.set(dv_st)
                                r_ps.add(ac_st)
                        sring[0] = (sring[0] + len(offs)) % 4
                        P.step("dv", lambda v, nchunks=nchunks: v.reduce_max(
                            out=mrow[:], in_=mcol[:, 0:nchunks], axis=AX.X),
                            wait=[dv_st], inc=dv_st)
                        rc = qt * H + h
                        P.step("dv", lambda v, slot=slot, rc=rc: v.tensor_scalar(
                            out=negm8[:, slot:slot + 1], in0=mrow[:],
                            scalar1=rstdq_win[:, rc:rc + 1], scalar2=-1.0,
                            op0=ALU.mult, op1=ALU.mult),
                            wait=[dv_st, (dv_pr, win_ready)]
                            + negm_free.get(slot, []),
                            inc=dv_st)
                        stats_mark[(p, slot)] = dv_st.n

                    def emit_slot_exp(p, hi, qt):
                        slot = hi * 4 + qt
                        spar = slot % 2
                        P_sb = (P_A, P_B)[hi]
                        r_P = (r_PA, r_PB)[hi]
                        klen = 128 * (4 * qb + qt + 1)
                        po0 = poffs[qt]
                        rc = qt * H + 2 * p + hi
                        P.step("ac", lambda a, P_sb=P_sb, po0=po0, klen=klen, slot=slot, spar=spar, rc=rc: a.activation(
                            out=P_sb[:, po0:po0 + klen],
                            in_=S2[spar][:, 0:klen], func=AF.Exp,
                            scale=rstdq_win[:, rc:rc + 1],
                            bias=negm8[:, slot:slot + 1],
                            accum_out=l_buf[:, slot:slot + 1]),
                            wait=[(dv_st, stats_mark[(p, slot)])]
                            + r_P.waits()
                            + (r_lbuf.waits() if slot == 0 else []),
                            inc=ac_ex)
                        negm_free[slot] = [(ac_ex, ac_ex.n)]
                        r_S[spar].set(ac_ex)

                    def emit_wo(p, rl_done):
                        P.step("pe", lambda t: t.matmul(
                            ps_d[:, :], woT_pair[:], ctx_bf[:], start=True,
                            stop=True),
                            wait=[dv_ao, (dv_su, SETUP)] + r_psd.waits(),
                            inc=pe_ao)
                        r_ctxbf.set(pe_ao)
                        P.step("dv", lambda v: v.tensor_copy(attn_bf[:],
                                                             ps_d[:, :]),
                               wait=[pe_ao] + r_attnbf.waits(), inc=dv_ao)
                        r_psd.set(dv_ao)
                        ab_done = dv_ao.n
                        for qt in range(NQT):
                            P.step("pe", lambda t, qt=qt: t.transpose(
                                pt_x[:, qt * 128:(qt + 1) * 128],
                                attn_bf[:, qt * 128:(qt + 1) * 128],
                                id_bf[:]),
                                wait=([(dv_ao, ab_done)] + r_ptx.waits())
                                if qt == 0 else [],
                                inc=pe_ao if qt == NQT - 1 else None)
                        r_attnbf.set(pe_ao)
                        trn = pe_ao.n
                        for qt in range(NQT):
                            for hi in range(2):
                                slot = hi * 4 + qt
                                col0 = qt * D + p * 128 + hi * HD
                                P.step("dv", lambda v, qt=qt, hi=hi, slot=slot, col0=col0: v.scalar_tensor_tensor(
                                    out=x2_buf[:, col0:col0 + HD],
                                    in0=pt_x[:, qt * 128 + hi * HD:
                                             qt * 128 + (hi + 1) * HD],
                                    scalar=rl_buf[:, slot:slot + 1],
                                    in1=x2_buf[:, col0:col0 + HD],
                                    op0=ALU.mult, op1=ALU.add),
                                    wait=([(pe_ao, trn), (dv_res, rl_done),
                                           (d_x2, x2_ready)]
                                          if (qt == 0 and hi == 0) else []),
                                    inc=(dv_res
                                         if (qt == NQT - 1 and hi == 1)
                                         else None))
                        r_ptx.set(dv_res)
                        return dv_res.n

                    def emit_pair_ctx(p):
                        if wo_q:
                            wo_mark[0] = wo_q.pop(0)()
                        P.step("dv", lambda v: v.reciprocal(rl_buf[:],
                                                            l_buf[:]),
                               wait=[ac_ex, (dv_res, wo_mark[0])],
                               inc=dv_res)
                        r_lbuf.set(dv_res)
                        rl_done = dv_res.n
                        nj = 4 * qb + 4
                        ptb = (pt_x, pt_y)
                        rptb = (r_ptx, r_pty)
                        for j in range(nj):
                            par = j % 2
                            qt_min = max(0, j - 4 * qb)
                            for hi in range(2):
                                P_sb = (P_A, P_B)[hi]
                                for qt in range(qt_min, NQT):
                                    src0 = poffs[qt] + j * 128
                                    P.step("pe", lambda t, P_sb=P_sb, hi=hi, qt=qt, src0=src0, par=par: t.transpose(
                                        ptb[par][:, hi * 512 + qt * 128:
                                                 hi * 512 + (qt + 1) * 128],
                                        P_sb[:, src0:src0 + 128], id_bf[:]),
                                        wait=([ac_ex] + rptb[par].waits())
                                        if (hi == 0 and qt == qt_min) else [],
                                        inc=pe_pt
                                        if (hi == 1 and qt == NQT - 1)
                                        else None)
                            trn = pe_pt.n
                            P.step("dv", lambda v, par=par, qt_min=qt_min: v.tensor_copy(
                                PT2[par][:].rearrange("p (h f) -> p h f",
                                                      f=512)
                                [:, :, qt_min * 128:512],
                                ptb[par][:].rearrange("p (h f) -> p h f",
                                                      f=512)
                                [:, :, qt_min * 128:512]),
                                wait=[(pe_pt, trn)] + r_PT[par].waits(),
                                inc=dv_pt)
                            ev_w = (dv_pt, dv_pt.n)
                            rptb[par].set(dv_pt)
                            for hi in range(2):
                                h = 2 * p + hi
                                grp = h // HHALF
                                hh = h % HHALF
                                v0 = j * GW + hh * HD
                                P.step("pe", lambda t, hi=hi, grp=grp, v0=v0, par=par, qt_min=qt_min, j=j, nj=nj: t.matmul(
                                    ps_c[hi * HD:(hi + 1) * HD,
                                         qt_min * 128:512],
                                    vstore[grp][:, v0:v0 + HD],
                                    PT2[par][:, hi * 512 + qt_min * 128:
                                             hi * 512 + 512],
                                    start=(j == 0), stop=(j == nj - 1),
                                    tile_position=(0, hi * HD),
                                    skip_group_check=True),
                                    wait=([ev_w,
                                           (dv_pr, win_ready),
                                           (ac_pr, win_ready_ac)]
                                          + (r_psc.waits() if j == 0 else []))
                                    if hi == 0 else [],
                                    inc=pe_cx if hi == 1 else None)
                            r_PT[par].set(pe_cx)
                        r_PA.set(pe_pt)
                        r_PB.set(pe_pt)

                        P.step("dv", lambda v: v.tensor_copy(ctx_bf[:],
                                                             ps_c[:, :]),
                               wait=[pe_cx] + r_ctxbf.waits(), inc=dv_ao)
                        r_psc.set(dv_ao)
                        wo_q.append(lambda p=p, rl_done=rl_done:
                                    emit_wo(p, rl_done))

                    pend = deque()
                    w1q = []
                    slot_ctr = [0]
                    for p in range(H // 2):
                        if p == H // 2 - 3:
                            nb, nqb = (b, qb + 1) if qb + 1 < NB else \
                                      (b + 1, 0)
                            if nb < B:
                                emit_W1(nb, nqb, thunks=w1q)
                        for hi in range(2):
                            for qt in range(NQT):
                                slot_ctr[0] += 1
                                if w1q and slot_ctr[0] % 3 == 0:
                                    w1q.pop(0)()
                                if len(pend) >= 2:
                                    g = pend.popleft()
                                    emit_slot_exp(*g)
                                    if g[1] == 1 and g[2] == NQT - 1:
                                        emit_pair_ctx(g[0])
                                emit_slot_stats(p, hi, qt)
                                pend.append((p, hi, qt))
                    while pend:
                        g = pend.popleft()
                        emit_slot_exp(*g)
                        if g[1] == 1 and g[2] == NQT - 1:
                            emit_pair_ctx(g[0])
                    while w1q:
                        w1q.pop(0)()
                    while wo_q:
                        wo_mark[0] = wo_q.pop(0)()
                    sp = dv_res.n
                    for tc in range(4):
                        row0 = b * T + win0 + tc * 128
                        P.step("po", lambda g, row0=row0, tc=tc: g.dma_start(
                            out=x2s[row0:row0 + 128, :],
                            in_=x2_buf[:, tc * D:(tc + 1) * D]),
                            wait=[(dv_res, sp)] if tc == 0 else [],
                            inc=f_x2, inck=16)
                    r_win.set(pe_s, pe_pr)
                r_kv.set(pe_s, pe_cx)

        with ExitStack() as ffn:
            def fsb(name, shape, dt=BF16):
                return ffn.enter_context(nc.sbuf_tensor(name, shape, dt))

            w1b = fsb("w1b", [128, NFC * FF])
            w2b = fsb("w2b", [128, NHC * D])
            xn2 = fsb("xn2", [128, D])
            nmr = ffn.enter_context(nc.sbuf_tensor("nmr", [128, 1], F32))
            Xn2T2 = [fsb(f"Xn2T{i}", [128, NFC * 512]) for i in range(2)]
            r_Xn2Tb = [Rel(), Rel()]
            hT = fsb("hT", [128, NHC * 512])
            y_bf = [fsb(f"ybf{i}", [128, 512]) for i in range(2)]
            r_xn2, r_Xn2T = Rel(), Rel()
            r_ybf = [Rel(), Rel()]

            SPILLS = f_x2.n
            LN1_DV = dv_ln.n

            prev_w2_end = 0
            ln2_marks = {}
            st_guard = [0]

            def emit_ln2(s_):
                b_, qb_ = divmod(s_, NB)
                win0_ = qb_ * 512
                lds = []

                def emit_load(tc):
                    row0 = b_ * T + win0_ + tc * 128
                    par = tc % 2
                    P.step("po", lambda g, row0=row0, par=par: g.dma_start(
                        out=xln[par][:], in_=x2s[row0:row0 + 128, :]),
                        wait=[f_xl[par], (f_x2, SPILLS)],
                        inc=d_xl[par], inck=16)
                    lds.append(d_xl[par].n)

                def emit_chain(tc):
                    par = tc % 2
                    ld = lds[tc]
                    P.step("ac", lambda a, par=par: a.activation(
                        out=xn2[:], in_=xln[par][:], func=AF.Copy,
                        accum_out=musum[:]),
                        wait=[(d_xl[par], ld), (po_f, st_guard[0]),
                              (dv_ln, LN1_DV)] + r_xn2.waits(),
                        inc=ac_f)
                    cp_mark = ac_f.n
                    P.step("ac", lambda a, par=par: a.activation(
                        out=xn2[:], in_=xln[par][:],
                        func=AF.Square, accum_out=sqsum[:]),
                        wait=[(ac_f, cp_mark)], inc=ac_f)
                    cs_mark = ac_f.n
                    P.step("po", lambda g: g.tensor_scalar(
                        out=negmu[:], in0=musum[:], scalar1=-1.0 / D,
                        scalar2=None, op0=ALU.mult),
                        wait=[(ac_f, cs_mark), (dv_ln, LN1_DV), po_f],
                        inc=po_f)
                    P.step("po", lambda g: g.tensor_tensor(
                        out=mu2[:], in0=negmu[:], in1=negmu[:],
                        op=ALU.mult), wait=[po_f], inc=po_f)
                    P.step("po", lambda g: g.tensor_scalar(
                        out=varb[:], in0=sqsum[:], scalar1=1.0 / D,
                        scalar2=None, op0=ALU.mult), wait=[po_f], inc=po_f)
                    P.step("po", lambda g: g.tensor_tensor(
                        out=varb[:], in0=varb[:], in1=mu2[:],
                        op=ALU.subtract), wait=[po_f], inc=po_f)
                    vb_mark = po_f.n
                    P.step("ac", lambda a: a.activation(
                        out=lnv[:], in_=varb[:], func=AF.Ln,
                        bias=c_lneps[:]),
                        wait=[(po_f, vb_mark), ac_f], inc=ac_f)
                    P.step("ac", lambda a: a.activation(
                        out=rstd[:], in_=lnv[:], func=AF.Exp,
                        scale=c_nhalf[:]), wait=[ac_f], inc=ac_f)
                    ex_mark = ac_f.n
                    P.step("po", lambda g: g.tensor_tensor(
                        out=nmr[:], in0=negmu[:], in1=rstd[:],
                        op=ALU.mult),
                        wait=[(ac_f, ex_mark), po_f], inc=po_f)
                    st_guard[0] = po_f.n
                    P.step("ac", lambda a, par=par: a.activation(
                        out=xn2[:], in_=xln[par][:], func=AF.Identity,
                        scale=rstd[:], bias=nmr[:]),
                        wait=[(po_f, st_guard[0]), ac_f] + r_xn2.waits(),
                        inc=f_xl[par])
                    xnd = f_xl[par].n
                    for fc in range(NFC):
                        P.step("pe", lambda t, fc=fc: t.transpose(
                            pt_x[:, fc * 128:(fc + 1) * 128],
                            xn2[:, fc * 128:(fc + 1) * 128], id_bf[:]),
                            wait=([(f_xl[par], xnd)] + r_ptx.waits())
                            if fc == 0 else [],
                            inc=pe_ft if fc == NFC - 1 else None)
                    r_xn2.set(pe_ft)
                    P.step("dv", lambda v, tc=tc, s_=s_: v.tensor_copy(
                        Xn2T2[s_ % 2][:].rearrange("p (c f) -> p c f", f=512)
                        [:, :, tc * 128:(tc + 1) * 128],
                        pt_x[:].rearrange("p (c f) -> p c f", f=128)),
                        wait=[pe_ft]
                        + (r_Xn2Tb[s_ % 2].waits() if tc == 0 else []),
                        inc=dv_f)
                    r_ptx.set(dv_f)

                for tc in range(4):
                    emit_load(tc)
                    if tc >= 1:
                        emit_chain(tc - 1)
                emit_chain(3)
                ln2_marks[s_] = dv_f.n

            emit_ln2(0)
            if B * NB > 1:
                emit_ln2(1)
            w1_marks = []
            HCG = FF // 4
            for hg in range(4):
                for fc in range(NFC):
                    P.step("po", lambda g, fc=fc, hg=hg: g.dma_start(
                        out=w1b[:, fc * FF + hg * HCG:
                                fc * FF + (hg + 1) * HCG],
                        in_=w1_d[fc * 128:(fc + 1) * 128,
                                 hg * HCG:(hg + 1) * HCG]),
                        wait=[(d_g, w1_marks[-1])]
                        if (fc == 0 and hg > 0) else [],
                        inc=d_g, inck=16)
                w1_marks.append(d_g.n)
            w2_marks = []
            for kg in range(4):
                for kc in range(kg * (NHC // 4), (kg + 1) * (NHC // 4)):
                    P.step("po", lambda g, kc=kc, kg=kg: g.dma_start(
                        out=w2b[:, kc * D:(kc + 1) * D],
                        in_=w2_d[kc * 128:(kc + 1) * 128, :]),
                        wait=[(d_g, (w2_marks[-1] if kg > 0
                                     else w1_marks[-1]))]
                        if kc == kg * (NHC // 4) else [],
                        inc=d_g, inck=16)
                w2_marks.append(d_g.n)

            for s in range(B * NB):
                b, qb = divmod(s, NB)
                win0 = qb * 512
                if s not in ln2_marks:
                    emit_ln2(s)
                if s + 1 < B * NB and s + 1 not in ln2_marks:
                    emit_ln2(s + 1)
                xn2T_ready = ln2_marks[s]
                for tc in range(4):
                    row0 = b * T + win0 + tc * 128
                    P.step("sy", lambda sy, row0=row0, tc=tc: sy.dma_start(
                        out=x2_buf[:, tc * D:(tc + 1) * D],
                        in_=x2s[row0:row0 + 128, :]),
                        wait=[f_x2, st_out] if tc == 0 else [], inc=d_x2,
                        inck=16)
                x2_here = d_x2.n

                pp1 = [(ps_c, r_psc), (ps_d, r_psd)]
                gelu_marks2 = {}
                for hc in range(0 if skip_ffn else NHC):
                    ps, r_ps = pp1[hc % 2]
                    for fc in range(NFC):
                        P.step("pe", lambda t, fc=fc, hc=hc, ps=ps, s=s: t.matmul(
                            ps[:, :],
                            w1b[:, fc * FF + hc * 128:
                                fc * FF + (hc + 1) * 128],
                            Xn2T2[s % 2][:, fc * 512:(fc + 1) * 512],
                            start=(fc == 0), stop=(fc == NFC - 1)),
                            wait=([(dv_f, xn2T_ready),
                                   (d_g, w1_marks[(hc * 128) // HCG])]
                                  + r_ps.waits()) if fc == 0 else [],
                            inc=pe_f if fc == NFC - 1 else None)
                    mmn = pe_f.n
                    P.step("ac", lambda a, hc=hc, ps=ps: a.activation(
                        out=hT[:, hc * 512:(hc + 1) * 512], in_=ps[:, :],
                        func=AF.Gelu, bias=b1col[:, hc:hc + 1]),
                        wait=[(pe_f, mmn), (pe_f, prev_w2_end),
                              (dv_su, SETUP)], inc=ac_f)
                    gelu_marks2[hc] = ac_f.n
                    r_ps.set(ac_f)
                r_Xn2Tb[s % 2].set(pe_f)
                pp2 = [(ps_a, r_psa), (ps_b, r_psb)]
                for oc in range(0 if skip_ffn else NOC):
                    ps, r_ps = pp2[oc % 2]
                    for kc in range(NHC):
                        P.step("pe", lambda t, oc=oc, kc=kc, ps=ps: t.matmul(
                            ps[:, :],
                            w2b[:, kc * D + oc * 128:
                                kc * D + (oc + 1) * 128],
                            hT[:, kc * 512:(kc + 1) * 512],
                            start=(kc == 0), stop=(kc == NHC - 1)),
                            wait=([(ac_f, gelu_marks2[kc]),
                                   (d_g, w2_marks[kc // (NHC // 4)])]
                                  + (r_ps.waits() if kc == 0 else [])),
                            inc=pe_f if kc == NHC - 1 else None)
                    mm2 = pe_f.n
                    ypar = oc % 2
                    P.step("dv", lambda v, oc=oc, ps=ps, ypar=ypar: v.tensor_scalar(
                        out=y_bf[ypar][:], in0=ps[:, :],
                        scalar1=b2col[:, oc:oc + 1], scalar2=None, op0=ALU.add),
                        wait=[(pe_f, mm2), (dv_su, SETUP)]
                        + r_ybf[ypar].waits(), inc=dv_f)
                    yn = dv_f.n
                    r_ps.set(dv_f)
                    for tc in range(4):
                        P.step("pe", lambda t, tc=tc, ypar=ypar: t.transpose(
                            pt_y[:, tc * 128:(tc + 1) * 128],
                            y_bf[ypar][:, tc * 128:(tc + 1) * 128], id_bf[:]),
                            wait=([(dv_f, yn)] + r_pty.waits())
                            if tc == 0 else [],
                            inc=pe_ft if tc == 3 else None)
                    r_ybf[ypar].set(pe_ft)
                    trn = pe_ft.n
                    for tc in range(4):
                        P.step("dv", lambda v, tc=tc, oc=oc: v.tensor_add(
                            x2_buf[:, tc * D + oc * 128:
                                   tc * D + (oc + 1) * 128],
                            x2_buf[:, tc * D + oc * 128:
                                   tc * D + (oc + 1) * 128],
                            pt_y[:, tc * 128:(tc + 1) * 128]),
                            wait=[(pe_ft, trn), (d_x2, x2_here)]
                            if tc == 0 else [],
                            inc=dv_f if tc == 3 else None)
                    r_pty.set(dv_f)
                prev_w2_end = pe_f.n
                add_done = dv_f.n
                for tc in range(4):
                    t0 = win0 + tc * 128
                    P.step("sy", lambda sy, b=b, t0=t0, tc=tc: sy.dma_start(
                        out=y_out[b, t0:t0 + 128, :],
                        in_=x2_buf[:, tc * D:(tc + 1) * D]),
                        wait=[(dv_f, add_done)] if tc == 0 else [],
                        inc=st_out, inck=16)

            P.step("po", lambda g: g.nop(), wait=[st_out])

        with nc.Block() as block:
            P.emit(block)

    return nc


_CACHE = {}


def _get_program():
    if "nc" not in _CACHE:
        _CACHE["nc"] = build_program()
    return _CACHE["nc"]


def kernel(L0_attn_x, L0_attn_W_DQ, L0_attn_W_UQ, L0_attn_W_DKV, L0_attn_W_UKV,
           L0_attn_W_QR, L0_attn_W_KR, L0_attn_W_O, L0_attn_q_norm_w,
           L0_attn_k_norm_w, L0_ffn_w1, L0_ffn_b1, L0_ffn_w2, L0_ffn_b2):
    x = np.ascontiguousarray(np.asarray(L0_attn_x, dtype=np.float32))
    n_cores = 8
    Bs = x.shape[0] // n_cores
    shards = [np.ascontiguousarray(x[i * Bs:(i + 1) * Bs])
              for i in range(n_cores)]
    w = {
        "WDQ": L0_attn_W_DQ, "WUQ": L0_attn_W_UQ, "WDKV": L0_attn_W_DKV,
        "WUKV": L0_attn_W_UKV, "WQR": L0_attn_W_QR, "WKR": L0_attn_W_KR,
        "WO": L0_attn_W_O, "QNW": L0_attn_q_norm_w, "KNW": L0_attn_k_norm_w,
        "W1": L0_ffn_w1, "B1": L0_ffn_b1, "W2": L0_ffn_w2, "B2": L0_ffn_b2,
    }
    w = {k: np.ascontiguousarray(np.asarray(v, dtype=np.float32))
         for k, v in w.items()}
    nc = _get_program()
    in_maps = [dict(x=shards[i], **w) for i in range(n_cores)]
    res = run_bass_kernel_spmd(nc, in_maps, core_ids=list(range(n_cores)))
    return np.concatenate([r["y"] for r in res.results], axis=0)


# version 32
# speedup vs baseline: 1.0278x; 1.0022x over previous
import numpy as np
from collections import deque

import concourse.bass as bass
import concourse.mybir as mybir
from concourse.bass_utils import run_bass_kernel_spmd
from contextlib import ExitStack

F32 = mybir.dt.float32
BF16 = mybir.dt.bfloat16
AF = mybir.ActivationFunctionType
ALU = mybir.AluOpType
AX = mybir.AxisListType

D = 1024
HD = 64
DL = 512
NFC = D // 128
LN_EPS = 1e-5
RMS_EPS = 1.1920929e-07
NEG_BIG = -1e30


class Sem:
    __slots__ = ("h", "n", "name")

    def __init__(self, nc, name):
        self.h = nc.alloc_semaphore(name)
        self.n = 0
        self.name = name


class Rel:
    __slots__ = ("w",)

    def __init__(self):
        self.w = []

    def set(self, *sems):
        self.w = [(s, s.n) for s in sems]

    def add(self, *sems):
        self.w += [(s, s.n) for s in sems]

    def waits(self):
        return list(self.w)


class Prog:
    ENGS = ("pe", "dv", "ac", "po", "sy")

    def __init__(self, nc):
        self.nc = nc
        self.ops = {e: [] for e in self.ENGS}

    def step(self, eng, fn, wait=(), inc=None, inck=1):
        ws = []
        for w in wait:
            if isinstance(w, Sem):
                ws.append((w, w.n))
            else:
                ws.append((w[0], w[1]))
        self.ops[eng].append((fn, ws, inc, inck))
        if inc is not None:
            inc.n += inck

    def emit(self, block):
        mapping = dict(pe=block.tensor, dv=block.vector, ac=block.scalar,
                       po=block.gpsimd, sy=block.sync)
        for eng in self.ENGS:
            ops = self.ops[eng]

            def body(E, ops=ops):
                last = {}
                for fn, ws, inc, inck in ops:
                    for s, cnt in ws:
                        if cnt > 0 and last.get(s.name, 0) < cnt:
                            E.wait_ge(s.h, cnt)
                            last[s.name] = cnt
                    instr = fn(E)
                    if inc is not None:
                        instr.then_inc(inc.h, inck)

            mapping[eng](body)


def build_program(B=4, T=2048, FF=4096, H=16, skip_attn=False, skip_ffn=False, skip_proj=False):
    assert T % 512 == 0 and FF % 256 == 0 and H % 2 == 0 and H <= 16
    NB = T // 512
    TC = T // 128
    HHALF = min(8, H)
    NGRP = (H + HHALF - 1) // HHALF
    GW = HHALF * HD
    NQT = 4
    NHC = FF // 128
    NOC = D // 128

    nc = bass.Bass("TRN2", target_bir_lowering=False)

    x_in = nc.dram_tensor("x", [B, T, D], F32, kind="ExternalInput").ap()
    wdq_d = nc.dram_tensor("WDQ", [DL, HD], F32, kind="ExternalInput").ap()
    wuq_d = nc.dram_tensor("WUQ", [HD, DL], F32, kind="ExternalInput").ap()
    wdkv_d = nc.dram_tensor("WDKV", [2 * DL, HD], F32, kind="ExternalInput").ap()
    wukv_d = nc.dram_tensor("WUKV", [2 * HD, 2 * DL], F32, kind="ExternalInput").ap()
    wqr_d = nc.dram_tensor("WQR", [HD, HD], F32, kind="ExternalInput").ap()
    wkr_d = nc.dram_tensor("WKR", [HD, HD], F32, kind="ExternalInput").ap()
    wo_d = nc.dram_tensor("WO", [HD, HD], F32, kind="ExternalInput").ap()
    qw_d = nc.dram_tensor("QNW", [HD], F32, kind="ExternalInput").ap()
    kw_d = nc.dram_tensor("KNW", [HD], F32, kind="ExternalInput").ap()
    w1_d = nc.dram_tensor("W1", [D, FF], F32, kind="ExternalInput").ap()
    b1_d = nc.dram_tensor("B1", [FF], F32, kind="ExternalInput").ap()
    w2_d = nc.dram_tensor("W2", [FF, D], F32, kind="ExternalInput").ap()
    b2_d = nc.dram_tensor("B2", [D], F32, kind="ExternalInput").ap()
    y_out = nc.dram_tensor("y", [B, T, D], F32, kind="ExternalOutput").ap()
    x2s = nc.dram_tensor("x2s", [B * T, D], F32, kind="Internal").ap()

    es = ExitStack()
    with es:
        P = Prog(nc)

        def sb(name, shape, dt=F32):
            return es.enter_context(nc.sbuf_tensor(name, shape, dt))

        id_bf = sb("id_bf", [128, 128], BF16)
        negmask = sb("negmask", [128, 128], F32)
        c0125 = sb("c0125", [128, 1], F32)
        c_lneps = sb("c_lneps", [128, 1], F32)
        c_rmseps = sb("c_rmseps", [128, 1], F32)
        c_nhalf = sb("c_nhalf", [128, 1], F32)
        c_inv64 = sb("c_inv64", [128, 1], F32)

        wqT2 = sb("wqT2", [128, HD], BF16)
        wkT2 = sb("wkT2", [128, HD], BF16)
        wvT2 = sb("wvT2", [128, HD], BF16)
        wqrT = sb("wqrT", [128, HD], BF16)
        wkrT = sb("wkrT", [128, HD], BF16)
        woT_pair = sb("woT_pair", [128, 128], BF16)
        qnw_col = sb("qnw_col", [HD, 1], F32)
        knw_col = sb("knw_col", [HD, 1], F32)
        b1col = sb("b1col", [128, NHC], F32)
        b2col = sb("b2col", [128, NOC], F32)

        x2_buf = sb("x2_buf", [128, 4 * D], F32)
        xln = [sb(f"xln{i}", [128, D], BF16) for i in range(2)]
        xn_bf2 = [sb(f"xn_bf{i}", [128, D], BF16) for i in range(2)]
        musum = sb("musum", [128, 1], F32)
        negmu = sb("negmu", [128, 1], F32)
        sqsum = sb("sqsum", [128, 1], F32)
        mu2 = sb("mu2", [128, 1], F32)
        varb = sb("varb", [128, 1], F32)
        lnv = sb("lnv", [128, 1], F32)
        rstd = sb("rstd", [128, 1], F32)

        ps_s0 = es.enter_context(nc.psum_tensor("ps_s0", [128, 512], F32))
        ps_s1 = es.enter_context(nc.psum_tensor("ps_s1", [128, 512], F32))
        ps_a = es.enter_context(nc.psum_tensor("ps_a", [128, 512], F32))
        ps_b = es.enter_context(nc.psum_tensor("ps_b", [128, 512], F32))
        ps_c = es.enter_context(nc.psum_tensor("ps_c", [128, 512], F32))
        ps_d = es.enter_context(nc.psum_tensor("ps_d", [128, 512], F32))
        pt_x = es.enter_context(nc.psum_tensor("pt_x", [128, 1024], BF16))
        pt_y = es.enter_context(nc.psum_tensor("pt_y", [128, 1024], BF16))

        d_w = Sem(nc, "d_w")
        d_x2 = Sem(nc, "d_x2")
        d_xl = [Sem(nc, f"d_xl{i}") for i in range(2)]
        d_g = Sem(nc, "d_g")
        po_set = Sem(nc, "po_set")
        po_chain = Sem(nc, "po_chain")
        pe_su = Sem(nc, "pe_su")
        dv_su = Sem(nc, "dv_su")
        pe_ln = Sem(nc, "pe_ln")
        dv_ln = Sem(nc, "dv_ln")
        ac_ln = Sem(nc, "ac_ln")
        pe_pr = Sem(nc, "pe_pr")
        ac_pr = Sem(nc, "ac_pr")
        dv_pr = Sem(nc, "dv_pr")
        pe_s = Sem(nc, "pe_s")
        dv_st = Sem(nc, "dv_st")
        ac_st = Sem(nc, "ac_st")
        po_st = Sem(nc, "po_st")
        ac_ex = Sem(nc, "ac_ex")
        pe_pt = Sem(nc, "pe_pt")
        dv_pt = Sem(nc, "dv_pt")
        ac_pt = Sem(nc, "ac_pt")
        pe_cx = Sem(nc, "pe_cx")
        dv_ao = Sem(nc, "dv_ao")
        pe_ao = Sem(nc, "pe_ao")
        dv_res = Sem(nc, "dv_res")
        f_x2 = Sem(nc, "f_x2")
        f_x2e = Sem(nc, "f_x2e")
        f_xl = [Sem(nc, f"f_xl{i}") for i in range(2)]
        pe_f = Sem(nc, "pe_f")
        ac_f = Sem(nc, "ac_f")
        dv_f = Sem(nc, "dv_f")
        po_f = Sem(nc, "po_f")
        pe_ft = Sem(nc, "pe_ft")
        st_out = Sem(nc, "st_out")

        r_ptx, r_pty = Rel(), Rel()
        r_xnbf2 = [Rel(), Rel()]
        r_psa, r_psb, r_psc, r_psd = Rel(), Rel(), Rel(), Rel()
        r_ps0, r_ps1 = Rel(), Rel()
        r_xnbf = Rel()
        r_S = [Rel(), Rel()]
        r_PA, r_PB = Rel(), Rel()
        r_PT = [Rel(), Rel()]

        r_win = Rel()
        r_kv = Rel()
        r_lbuf = Rel()

        P.step("po", lambda g: g.memset(id_bf[:], 0.0), inc=po_chain)
        P.step("po", lambda g: g.affine_select(
            out=id_bf[:], in_=id_bf[:], compare_op=ALU.not_equal, fill=1.0,
            base=0, pattern=[[-1, 128]], channel_multiplier=1),
            wait=[po_chain], inc=po_set)
        P.step("po", lambda g: g.memset(negmask[:], 0.0), inc=po_chain)
        P.step("po", lambda g: g.affine_select(
            out=negmask[:], in_=negmask[:], compare_op=ALU.is_ge, fill=NEG_BIG,
            base=0, pattern=[[-1, 128]], channel_multiplier=1),
            wait=[po_chain], inc=po_set)
        for buf, val in ((c0125, 0.125), (c_lneps, LN_EPS), (c_rmseps, RMS_EPS),
                         (c_nhalf, -0.5), (c_inv64, 1.0 / 64.0)):
            P.step("po", lambda g, buf=buf, val=val: g.memset(buf[:], val),
                   inc=po_set)
        P.step("po", lambda g: g.memset(woT_pair[:], 0.0), inc=po_set)
        POSET = po_set.n

        with ExitStack() as su:
            def tb(name, shape, dt=F32):
                return su.enter_context(nc.sbuf_tensor(name, shape, dt))

            id_f = tb("id_f", [128, 128])
            P.step("po", lambda g: g.memset(id_f[:], 0.0), inc=po_chain)
            P.step("po", lambda g: g.affine_select(
                out=id_f[:], in_=id_f[:], compare_op=ALU.not_equal, fill=1.0,
                base=0, pattern=[[-1, 128]], channel_multiplier=1),
                wait=[po_chain], inc=po_set)
            wdq_s = tb("wdq_s", [128, 4 * HD])
            wuq_s = tb("wuq_s", [HD, DL])
            wdkv_s = tb("wdkv_s", [128, 8 * HD])
            wukv_s = tb("wukv_s", [2 * HD, 2 * DL])
            wqr_s = tb("wqr_s", [HD, HD])
            wkr_s = tb("wkr_s", [HD, HD])
            wo_s = tb("wo_s", [HD, HD])
            wuqT_s = tb("wuqT_s", [128, 4 * HD])
            wukvT_s = tb("wukvT_s", [128, 8 * 2 * HD])
            b1_raw = tb("b1_raw", [NHC, 128])
            b2_raw = tb("b2_raw", [NOC, 128])

            def wload(dst, src):
                P.step("sy", lambda s, d=dst, sr=src: s.dma_start(out=d, in_=sr),
                       inc=d_w, inck=16)

            wload(wdq_s[:].rearrange("p (c f) -> p c f", f=HD),
                  wdq_d.rearrange("(c p) f -> p c f", p=128))
            wload(wuq_s[:], wuq_d)
            wload(wdkv_s[:].rearrange("p (c f) -> p c f", f=HD),
                  wdkv_d.rearrange("(c p) f -> p c f", p=128))
            wload(wukv_s[:], wukv_d)
            wload(wqr_s[:], wqr_d)
            wload(wkr_s[:], wkr_d)
            wload(wo_s[:], wo_d)
            wload(qnw_col[:], qw_d.rearrange("(f o) -> f o", o=1))
            wload(knw_col[:], kw_d.rearrange("(f o) -> f o", o=1))
            wload(b1_raw[:], b1_d.rearrange("(c p) -> c p", p=128))
            wload(b2_raw[:], b2_d.rearrange("(c p) -> c p", p=128))
            WALL = d_w.n

            for c in range(4):
                P.step("pe", lambda t, c=c: t.transpose(
                    ps_a[:, c * HD:(c + 1) * HD],
                    wuq_s[:, c * 128:(c + 1) * 128], id_f[0:HD, 0:HD]),
                    wait=[(d_w, WALL), po_set],
                    inc=pe_su if c == 3 else None)
            P.step("dv", lambda v: v.tensor_copy(wuqT_s[:], ps_a[:, 0:4 * HD]),
                   wait=[pe_su], inc=dv_su)
            r_psa.set(dv_su)
            for rnd in range(2):
                for c in range(4):
                    cc = rnd * 4 + c
                    P.step("pe", lambda t, cc=cc, c=c: t.transpose(
                        ps_b[:, c * 128:(c + 1) * 128],
                        wukv_s[:, cc * 128:(cc + 1) * 128], id_f[:]),
                        wait=r_psb.waits() if c == 0 else [],
                        inc=pe_su if c == 3 else None)
                P.step("dv", lambda v, rnd=rnd: v.tensor_copy(
                    wukvT_s[:, rnd * 512:(rnd + 1) * 512], ps_b[:, :]),
                    wait=[pe_su], inc=dv_su)
                r_psb.set(dv_su)
            P.step("pe", lambda t: t.transpose(ps_a[0:HD, 0:HD], wqr_s[:],
                                               id_f[0:HD, 0:HD]), wait=r_psa.waits())
            P.step("pe", lambda t: t.transpose(ps_a[0:HD, HD:2 * HD], wkr_s[:],
                                               id_f[0:HD, 0:HD]))
            P.step("pe", lambda t: t.transpose(ps_a[0:HD, 2 * HD:3 * HD],
                                               wo_s[:], id_f[0:HD, 0:HD]), inc=pe_su)
            P.step("dv", lambda v: v.tensor_copy(wqrT[0:HD, :], ps_a[0:HD, 0:HD]),
                   wait=[pe_su])
            P.step("dv", lambda v: v.tensor_copy(wqrT[HD:128, :], ps_a[0:HD, 0:HD]))
            P.step("dv", lambda v: v.tensor_copy(wkrT[0:HD, :], ps_a[0:HD, HD:2 * HD]))
            P.step("dv", lambda v: v.tensor_copy(wkrT[HD:128, :], ps_a[0:HD, HD:2 * HD]))
            P.step("dv", lambda v: v.tensor_copy(
                woT_pair[0:HD, 0:HD], ps_a[0:HD, 2 * HD:3 * HD]))
            P.step("dv", lambda v: v.tensor_copy(
                woT_pair[HD:128, HD:128], ps_a[0:HD, 2 * HD:3 * HD]), inc=dv_su)
            r_psa.set(dv_su)
            P.step("pe", lambda t: t.transpose(ps_b[:, 0:NHC], b1_raw[:],
                                               id_f[0:NHC, 0:NHC]), wait=r_psb.waits())
            P.step("pe", lambda t: t.transpose(ps_b[:, 256:256 + NOC], b2_raw[:],
                                               id_f[0:NOC, 0:NOC]), inc=pe_su)
            P.step("dv", lambda v: v.tensor_copy(b1col[:], ps_b[:, 0:NHC]),
                   wait=[pe_su])
            P.step("dv", lambda v: v.tensor_copy(b2col[:], ps_b[:, 256:256 + NOC]),
                   inc=dv_su)
            r_psb.set(dv_su)

            for c in range(4):
                P.step("pe", lambda t, c=c: t.matmul(
                    ps_c[0:HD, 0:HD],
                    wdq_s[:].rearrange("p (c f) -> p c f", f=HD)[:, c, :],
                    wuqT_s[:, c * HD:(c + 1) * HD],
                    start=(c == 0), stop=(c == 3)),
                    wait=[dv_su] if c == 0 else [],
                    inc=pe_su if c == 3 else None)
            for c in range(8):
                P.step("pe", lambda t, c=c: t.matmul(
                    ps_d[0:HD, 0:2 * HD],
                    wdkv_s[:].rearrange("p (c f) -> p c f", f=HD)[:, c, :],
                    wukvT_s[:, c * 2 * HD:(c + 1) * 2 * HD],
                    start=(c == 0), stop=(c == 7)),
                    inc=pe_su if c == 7 else None)
            P.step("dv", lambda v: v.tensor_copy(wqT2[0:HD, :], ps_c[0:HD, 0:HD]),
                   wait=[pe_su])
            P.step("dv", lambda v: v.tensor_copy(wqT2[HD:128, :],
                                                 ps_c[0:HD, 0:HD]))
            P.step("dv", lambda v: v.tensor_copy(wkT2[0:HD, :],
                                                 ps_d[0:HD, 0:HD]))
            P.step("dv", lambda v: v.tensor_copy(wkT2[HD:128, :],
                                                 ps_d[0:HD, 0:HD]))
            P.step("dv", lambda v: v.tensor_copy(wvT2[0:HD, :],
                                                 ps_d[0:HD, HD:2 * HD]))
            P.step("dv", lambda v: v.tensor_copy(wvT2[HD:128, :],
                                                 ps_d[0:HD, HD:2 * HD]),
                   inc=dv_su)
            r_psc.set(dv_su)
            r_psd.set(dv_su)
            SETUP = dv_su.n

        with ExitStack() as att:
            def asb(name, shape, dt=BF16):
                return att.enter_context(nc.sbuf_tensor(name, shape, dt))

            PCAP = 128 * sum(4 * (NB - 1) + qt + 1 for qt in range(NQT))
            Xn1T = asb("Xn1T", [64, H * 512])
            kstack = [asb(f"kst{h}", [128, T]) for h in range(H)]
            qstack = [asb(f"qst{h}", [128, 512]) for h in range(H)]
            vstore = [asb(f"vst{g}", [128, TC * GW]) for g in range(NGRP)]
            qn_tok = asb("qn_tok", [128, GW])
            msq = asb("msq", [128, HHALF], F32)
            ln8 = asb("ln8", [128, HHALF], F32)
            rstd8 = asb("rstd8", [128, HHALF], F32)
            rstdq_win = asb("rstdq_win", [128, 4 * H], F32)
            S2 = [asb(f"S_sb{i}", [128, T], F32) for i in range(2)]
            mcol = asb("mcol", [128, 6], F32)
            mrow = asb("mrow", [128, 1], F32)
            negm8 = asb("negm8", [128, 8], F32)
            P_A = asb("P_A", [128, PCAP])
            P_B = asb("P_B", [128, PCAP])
            PT2 = [asb(f"PT2{par}", [128, 1024]) for par in range(2)]
            l_buf = asb("l_buf", [128, 8], F32)
            rl_buf = asb("rl_buf", [128, 8], F32)
            ctx_bf = asb("ctx_bf", [128, 512])
            attn_bf = ctx_bf
            sqp_bf = asb("sqp_bf", [128, GW])
            r_ctxbf = Rel()
            r_sqp = Rel()
            r_attnbf = r_ctxbf
            r_qntok = Rel()
            r_rstd8, r_msq = Rel(), Rel()

            xnT_marks = {}

            def emit_W1(b_, qb_, thunks=None):
                win0_ = qb_ * 512

                def mk_chain(tc):
                    def chain():
                        t0 = win0_ + tc * 128
                        par = tc % 2
                        P.step("po", lambda g, b=b_, t0=t0, par=par: g.dma_start(
                            out=xln[par][:], in_=x_in[b, t0:t0 + 128, :]),
                            wait=[f_xl[par]], inc=d_xl[par], inck=16)
                        ld = d_xl[par].n
                        P.step("dv", lambda v, par=par: v.reduce_sum(
                            musum[:], xln[par][:], axis=AX.X),
                            wait=[(d_xl[par], ld)], inc=dv_ln)
                        P.step("ac", lambda a, par=par: a.activation(
                            out=xn_bf2[tc % 2][:], in_=xln[par][:], func=AF.Square,
                            accum_out=sqsum[:]),
                            wait=[(d_xl[par], ld), dv_ln]
                            + r_xnbf2[tc % 2].waits(),
                            inc=ac_ln)
                        P.step("dv", lambda v: v.tensor_scalar(
                            out=negmu[:], in0=musum[:], scalar1=-1.0 / D,
                            scalar2=None, op0=ALU.mult), wait=[dv_ln],
                            inc=dv_ln)
                        P.step("dv", lambda v: v.tensor_mul(
                            mu2[:], negmu[:], negmu[:]), wait=[dv_ln],
                            inc=dv_ln)
                        P.step("dv", lambda v: v.tensor_scalar(
                            out=varb[:], in0=sqsum[:], scalar1=1.0 / D,
                            scalar2=None, op0=ALU.mult),
                            wait=[ac_ln, dv_ln], inc=dv_ln)
                        P.step("dv", lambda v: v.tensor_sub(
                            varb[:], varb[:], mu2[:]), wait=[dv_ln],
                            inc=dv_ln)
                        P.step("ac", lambda a: a.activation(
                            out=lnv[:], in_=varb[:], func=AF.Ln,
                            bias=c_lneps[:]),
                            wait=[dv_ln, ac_ln, (po_set, POSET)], inc=ac_ln)
                        P.step("ac", lambda a: a.activation(
                            out=rstd[:], in_=lnv[:], func=AF.Exp,
                            scale=c_nhalf[:]),
                            wait=[ac_ln, f_xl[0], f_xl[1]], inc=ac_ln)
                        P.step("dv", lambda v, par=par, tc=tc: v.tensor_scalar(
                            out=xn_bf2[tc % 2][:], in0=xln[par][:],
                            scalar1=negmu[:],
                            scalar2=rstd[:], op0=ALU.add, op1=ALU.mult),
                            wait=[ac_ln] + r_xnbf2[tc % 2].waits(),
                            inc=f_xl[par])
                    return chain

                def mk_tr(tc):
                    def tr():
                        par = tc % 2
                        xn_done = f_xl[par].n
                        for fc in range(NFC):
                            P.step("pe", lambda t, fc=fc: t.transpose(
                                pt_x[:, fc * 128:(fc + 1) * 128],
                                xn_bf2[tc % 2][:, fc * 128:(fc + 1) * 128],
                                id_bf[:]),
                                wait=([(f_xl[par], xn_done), (po_set, POSET)]
                                      + r_ptx.waits() + [(pe_pr, pe_pr.n)])
                                if fc == 0 else [],
                                inc=pe_ln if fc == NFC - 1 else None)
                        r_xnbf2[tc % 2].set(pe_ln)
                        P.step("dv", lambda v, tc=tc: v.tensor_copy(
                            Xn1T[:].rearrange("p (c f) -> p c f", f=1024)
                            [:, :, tc * 128:(tc + 1) * 128],
                            pt_x[0:64, :].rearrange("p (c f) -> p c f", f=128)
                            [:, 0:H // 2, :]),
                            wait=[pe_ln], inc=dv_ln)
                        P.step("dv", lambda v, tc=tc: v.tensor_copy(
                            Xn1T[:].rearrange("p (c f) -> p c f", f=1024)
                            [:, :, 512 + tc * 128:512 + (tc + 1) * 128],
                            pt_x[64:128, :].rearrange("p (c f) -> p c f",
                                                      f=128)
                            [:, 0:H // 2, :]),
                            inc=dv_ln)
                        r_ptx.set(dv_ln)
                        if tc == 3:
                            xnT_marks[(b_, qb_)] = dv_ln.n
                    return tr

                seq = []
                for tc in range(4):
                    seq.append(mk_chain(tc))
                    seq.append(mk_tr(tc))
                if thunks is None:
                    for f in seq:
                        f()
                else:
                    thunks.extend(seq)

            wo_q = []
            wo_mark = [0]
            for b in range(B):
                for qb in range(NB):
                    win0 = qb * 512

                    for tc in range(4):
                        t0 = win0 + tc * 128
                        P.step("sy", lambda s, b=b, t0=t0, tc=tc: s.dma_start(
                            out=x2_buf[:, tc * D:(tc + 1) * D],
                            in_=x_in[b, t0:t0 + 128, :]),
                            wait=[f_x2] if tc == 0 else [], inc=d_x2, inck=16)
                    x2_ready = d_x2.n

                    rot4 = [(ps_a, r_psa), (ps_b, r_psb),
                            (ps_c, r_psc), (ps_d, r_psd)]
                    uidx = [0]
                    pendu = deque()

                    def w2_stage2(tc, tcg, grp, kind, ps, r_ps, mm_done):
                        if kind == "v":
                            P.step("ac", lambda a, grp=grp, tcg=tcg, ps=ps: a.activation(
                                out=vstore[grp][:, tcg * GW:(tcg + 1) * GW],
                                in_=ps[:, 0:GW], func=AF.Copy),
                                wait=[(pe_pr, mm_done)] + r_kv.waits(),
                                inc=ac_pr)
                            r_ps.set(ac_pr)
                            return

                        if kind == "q":
                            P.step("dv", lambda v, ps=ps: v.tensor_copy(
                                qn_tok[:], ps[:, 0:GW]),
                                wait=[(pe_pr, mm_done)] + r_qntok.waits(),
                                inc=dv_pr)
                            ev_mark = dv_pr.n
                            for hh in range(HHALF):
                                P.step("pe", lambda t, hh=hh: t.transpose(
                                    pt_y[0:HD, hh * 128:(hh + 1) * 128],
                                    qn_tok[:, hh * HD:(hh + 1) * HD],
                                    id_bf[:]),
                                    wait=([(dv_pr, ev_mark)] + r_pty.waits())
                                    if hh == 0 else [],
                                    inc=pe_pr if hh == HHALF - 1 else None)
                            r_qntok.set(pe_pr)
                            tr_done = pe_pr.n
                            for hh in range(HHALF):
                                h = grp * HHALF + hh
                                dst = qstack[h][0:HD, tc * 128:(tc + 1) * 128]
                                P.step("dv", lambda v, hh=hh, dst=dst: v.tensor_scalar(
                                    out=dst,
                                    in0=pt_y[0:HD, hh * 128:(hh + 1) * 128],
                                    scalar1=qnw_col[:], scalar2=None,
                                    op0=ALU.mult),
                                    wait=([(pe_pr, tr_done)] + r_win.waits())
                                    if hh == 0 else [],
                                    inc=dv_pr if hh == HHALF - 1 else None)
                            r_pty.set(dv_pr)
                            P.step("ac", lambda a, ps=ps: a.activation(
                                out=sqp_bf[:], in_=ps[:, 0:GW],
                                func=AF.Square),
                                wait=[(pe_pr, mm_done), (dv_pr, ev_mark)]
                                + r_sqp.waits(),
                                inc=ac_pr)
                            sq_mark = ac_pr.n
                            P.step("dv", lambda v: v.tensor_reduce(
                                out=msq[:],
                                in_=sqp_bf[:].rearrange("p (h f) -> p h f",
                                                        f=HD),
                                axis=AX.X, op=ALU.add),
                                wait=[(ac_pr, sq_mark)] + r_msq.waits(),
                                inc=dv_pr)
                            r_sqp.set(dv_pr)
                            red_mark = dv_pr.n
                            r_ps.set(dv_pr)
                            P.step("ac", lambda a: a.activation(
                                out=ln8[:], in_=msq[:], func=AF.Ln,
                                bias=c_rmseps[:], scale=c_inv64[:]),
                                wait=[(dv_pr, red_mark), ac_pr], inc=ac_pr)
                            r_msq.set(ac_pr)
                            P.step("ac", lambda a: a.activation(
                                out=rstd8[:], in_=ln8[:], func=AF.Exp,
                                scale=c_nhalf[:]),
                                wait=[ac_pr] + r_rstd8.waits(), inc=ac_pr)
                            rstd_done = ac_pr.n
                            c0 = tc * H + grp * HHALF
                            P.step("dv", lambda v, c0=c0: v.tensor_scalar(
                                out=rstdq_win[:, c0:c0 + HHALF],
                                in0=rstd8[:], scalar1=0.125, scalar2=None,
                                op0=ALU.mult),
                                wait=[(ac_pr, rstd_done)], inc=dv_pr)
                            r_rstd8.set(dv_pr)
                            return

                        P.step("ac", lambda a, ps=ps: a.activation(
                            out=sqp_bf[:], in_=ps[:, 0:GW],
                            func=AF.Square),
                            wait=[(pe_pr, mm_done)] + r_sqp.waits(),
                            inc=ac_pr)
                        sq_mark = ac_pr.n
                        P.step("dv", lambda v: v.tensor_reduce(
                            out=msq[:],
                            in_=sqp_bf[:].rearrange(
                                "p (h f) -> p h f", f=HD),
                            axis=AX.X, op=ALU.add),
                            wait=[(ac_pr, sq_mark)] + r_msq.waits(),
                            inc=dv_pr)
                        r_sqp.set(dv_pr)
                        red_mark = dv_pr.n
                        P.step("ac", lambda a: a.activation(
                            out=ln8[:], in_=msq[:], func=AF.Ln,
                            bias=c_rmseps[:], scale=c_inv64[:]),
                            wait=[(dv_pr, red_mark), ac_pr], inc=ac_pr)
                        r_msq.set(ac_pr)
                        P.step("ac", lambda a: a.activation(
                            out=rstd8[:], in_=ln8[:], func=AF.Exp,
                            scale=c_nhalf[:]),
                            wait=[ac_pr] + r_rstd8.waits(), inc=ac_pr)
                        rstd_done = ac_pr.n
                        for hh in range(HHALF):
                            P.step("dv", lambda v, hh=hh, ps=ps: v.tensor_scalar(
                                out=qn_tok[:, hh * HD:(hh + 1) * HD],
                                in0=ps[:, hh * HD:(hh + 1) * HD],
                                scalar1=rstd8[:, hh:hh + 1],
                                scalar2=None, op0=ALU.mult),
                                wait=([(ac_pr, rstd_done)]
                                      + r_qntok.waits())
                                if hh == 0 else [],
                                inc=dv_pr if hh == HHALF - 1 else None)
                        r_rstd8.set(dv_pr)
                        r_ps.set(dv_pr)
                        r_ps.add(ac_pr)
                        appl = dv_pr.n
                        for hh in range(HHALF):
                            P.step("pe", lambda t, hh=hh: t.transpose(
                                pt_y[0:HD, hh * 128:(hh + 1) * 128],
                                qn_tok[:, hh * HD:(hh + 1) * HD],
                                id_bf[:]),
                                wait=([(dv_pr, appl)] + r_pty.waits())
                                if hh == 0 else [],
                                inc=pe_pr if hh == HHALF - 1 else None)
                        r_qntok.set(pe_pr)
                        tr_done = pe_pr.n
                        for hh in range(HHALF):
                            h = grp * HHALF + hh
                            dst = kstack[h][0:HD, tcg * 128:(tcg + 1) * 128]
                            P.step("dv", lambda v, hh=hh, dst=dst: v.tensor_scalar(
                                out=dst,
                                in0=pt_y[0:HD, hh * 128:(hh + 1) * 128],
                                scalar1=knw_col[:], scalar2=None,
                                op0=ALU.mult),
                                wait=([(pe_pr, tr_done)] + r_kv.waits())
                                if hh == 0 else [],
                                inc=dv_pr if hh == HHALF - 1 else None)
                        r_pty.set(dv_pr)
                    if (b, qb) not in xnT_marks:
                        emit_W1(b, qb)
                    xnT_done = xnT_marks[(b, qb)]
                    for tc in range(4):
                        t0 = win0 + tc * 128
                        tcg = t0 // 128
                        for grp in range(0 if skip_proj else NGRP):
                            for kind in ("q", "k", "v"):
                                ps, r_ps = rot4[uidx[0] % 4]
                                uidx[0] += 1
                                wT = {"q": wqT2, "k": wkT2, "v": wvT2}[kind]
                                for hh in range(HHALF):
                                    h = grp * HHALF + hh
                                    P.step("pe", lambda t, h=h, hh=hh, wT=wT, tc=tc, ps=ps: t.matmul(
                                        ps[:, hh * HD:(hh + 1) * HD],
                                        Xn1T[0:HD, h * 512 + tc * 128:
                                             h * 512 + (tc + 1) * 128],
                                        wT[0:HD, :], start=True,
                                        stop=True),
                                        wait=([(dv_ln, xnT_done),
                                               (dv_su, SETUP)] + r_ps.waits())
                                        if hh == 0 else [],
                                        inc=pe_pr if hh == HHALF - 1 else None)
                                pendu.append((tc, tcg, grp, kind, ps, r_ps,
                                              pe_pr.n))
                                if len(pendu) >= 3:
                                    w2_stage2(*pendu.popleft())

                    while pendu:
                        w2_stage2(*pendu.popleft())
                    stacks_done = dv_pr.n
                    pingpong = [(ps_a, r_psa), (ps_b, r_psb)]
                    pidx = 0
                    for kind in (() if skip_proj else ("qr", "kr")):
                        wT = wqrT if kind == "qr" else wkrT
                        for h in range(H):
                            ps, r_ps = pingpong[pidx]
                            pidx ^= 1
                            if kind == "qr":
                                rhs = qstack[h][0:HD, 0:512]
                            else:
                                rhs = Xn1T[0:HD, h * 512:(h + 1) * 512]
                            P.step("pe", lambda t, wT=wT, rhs=rhs, ps=ps: t.matmul(
                                ps[0:HD, :], wT[0:HD, :], rhs,
                                start=True, stop=True),
                                wait=[(dv_pr, stacks_done)] + r_ps.waits(),
                                inc=pe_pr)
                            mmn = pe_pr.n
                            if kind == "qr":
                                dst = qstack[h][HD:128, 0:512]
                            else:
                                dst = kstack[h][HD:128, win0:win0 + 512]
                            P.step("ac", lambda a, dst=dst, ps=ps: a.activation(
                                out=dst, in_=ps[0:HD, :], func=AF.Copy),
                                wait=[(pe_pr, mmn)], inc=ac_pr)
                            r_ps.set(ac_pr)
                    win_ready = dv_pr.n
                    win_ready_ac = ac_pr.n

                    if skip_attn:
                        P.step("dv", lambda v: v.tensor_copy(
                            mrow[:], musum[:]), wait=[(d_x2, x2_ready)],
                            inc=dv_res)
                        sp = dv_res.n
                        for tc in range(4):
                            row0 = b * T + win0 + tc * 128
                            P.step("po", lambda g, row0=row0, tc=tc: g.dma_start(
                                out=x2s[row0:row0 + 128, :],
                                in_=x2_buf[:, tc * D:(tc + 1) * D]),
                                wait=[(dv_res, sp)] if tc == 0 else [],
                                inc=f_x2, inck=16)
                        r_win.set(pe_s, pe_pr)
                        continue
                    poffs = [0]
                    for qt in range(NQT):
                        poffs.append(poffs[-1] + 128 * (4 * qb + qt + 1))

                    stats_mark = {}
                    negm_free = {}
                    sring = [0]

                    def emit_slot_stats(p, hi, qt):
                        h = 2 * p + hi
                        slot = hi * 4 + qt
                        spar = slot % 2
                        S_sb = S2[spar]
                        Tq = 4 * qb + qt
                        klen = 128 * (Tq + 1)
                        offs = list(range(0, klen, 512))
                        nchunks = 0
                        sping = [(ps_s0, r_ps0), (ps_s1, r_ps1),
                                 (ps_a, r_psa), (ps_b, r_psb)]
                        first_stat = True
                        for ci, off in enumerate(offs):
                            w = min(512, klen - off)
                            ps, r_ps = sping[(sring[0] + ci) % 4]
                            has_diag = (off + w == klen)
                            P.step("pe", lambda t, h=h, qt=qt, off=off, w=w, ps=ps: t.matmul(
                                ps[:, 0:w],
                                qstack[h][:, qt * 128:(qt + 1) * 128],
                                kstack[h][:, off:off + w],
                                start=True, stop=True),
                                wait=[(dv_pr, win_ready),
                                      (ac_pr, win_ready_ac)] + r_ps.waits(),
                                inc=pe_s)
                            mmn = pe_s.n
                            wmain = w - 128 if has_diag else w
                            sg = r_S[spar].waits()
                            first_stat = False
                            if wmain > 0:
                                P.step("dv", lambda v, off=off, wmain=wmain, ps=ps, nchunks=nchunks, S_sb=S_sb: v.tensor_scalar(
                                    out=S_sb[:, off:off + wmain],
                                    in0=ps[:, 0:wmain], scalar1=1.0,
                                    scalar2=None, op0=ALU.mult,
                                    op1=ALU.max,
                                    accum_out=mcol[:, nchunks:nchunks + 1]),
                                    wait=[(pe_s, mmn)] + sg, inc=dv_st)
                                r_ps.set(dv_st)
                                nchunks += 1
                            if has_diag:
                                P.step("dv", lambda v, off=off, w=w, wmain=wmain, ps=ps, S_sb=S_sb: v.scalar_tensor_tensor(
                                    out=S_sb[:, off + wmain:off + w],
                                    in0=ps[:, wmain:w], scalar=1.0,
                                    in1=negmask[:], op0=ALU.mult,
                                    op1=ALU.add),
                                    wait=[(pe_s, mmn), (po_set, POSET)] + sg,
                                    inc=dv_st)
                                P.step("dv", lambda v, off=off, w=w, wmain=wmain, nchunks=nchunks, S_sb=S_sb: v.reduce_max(
                                    out=mcol[:, nchunks:nchunks + 1],
                                    in_=S_sb[:, off + wmain:off + w],
                                    axis=AX.X),
                                    wait=[dv_st], inc=dv_st)
                                nchunks += 1
                                r_ps.set(dv_st)
                                r_ps.add(ac_st)
                        sring[0] = (sring[0] + len(offs)) % 4
                        P.step("dv", lambda v, nchunks=nchunks: v.reduce_max(
                            out=mrow[:], in_=mcol[:, 0:nchunks], axis=AX.X),
                            wait=[dv_st], inc=dv_st)
                        rc = qt * H + h
                        P.step("dv", lambda v, slot=slot, rc=rc: v.tensor_scalar(
                            out=negm8[:, slot:slot + 1], in0=mrow[:],
                            scalar1=rstdq_win[:, rc:rc + 1], scalar2=-1.0,
                            op0=ALU.mult, op1=ALU.mult),
                            wait=[dv_st, (dv_pr, win_ready)]
                            + negm_free.get(slot, []),
                            inc=dv_st)
                        stats_mark[(p, slot)] = dv_st.n

                    def emit_slot_exp(p, hi, qt):
                        slot = hi * 4 + qt
                        spar = slot % 2
                        P_sb = (P_A, P_B)[hi]
                        r_P = (r_PA, r_PB)[hi]
                        klen = 128 * (4 * qb + qt + 1)
                        po0 = poffs[qt]
                        rc = qt * H + 2 * p + hi
                        P.step("ac", lambda a, P_sb=P_sb, po0=po0, klen=klen, slot=slot, spar=spar, rc=rc: a.activation(
                            out=P_sb[:, po0:po0 + klen],
                            in_=S2[spar][:, 0:klen], func=AF.Exp,
                            scale=rstdq_win[:, rc:rc + 1],
                            bias=negm8[:, slot:slot + 1],
                            accum_out=l_buf[:, slot:slot + 1]),
                            wait=[(dv_st, stats_mark[(p, slot)])]
                            + r_P.waits()
                            + (r_lbuf.waits() if slot == 0 else []),
                            inc=ac_ex)
                        negm_free[slot] = [(ac_ex, ac_ex.n)]
                        r_S[spar].set(ac_ex)

                    def emit_wo(p, rl_done):
                        P.step("pe", lambda t: t.matmul(
                            ps_d[:, :], woT_pair[:], ctx_bf[:], start=True,
                            stop=True),
                            wait=[dv_ao, (dv_su, SETUP)] + r_psd.waits(),
                            inc=pe_ao)
                        r_ctxbf.set(pe_ao)
                        P.step("ac", lambda a: a.activation(
                            out=attn_bf[:], in_=ps_d[:, :], func=AF.Copy),
                            wait=[pe_ao] + r_attnbf.waits(), inc=ac_ex)
                        r_psd.set(ac_ex)
                        ab_done = ac_ex.n
                        for qt in range(NQT):
                            P.step("pe", lambda t, qt=qt: t.transpose(
                                pt_x[:, qt * 128:(qt + 1) * 128],
                                attn_bf[:, qt * 128:(qt + 1) * 128],
                                id_bf[:]),
                                wait=([(ac_ex, ab_done)] + r_ptx.waits())
                                if qt == 0 else [],
                                inc=pe_ao if qt == NQT - 1 else None)
                        r_attnbf.set(pe_ao)
                        trn = pe_ao.n
                        for qt in range(NQT):
                            for hi in range(2):
                                slot = hi * 4 + qt
                                col0 = qt * D + p * 128 + hi * HD
                                P.step("dv", lambda v, qt=qt, hi=hi, slot=slot, col0=col0: v.scalar_tensor_tensor(
                                    out=x2_buf[:, col0:col0 + HD],
                                    in0=pt_x[:, qt * 128 + hi * HD:
                                             qt * 128 + (hi + 1) * HD],
                                    scalar=rl_buf[:, slot:slot + 1],
                                    in1=x2_buf[:, col0:col0 + HD],
                                    op0=ALU.mult, op1=ALU.add),
                                    wait=([(pe_ao, trn), (dv_res, rl_done),
                                           (d_x2, x2_ready)]
                                          if (qt == 0 and hi == 0) else []),
                                    inc=(dv_res
                                         if (qt == NQT - 1 and hi == 1)
                                         else None))
                        r_ptx.set(dv_res)
                        return dv_res.n

                    def emit_pair_ctx(p):
                        if wo_q:
                            wo_mark[0] = wo_q.pop(0)()
                        P.step("dv", lambda v: v.reciprocal(rl_buf[:],
                                                            l_buf[:]),
                               wait=[ac_ex, (dv_res, wo_mark[0])],
                               inc=dv_res)
                        r_lbuf.set(dv_res)
                        rl_done = dv_res.n
                        nj = 4 * qb + 4
                        ptb = (pt_x, pt_y)
                        rptb = (r_ptx, r_pty)
                        for j in range(nj):
                            par = j % 2
                            qt_min = max(0, j - 4 * qb)
                            for hi in range(2):
                                P_sb = (P_A, P_B)[hi]
                                for qt in range(qt_min, NQT):
                                    src0 = poffs[qt] + j * 128
                                    P.step("pe", lambda t, P_sb=P_sb, hi=hi, qt=qt, src0=src0, par=par: t.transpose(
                                        ptb[par][:, hi * 512 + qt * 128:
                                                 hi * 512 + (qt + 1) * 128],
                                        P_sb[:, src0:src0 + 128], id_bf[:]),
                                        wait=([ac_ex] + rptb[par].waits())
                                        if (hi == 0 and qt == qt_min) else [],
                                        inc=pe_pt
                                        if (hi == 1 and qt == NQT - 1)
                                        else None)
                            trn = pe_pt.n
                            P.step("dv", lambda v, par=par, qt_min=qt_min: v.tensor_copy(
                                PT2[par][:].rearrange("p (h f) -> p h f",
                                                      f=512)
                                [:, :, qt_min * 128:512],
                                ptb[par][:].rearrange("p (h f) -> p h f",
                                                      f=512)
                                [:, :, qt_min * 128:512]),
                                wait=[(pe_pt, trn)] + r_PT[par].waits(),
                                inc=dv_pt)
                            ev_w = (dv_pt, dv_pt.n)
                            rptb[par].set(dv_pt)
                            for hi in range(2):
                                h = 2 * p + hi
                                grp = h // HHALF
                                hh = h % HHALF
                                v0 = j * GW + hh * HD
                                P.step("pe", lambda t, hi=hi, grp=grp, v0=v0, par=par, qt_min=qt_min, j=j, nj=nj: t.matmul(
                                    ps_c[hi * HD:(hi + 1) * HD,
                                         qt_min * 128:512],
                                    vstore[grp][:, v0:v0 + HD],
                                    PT2[par][:, hi * 512 + qt_min * 128:
                                             hi * 512 + 512],
                                    start=(j == 0), stop=(j == nj - 1),
                                    tile_position=(0, hi * HD),
                                    skip_group_check=True),
                                    wait=([ev_w,
                                           (dv_pr, win_ready),
                                           (ac_pr, win_ready_ac)]
                                          + (r_psc.waits() if j == 0 else []))
                                    if hi == 0 else [],
                                    inc=pe_cx if hi == 1 else None)
                            r_PT[par].set(pe_cx)
                        r_PA.set(pe_pt)
                        r_PB.set(pe_pt)

                        P.step("dv", lambda v: v.tensor_copy(ctx_bf[:],
                                                             ps_c[:, :]),
                               wait=[pe_cx] + r_ctxbf.waits(), inc=dv_ao)
                        r_psc.set(dv_ao)
                        wo_q.append(lambda p=p, rl_done=rl_done:
                                    emit_wo(p, rl_done))

                    pend = deque()
                    w1q = []
                    slot_ctr = [0]
                    for p in range(H // 2):
                        if p == H // 2 - 3:
                            nb, nqb = (b, qb + 1) if qb + 1 < NB else \
                                      (b + 1, 0)
                            if nb < B:
                                emit_W1(nb, nqb, thunks=w1q)
                        for hi in range(2):
                            for qt in range(NQT):
                                slot_ctr[0] += 1
                                if w1q and slot_ctr[0] % 3 == 0:
                                    w1q.pop(0)()
                                if len(pend) >= 2:
                                    g = pend.popleft()
                                    emit_slot_exp(*g)
                                    if g[1] == 1 and g[2] == NQT - 1:
                                        emit_pair_ctx(g[0])
                                emit_slot_stats(p, hi, qt)
                                pend.append((p, hi, qt))
                    while pend:
                        g = pend.popleft()
                        emit_slot_exp(*g)
                        if g[1] == 1 and g[2] == NQT - 1:
                            emit_pair_ctx(g[0])
                    while w1q:
                        w1q.pop(0)()
                    while wo_q:
                        wo_mark[0] = wo_q.pop(0)()
                    sp = dv_res.n
                    for tc in range(4):
                        row0 = b * T + win0 + tc * 128
                        P.step("po", lambda g, row0=row0, tc=tc: g.dma_start(
                            out=x2s[row0:row0 + 128, :],
                            in_=x2_buf[:, tc * D:(tc + 1) * D]),
                            wait=[(dv_res, sp)] if tc == 0 else [],
                            inc=f_x2, inck=16)
                    r_win.set(pe_s, pe_pr)
                r_kv.set(pe_s, pe_cx)

        with ExitStack() as ffn:
            def fsb(name, shape, dt=BF16):
                return ffn.enter_context(nc.sbuf_tensor(name, shape, dt))

            w1b = fsb("w1b", [128, NFC * FF])
            w2b = fsb("w2b", [128, NHC * D])
            xn2 = fsb("xn2", [128, D])
            nmr = ffn.enter_context(nc.sbuf_tensor("nmr", [128, 1], F32))
            Xn2T2 = [fsb(f"Xn2T{i}", [128, NFC * 512]) for i in range(2)]
            r_Xn2Tb = [Rel(), Rel()]
            hT = fsb("hT", [128, NHC * 512])
            y_bf = [fsb(f"ybf{i}", [128, 512]) for i in range(2)]
            r_xn2, r_Xn2T = Rel(), Rel()
            r_ybf = [Rel(), Rel()]

            SPILLS = f_x2.n
            LN1_DV = dv_ln.n

            prev_w2_end = 0
            ln2_marks = {}
            st_guard = [0]

            def emit_ln2(s_):
                b_, qb_ = divmod(s_, NB)
                win0_ = qb_ * 512
                lds = []

                def emit_load(tc):
                    row0 = b_ * T + win0_ + tc * 128
                    par = tc % 2
                    P.step("po", lambda g, row0=row0, par=par: g.dma_start(
                        out=xln[par][:], in_=x2s[row0:row0 + 128, :]),
                        wait=[f_xl[par], (f_x2, SPILLS)],
                        inc=d_xl[par], inck=16)
                    lds.append(d_xl[par].n)

                def emit_chain(tc):
                    par = tc % 2
                    ld = lds[tc]
                    P.step("ac", lambda a, par=par: a.activation(
                        out=xn2[:], in_=xln[par][:], func=AF.Copy,
                        accum_out=musum[:]),
                        wait=[(d_xl[par], ld), (po_f, st_guard[0]),
                              (dv_ln, LN1_DV)] + r_xn2.waits(),
                        inc=ac_f)
                    cp_mark = ac_f.n
                    P.step("ac", lambda a, par=par: a.activation(
                        out=xn2[:], in_=xln[par][:],
                        func=AF.Square, accum_out=sqsum[:]),
                        wait=[(ac_f, cp_mark)], inc=ac_f)
                    cs_mark = ac_f.n
                    P.step("po", lambda g: g.tensor_scalar(
                        out=negmu[:], in0=musum[:], scalar1=-1.0 / D,
                        scalar2=None, op0=ALU.mult),
                        wait=[(ac_f, cs_mark), (dv_ln, LN1_DV), po_f],
                        inc=po_f)
                    P.step("po", lambda g: g.tensor_tensor(
                        out=mu2[:], in0=negmu[:], in1=negmu[:],
                        op=ALU.mult), wait=[po_f], inc=po_f)
                    P.step("po", lambda g: g.tensor_scalar(
                        out=varb[:], in0=sqsum[:], scalar1=1.0 / D,
                        scalar2=None, op0=ALU.mult), wait=[po_f], inc=po_f)
                    P.step("po", lambda g: g.tensor_tensor(
                        out=varb[:], in0=varb[:], in1=mu2[:],
                        op=ALU.subtract), wait=[po_f], inc=po_f)
                    vb_mark = po_f.n
                    P.step("ac", lambda a: a.activation(
                        out=lnv[:], in_=varb[:], func=AF.Ln,
                        bias=c_lneps[:]),
                        wait=[(po_f, vb_mark), ac_f], inc=ac_f)
                    P.step("ac", lambda a: a.activation(
                        out=rstd[:], in_=lnv[:], func=AF.Exp,
                        scale=c_nhalf[:]), wait=[ac_f], inc=ac_f)
                    ex_mark = ac_f.n
                    P.step("po", lambda g: g.tensor_tensor(
                        out=nmr[:], in0=negmu[:], in1=rstd[:],
                        op=ALU.mult),
                        wait=[(ac_f, ex_mark), po_f], inc=po_f)
                    st_guard[0] = po_f.n
                    P.step("ac", lambda a, par=par: a.activation(
                        out=xn2[:], in_=xln[par][:], func=AF.Identity,
                        scale=rstd[:], bias=nmr[:]),
                        wait=[(po_f, st_guard[0]), ac_f] + r_xn2.waits(),
                        inc=f_xl[par])
                    xnd = f_xl[par].n
                    for fc in range(NFC):
                        P.step("pe", lambda t, fc=fc: t.transpose(
                            pt_x[:, fc * 128:(fc + 1) * 128],
                            xn2[:, fc * 128:(fc + 1) * 128], id_bf[:]),
                            wait=([(f_xl[par], xnd)] + r_ptx.waits())
                            if fc == 0 else [],
                            inc=pe_ft if fc == NFC - 1 else None)
                    r_xn2.set(pe_ft)
                    P.step("dv", lambda v, tc=tc, s_=s_: v.tensor_copy(
                        Xn2T2[s_ % 2][:].rearrange("p (c f) -> p c f", f=512)
                        [:, :, tc * 128:(tc + 1) * 128],
                        pt_x[:].rearrange("p (c f) -> p c f", f=128)),
                        wait=[pe_ft]
                        + (r_Xn2Tb[s_ % 2].waits() if tc == 0 else []),
                        inc=dv_f)
                    r_ptx.set(dv_f)

                for tc in range(4):
                    emit_load(tc)
                    if tc >= 1:
                        emit_chain(tc - 1)
                emit_chain(3)
                ln2_marks[s_] = dv_f.n

            emit_ln2(0)
            if B * NB > 1:
                emit_ln2(1)
            w1_marks = []
            HCG = FF // 4
            for hg in range(4):
                for fc in range(NFC):
                    P.step("po", lambda g, fc=fc, hg=hg: g.dma_start(
                        out=w1b[:, fc * FF + hg * HCG:
                                fc * FF + (hg + 1) * HCG],
                        in_=w1_d[fc * 128:(fc + 1) * 128,
                                 hg * HCG:(hg + 1) * HCG]),
                        wait=[(d_g, w1_marks[-1])]
                        if (fc == 0 and hg > 0) else [],
                        inc=d_g, inck=16)
                w1_marks.append(d_g.n)
            w2_marks = []
            for kg in range(4):
                for kc in range(kg * (NHC // 4), (kg + 1) * (NHC // 4)):
                    P.step("po", lambda g, kc=kc, kg=kg: g.dma_start(
                        out=w2b[:, kc * D:(kc + 1) * D],
                        in_=w2_d[kc * 128:(kc + 1) * 128, :]),
                        wait=[(d_g, (w2_marks[-1] if kg > 0
                                     else w1_marks[-1]))]
                        if kc == kg * (NHC // 4) else [],
                        inc=d_g, inck=16)
                w2_marks.append(d_g.n)

            for s in range(B * NB):
                b, qb = divmod(s, NB)
                win0 = qb * 512
                if s not in ln2_marks:
                    emit_ln2(s)
                if s + 1 < B * NB and s + 1 not in ln2_marks:
                    emit_ln2(s + 1)
                xn2T_ready = ln2_marks[s]
                for tc in range(4):
                    row0 = b * T + win0 + tc * 128
                    P.step("sy", lambda sy, row0=row0, tc=tc: sy.dma_start(
                        out=x2_buf[:, tc * D:(tc + 1) * D],
                        in_=x2s[row0:row0 + 128, :]),
                        wait=[f_x2, st_out] if tc == 0 else [], inc=d_x2,
                        inck=16)
                x2_here = d_x2.n

                pp1 = [(ps_c, r_psc), (ps_d, r_psd)]
                gelu_marks2 = {}
                for hc in range(0 if skip_ffn else NHC):
                    ps, r_ps = pp1[hc % 2]
                    for fc in range(NFC):
                        P.step("pe", lambda t, fc=fc, hc=hc, ps=ps, s=s: t.matmul(
                            ps[:, :],
                            w1b[:, fc * FF + hc * 128:
                                fc * FF + (hc + 1) * 128],
                            Xn2T2[s % 2][:, fc * 512:(fc + 1) * 512],
                            start=(fc == 0), stop=(fc == NFC - 1)),
                            wait=([(dv_f, xn2T_ready),
                                   (d_g, w1_marks[(hc * 128) // HCG])]
                                  + r_ps.waits()) if fc == 0 else [],
                            inc=pe_f if fc == NFC - 1 else None)
                    mmn = pe_f.n
                    P.step("ac", lambda a, hc=hc, ps=ps: a.activation(
                        out=hT[:, hc * 512:(hc + 1) * 512], in_=ps[:, :],
                        func=AF.Gelu, bias=b1col[:, hc:hc + 1]),
                        wait=[(pe_f, mmn), (pe_f, prev_w2_end),
                              (dv_su, SETUP)], inc=ac_f)
                    gelu_marks2[hc] = ac_f.n
                    r_ps.set(ac_f)
                r_Xn2Tb[s % 2].set(pe_f)
                pp2 = [(ps_a, r_psa), (ps_b, r_psb)]
                for oc in range(0 if skip_ffn else NOC):
                    ps, r_ps = pp2[oc % 2]
                    for kc in range(NHC):
                        P.step("pe", lambda t, oc=oc, kc=kc, ps=ps: t.matmul(
                            ps[:, :],
                            w2b[:, kc * D + oc * 128:
                                kc * D + (oc + 1) * 128],
                            hT[:, kc * 512:(kc + 1) * 512],
                            start=(kc == 0), stop=(kc == NHC - 1)),
                            wait=([(ac_f, gelu_marks2[kc]),
                                   (d_g, w2_marks[kc // (NHC // 4)])]
                                  + (r_ps.waits() if kc == 0 else [])),
                            inc=pe_f if kc == NHC - 1 else None)
                    mm2 = pe_f.n
                    ypar = oc % 2
                    P.step("dv", lambda v, oc=oc, ps=ps, ypar=ypar: v.tensor_scalar(
                        out=y_bf[ypar][:], in0=ps[:, :],
                        scalar1=b2col[:, oc:oc + 1], scalar2=None, op0=ALU.add),
                        wait=[(pe_f, mm2), (dv_su, SETUP)]
                        + r_ybf[ypar].waits(), inc=dv_f)
                    yn = dv_f.n
                    r_ps.set(dv_f)
                    for tc in range(4):
                        P.step("pe", lambda t, tc=tc, ypar=ypar: t.transpose(
                            pt_y[:, tc * 128:(tc + 1) * 128],
                            y_bf[ypar][:, tc * 128:(tc + 1) * 128], id_bf[:]),
                            wait=([(dv_f, yn)] + r_pty.waits())
                            if tc == 0 else [],
                            inc=pe_ft if tc == 3 else None)
                    r_ybf[ypar].set(pe_ft)
                    trn = pe_ft.n
                    for tc in range(4):
                        P.step("dv", lambda v, tc=tc, oc=oc: v.tensor_add(
                            x2_buf[:, tc * D + oc * 128:
                                   tc * D + (oc + 1) * 128],
                            x2_buf[:, tc * D + oc * 128:
                                   tc * D + (oc + 1) * 128],
                            pt_y[:, tc * 128:(tc + 1) * 128]),
                            wait=[(pe_ft, trn), (d_x2, x2_here)]
                            if tc == 0 else [],
                            inc=dv_f if tc == 3 else None)
                    r_pty.set(dv_f)
                prev_w2_end = pe_f.n
                add_done = dv_f.n
                for tc in range(4):
                    t0 = win0 + tc * 128
                    P.step("sy", lambda sy, b=b, t0=t0, tc=tc: sy.dma_start(
                        out=y_out[b, t0:t0 + 128, :],
                        in_=x2_buf[:, tc * D:(tc + 1) * D]),
                        wait=[(dv_f, add_done)] if tc == 0 else [],
                        inc=st_out, inck=16)

            P.step("po", lambda g: g.nop(), wait=[st_out])

        with nc.Block() as block:
            P.emit(block)

    return nc


_CACHE = {}


def _get_program():
    if "nc" not in _CACHE:
        _CACHE["nc"] = build_program()
    return _CACHE["nc"]


def kernel(L0_attn_x, L0_attn_W_DQ, L0_attn_W_UQ, L0_attn_W_DKV, L0_attn_W_UKV,
           L0_attn_W_QR, L0_attn_W_KR, L0_attn_W_O, L0_attn_q_norm_w,
           L0_attn_k_norm_w, L0_ffn_w1, L0_ffn_b1, L0_ffn_w2, L0_ffn_b2):
    x = np.ascontiguousarray(np.asarray(L0_attn_x, dtype=np.float32))
    n_cores = 8
    Bs = x.shape[0] // n_cores
    shards = [np.ascontiguousarray(x[i * Bs:(i + 1) * Bs])
              for i in range(n_cores)]
    w = {
        "WDQ": L0_attn_W_DQ, "WUQ": L0_attn_W_UQ, "WDKV": L0_attn_W_DKV,
        "WUKV": L0_attn_W_UKV, "WQR": L0_attn_W_QR, "WKR": L0_attn_W_KR,
        "WO": L0_attn_W_O, "QNW": L0_attn_q_norm_w, "KNW": L0_attn_k_norm_w,
        "W1": L0_ffn_w1, "B1": L0_ffn_b1, "W2": L0_ffn_w2, "B2": L0_ffn_b2,
    }
    w = {k: np.ascontiguousarray(np.asarray(v, dtype=np.float32))
         for k, v in w.items()}
    nc = _get_program()
    in_maps = [dict(x=shards[i], **w) for i in range(n_cores)]
    res = run_bass_kernel_spmd(nc, in_maps, core_ids=list(range(n_cores)))
    return np.concatenate([r["y"] for r in res.results], axis=0)


# version 33 (final)
# speedup vs baseline: 1.0315x; 1.0037x over previous
import numpy as np
from collections import deque

import concourse.bass as bass
import concourse.mybir as mybir
from concourse.bass_utils import run_bass_kernel_spmd
from contextlib import ExitStack

F32 = mybir.dt.float32
BF16 = mybir.dt.bfloat16
AF = mybir.ActivationFunctionType
ALU = mybir.AluOpType
AX = mybir.AxisListType

D = 1024
HD = 64
DL = 512
NFC = D // 128
LN_EPS = 1e-5
RMS_EPS = 1.1920929e-07
NEG_BIG = -1e30


class Sem:
    __slots__ = ("h", "n", "name")

    def __init__(self, nc, name):
        self.h = nc.alloc_semaphore(name)
        self.n = 0
        self.name = name


class Rel:
    __slots__ = ("w",)

    def __init__(self):
        self.w = []

    def set(self, *sems):
        self.w = [(s, s.n) for s in sems]

    def add(self, *sems):
        self.w += [(s, s.n) for s in sems]

    def waits(self):
        return list(self.w)


class Prog:
    ENGS = ("pe", "dv", "ac", "po", "sy")

    def __init__(self, nc):
        self.nc = nc
        self.ops = {e: [] for e in self.ENGS}

    def step(self, eng, fn, wait=(), inc=None, inck=1):
        ws = []
        for w in wait:
            if isinstance(w, Sem):
                ws.append((w, w.n))
            else:
                ws.append((w[0], w[1]))
        self.ops[eng].append((fn, ws, inc, inck))
        if inc is not None:
            inc.n += inck

    def emit(self, block):
        mapping = dict(pe=block.tensor, dv=block.vector, ac=block.scalar,
                       po=block.gpsimd, sy=block.sync)
        for eng in self.ENGS:
            ops = self.ops[eng]

            def body(E, ops=ops):
                last = {}
                for fn, ws, inc, inck in ops:
                    for s, cnt in ws:
                        if cnt > 0 and last.get(s.name, 0) < cnt:
                            E.wait_ge(s.h, cnt)
                            last[s.name] = cnt
                    instr = fn(E)
                    if inc is not None:
                        instr.then_inc(inc.h, inck)

            mapping[eng](body)


def build_program(B=4, T=2048, FF=4096, H=16, skip_attn=False, skip_ffn=False, skip_proj=False):
    assert T % 512 == 0 and FF % 256 == 0 and H % 2 == 0 and H <= 16
    NB = T // 512
    TC = T // 128
    HHALF = min(8, H)
    NGRP = (H + HHALF - 1) // HHALF
    GW = HHALF * HD
    NQT = 4
    NHC = FF // 128
    NOC = D // 128

    nc = bass.Bass("TRN2", target_bir_lowering=False)

    x_in = nc.dram_tensor("x", [B, T, D], F32, kind="ExternalInput").ap()
    wdq_d = nc.dram_tensor("WDQ", [DL, HD], F32, kind="ExternalInput").ap()
    wuq_d = nc.dram_tensor("WUQ", [HD, DL], F32, kind="ExternalInput").ap()
    wdkv_d = nc.dram_tensor("WDKV", [2 * DL, HD], F32, kind="ExternalInput").ap()
    wukv_d = nc.dram_tensor("WUKV", [2 * HD, 2 * DL], F32, kind="ExternalInput").ap()
    wqr_d = nc.dram_tensor("WQR", [HD, HD], F32, kind="ExternalInput").ap()
    wkr_d = nc.dram_tensor("WKR", [HD, HD], F32, kind="ExternalInput").ap()
    wo_d = nc.dram_tensor("WO", [HD, HD], F32, kind="ExternalInput").ap()
    qw_d = nc.dram_tensor("QNW", [HD], F32, kind="ExternalInput").ap()
    kw_d = nc.dram_tensor("KNW", [HD], F32, kind="ExternalInput").ap()
    w1_d = nc.dram_tensor("W1", [D, FF], F32, kind="ExternalInput").ap()
    b1_d = nc.dram_tensor("B1", [FF], F32, kind="ExternalInput").ap()
    w2_d = nc.dram_tensor("W2", [FF, D], F32, kind="ExternalInput").ap()
    b2_d = nc.dram_tensor("B2", [D], F32, kind="ExternalInput").ap()
    y_out = nc.dram_tensor("y", [B, T, D], F32, kind="ExternalOutput").ap()
    x2s = nc.dram_tensor("x2s", [B * T, D], F32, kind="Internal").ap()

    es = ExitStack()
    with es:
        P = Prog(nc)

        def sb(name, shape, dt=F32):
            return es.enter_context(nc.sbuf_tensor(name, shape, dt))

        id_bf = sb("id_bf", [128, 128], BF16)
        negmask = sb("negmask", [128, 128], F32)
        c0125 = sb("c0125", [128, 1], F32)
        c_lneps = sb("c_lneps", [128, 1], F32)
        c_rmseps = sb("c_rmseps", [128, 1], F32)
        c_nhalf = sb("c_nhalf", [128, 1], F32)
        c_inv64 = sb("c_inv64", [128, 1], F32)

        wqT2 = sb("wqT2", [128, HD], BF16)
        wkT2 = sb("wkT2", [128, HD], BF16)
        wvT2 = sb("wvT2", [128, HD], BF16)
        wqrT = sb("wqrT", [128, HD], BF16)
        wkrT = sb("wkrT", [128, HD], BF16)
        woT_pair = sb("woT_pair", [128, 128], BF16)
        qnw_col = sb("qnw_col", [HD, 1], F32)
        knw_col = sb("knw_col", [HD, 1], F32)
        b1col = sb("b1col", [128, NHC], F32)
        b2col = sb("b2col", [128, NOC], F32)

        x2_buf = sb("x2_buf", [128, 4 * D], F32)
        xln = [sb(f"xln{i}", [128, D], BF16) for i in range(2)]
        xn_bf2 = [sb(f"xn_bf{i}", [128, D], BF16) for i in range(2)]
        musum = sb("musum", [128, 1], F32)
        negmu = sb("negmu", [128, 1], F32)
        sqsum = sb("sqsum", [128, 1], F32)
        mu2 = sb("mu2", [128, 1], F32)
        varb = sb("varb", [128, 1], F32)
        lnv = sb("lnv", [128, 1], F32)
        rstd = sb("rstd", [128, 1], F32)

        ps_s0 = es.enter_context(nc.psum_tensor("ps_s0", [128, 512], F32))
        ps_s1 = es.enter_context(nc.psum_tensor("ps_s1", [128, 512], F32))
        ps_a = es.enter_context(nc.psum_tensor("ps_a", [128, 512], F32))
        ps_b = es.enter_context(nc.psum_tensor("ps_b", [128, 512], F32))
        ps_c = es.enter_context(nc.psum_tensor("ps_c", [128, 512], F32))
        ps_d = es.enter_context(nc.psum_tensor("ps_d", [128, 512], F32))
        pt_x = es.enter_context(nc.psum_tensor("pt_x", [128, 1024], BF16))
        pt_y = es.enter_context(nc.psum_tensor("pt_y", [128, 1024], BF16))

        d_w = Sem(nc, "d_w")
        d_x2 = Sem(nc, "d_x2")
        d_xl = [Sem(nc, f"d_xl{i}") for i in range(2)]
        d_g = Sem(nc, "d_g")
        po_set = Sem(nc, "po_set")
        po_chain = Sem(nc, "po_chain")
        pe_su = Sem(nc, "pe_su")
        dv_su = Sem(nc, "dv_su")
        pe_ln = Sem(nc, "pe_ln")
        dv_ln = Sem(nc, "dv_ln")
        ac_ln = Sem(nc, "ac_ln")
        pe_pr = Sem(nc, "pe_pr")
        ac_pr = Sem(nc, "ac_pr")
        dv_pr = Sem(nc, "dv_pr")
        pe_s = Sem(nc, "pe_s")
        dv_st = Sem(nc, "dv_st")
        ac_st = Sem(nc, "ac_st")
        po_st = Sem(nc, "po_st")
        ac_ex = Sem(nc, "ac_ex")
        pe_pt = Sem(nc, "pe_pt")
        dv_pt = Sem(nc, "dv_pt")
        ac_pt = Sem(nc, "ac_pt")
        pe_cx = Sem(nc, "pe_cx")
        dv_ao = Sem(nc, "dv_ao")
        pe_ao = Sem(nc, "pe_ao")
        dv_res = Sem(nc, "dv_res")
        f_x2 = Sem(nc, "f_x2")
        f_x2e = Sem(nc, "f_x2e")
        f_xl = [Sem(nc, f"f_xl{i}") for i in range(2)]
        pe_f = Sem(nc, "pe_f")
        ac_f = Sem(nc, "ac_f")
        dv_f = Sem(nc, "dv_f")
        po_f = Sem(nc, "po_f")
        pe_ft = Sem(nc, "pe_ft")
        st_out = Sem(nc, "st_out")

        r_ptx, r_pty = Rel(), Rel()
        r_xnbf2 = [Rel(), Rel()]
        r_psa, r_psb, r_psc, r_psd = Rel(), Rel(), Rel(), Rel()
        r_ps0, r_ps1 = Rel(), Rel()
        r_xnbf = Rel()
        r_S = [Rel(), Rel()]
        r_PA, r_PB = Rel(), Rel()
        r_PT = [Rel(), Rel()]

        r_win = Rel()
        r_kv = Rel()
        r_lbuf = Rel()

        P.step("po", lambda g: g.memset(id_bf[:], 0.0), inc=po_chain)
        P.step("po", lambda g: g.affine_select(
            out=id_bf[:], in_=id_bf[:], compare_op=ALU.not_equal, fill=1.0,
            base=0, pattern=[[-1, 128]], channel_multiplier=1),
            wait=[po_chain], inc=po_set)
        P.step("po", lambda g: g.memset(negmask[:], 0.0), inc=po_chain)
        P.step("po", lambda g: g.affine_select(
            out=negmask[:], in_=negmask[:], compare_op=ALU.is_ge, fill=NEG_BIG,
            base=0, pattern=[[-1, 128]], channel_multiplier=1),
            wait=[po_chain], inc=po_set)
        for buf, val in ((c0125, 0.125), (c_lneps, LN_EPS), (c_rmseps, RMS_EPS),
                         (c_nhalf, -0.5), (c_inv64, 1.0 / 64.0)):
            P.step("po", lambda g, buf=buf, val=val: g.memset(buf[:], val),
                   inc=po_set)
        P.step("po", lambda g: g.memset(woT_pair[:], 0.0), inc=po_set)
        POSET = po_set.n

        with ExitStack() as su:
            def tb(name, shape, dt=F32):
                return su.enter_context(nc.sbuf_tensor(name, shape, dt))

            id_f = tb("id_f", [128, 128])
            P.step("po", lambda g: g.memset(id_f[:], 0.0), inc=po_chain)
            P.step("po", lambda g: g.affine_select(
                out=id_f[:], in_=id_f[:], compare_op=ALU.not_equal, fill=1.0,
                base=0, pattern=[[-1, 128]], channel_multiplier=1),
                wait=[po_chain], inc=po_set)
            wdq_s = tb("wdq_s", [128, 4 * HD])
            wuq_s = tb("wuq_s", [HD, DL])
            wdkv_s = tb("wdkv_s", [128, 8 * HD])
            wukv_s = tb("wukv_s", [2 * HD, 2 * DL])
            wqr_s = tb("wqr_s", [HD, HD])
            wkr_s = tb("wkr_s", [HD, HD])
            wo_s = tb("wo_s", [HD, HD])
            wuqT_s = tb("wuqT_s", [128, 4 * HD])
            wukvT_s = tb("wukvT_s", [128, 8 * 2 * HD])
            b1_raw = tb("b1_raw", [NHC, 128])
            b2_raw = tb("b2_raw", [NOC, 128])

            def wload(dst, src):
                P.step("sy", lambda s, d=dst, sr=src: s.dma_start(out=d, in_=sr),
                       inc=d_w, inck=16)

            wload(wdq_s[:].rearrange("p (c f) -> p c f", f=HD),
                  wdq_d.rearrange("(c p) f -> p c f", p=128))
            wload(wuq_s[:], wuq_d)
            wload(wdkv_s[:].rearrange("p (c f) -> p c f", f=HD),
                  wdkv_d.rearrange("(c p) f -> p c f", p=128))
            wload(wukv_s[:], wukv_d)
            wload(wqr_s[:], wqr_d)
            wload(wkr_s[:], wkr_d)
            wload(wo_s[:], wo_d)
            wload(qnw_col[:], qw_d.rearrange("(f o) -> f o", o=1))
            wload(knw_col[:], kw_d.rearrange("(f o) -> f o", o=1))
            wload(b1_raw[:], b1_d.rearrange("(c p) -> c p", p=128))
            wload(b2_raw[:], b2_d.rearrange("(c p) -> c p", p=128))
            WALL = d_w.n

            for c in range(4):
                P.step("pe", lambda t, c=c: t.transpose(
                    ps_a[:, c * HD:(c + 1) * HD],
                    wuq_s[:, c * 128:(c + 1) * 128], id_f[0:HD, 0:HD]),
                    wait=[(d_w, WALL), po_set],
                    inc=pe_su if c == 3 else None)
            P.step("dv", lambda v: v.tensor_copy(wuqT_s[:], ps_a[:, 0:4 * HD]),
                   wait=[pe_su], inc=dv_su)
            r_psa.set(dv_su)
            for rnd in range(2):
                for c in range(4):
                    cc = rnd * 4 + c
                    P.step("pe", lambda t, cc=cc, c=c: t.transpose(
                        ps_b[:, c * 128:(c + 1) * 128],
                        wukv_s[:, cc * 128:(cc + 1) * 128], id_f[:]),
                        wait=r_psb.waits() if c == 0 else [],
                        inc=pe_su if c == 3 else None)
                P.step("dv", lambda v, rnd=rnd: v.tensor_copy(
                    wukvT_s[:, rnd * 512:(rnd + 1) * 512], ps_b[:, :]),
                    wait=[pe_su], inc=dv_su)
                r_psb.set(dv_su)
            P.step("pe", lambda t: t.transpose(ps_a[0:HD, 0:HD], wqr_s[:],
                                               id_f[0:HD, 0:HD]), wait=r_psa.waits())
            P.step("pe", lambda t: t.transpose(ps_a[0:HD, HD:2 * HD], wkr_s[:],
                                               id_f[0:HD, 0:HD]))
            P.step("pe", lambda t: t.transpose(ps_a[0:HD, 2 * HD:3 * HD],
                                               wo_s[:], id_f[0:HD, 0:HD]), inc=pe_su)
            P.step("dv", lambda v: v.tensor_copy(wqrT[0:HD, :], ps_a[0:HD, 0:HD]),
                   wait=[pe_su])
            P.step("dv", lambda v: v.tensor_copy(wqrT[HD:128, :], ps_a[0:HD, 0:HD]))
            P.step("dv", lambda v: v.tensor_copy(wkrT[0:HD, :], ps_a[0:HD, HD:2 * HD]))
            P.step("dv", lambda v: v.tensor_copy(wkrT[HD:128, :], ps_a[0:HD, HD:2 * HD]))
            P.step("dv", lambda v: v.tensor_copy(
                woT_pair[0:HD, 0:HD], ps_a[0:HD, 2 * HD:3 * HD]))
            P.step("dv", lambda v: v.tensor_copy(
                woT_pair[HD:128, HD:128], ps_a[0:HD, 2 * HD:3 * HD]), inc=dv_su)
            r_psa.set(dv_su)
            P.step("pe", lambda t: t.transpose(ps_b[:, 0:NHC], b1_raw[:],
                                               id_f[0:NHC, 0:NHC]), wait=r_psb.waits())
            P.step("pe", lambda t: t.transpose(ps_b[:, 256:256 + NOC], b2_raw[:],
                                               id_f[0:NOC, 0:NOC]), inc=pe_su)
            P.step("dv", lambda v: v.tensor_copy(b1col[:], ps_b[:, 0:NHC]),
                   wait=[pe_su])
            P.step("dv", lambda v: v.tensor_copy(b2col[:], ps_b[:, 256:256 + NOC]),
                   inc=dv_su)
            r_psb.set(dv_su)

            for c in range(4):
                P.step("pe", lambda t, c=c: t.matmul(
                    ps_c[0:HD, 0:HD],
                    wdq_s[:].rearrange("p (c f) -> p c f", f=HD)[:, c, :],
                    wuqT_s[:, c * HD:(c + 1) * HD],
                    start=(c == 0), stop=(c == 3)),
                    wait=[dv_su] if c == 0 else [],
                    inc=pe_su if c == 3 else None)
            for c in range(8):
                P.step("pe", lambda t, c=c: t.matmul(
                    ps_d[0:HD, 0:2 * HD],
                    wdkv_s[:].rearrange("p (c f) -> p c f", f=HD)[:, c, :],
                    wukvT_s[:, c * 2 * HD:(c + 1) * 2 * HD],
                    start=(c == 0), stop=(c == 7)),
                    inc=pe_su if c == 7 else None)
            P.step("dv", lambda v: v.tensor_copy(wqT2[0:HD, :], ps_c[0:HD, 0:HD]),
                   wait=[pe_su])
            P.step("dv", lambda v: v.tensor_copy(wqT2[HD:128, :],
                                                 ps_c[0:HD, 0:HD]))
            P.step("dv", lambda v: v.tensor_copy(wkT2[0:HD, :],
                                                 ps_d[0:HD, 0:HD]))
            P.step("dv", lambda v: v.tensor_copy(wkT2[HD:128, :],
                                                 ps_d[0:HD, 0:HD]))
            P.step("dv", lambda v: v.tensor_copy(wvT2[0:HD, :],
                                                 ps_d[0:HD, HD:2 * HD]))
            P.step("dv", lambda v: v.tensor_copy(wvT2[HD:128, :],
                                                 ps_d[0:HD, HD:2 * HD]),
                   inc=dv_su)
            r_psc.set(dv_su)
            r_psd.set(dv_su)
            SETUP = dv_su.n

        with ExitStack() as att:
            def asb(name, shape, dt=BF16):
                return att.enter_context(nc.sbuf_tensor(name, shape, dt))

            PCAP = 128 * sum(4 * (NB - 1) + qt + 1 for qt in range(NQT))
            Xn1T = asb("Xn1T", [64, H * 512])
            kstack = [asb(f"kst{h}", [128, T]) for h in range(H)]
            qstack = [asb(f"qst{h}", [128, 512]) for h in range(H)]
            vstore = [asb(f"vst{g}", [128, TC * GW]) for g in range(NGRP)]
            qn_tok = asb("qn_tok", [128, GW])
            msq = asb("msq", [128, HHALF], F32)
            ln8 = asb("ln8", [128, HHALF], F32)
            rstd8 = asb("rstd8", [128, HHALF], F32)
            rstdq_win = asb("rstdq_win", [128, 4 * H], F32)
            S2 = [asb(f"S_sb{i}", [128, T], F32) for i in range(2)]
            mcol = asb("mcol", [128, 6], F32)
            mrow = asb("mrow", [128, 1], F32)
            negm8 = asb("negm8", [128, 8], F32)
            P_A = asb("P_A", [128, PCAP])
            P_B = asb("P_B", [128, PCAP])
            PT2 = [asb(f"PT2{par}", [128, 1024]) for par in range(2)]
            l_buf = asb("l_buf", [128, 8], F32)
            rl_buf = asb("rl_buf", [128, 8], F32)
            ctx_bf = asb("ctx_bf", [128, 512])
            attn_bf = ctx_bf
            sqp_bf = asb("sqp_bf", [128, GW])
            r_ctxbf = Rel()
            r_sqp = Rel()
            r_attnbf = r_ctxbf
            r_qntok = Rel()
            r_rstd8, r_msq = Rel(), Rel()

            xnT_marks = {}

            def emit_W1(b_, qb_, thunks=None):
                win0_ = qb_ * 512

                def mk_chain(tc):
                    def chain():
                        t0 = win0_ + tc * 128
                        par = tc % 2
                        P.step("po", lambda g, b=b_, t0=t0, par=par: g.dma_start(
                            out=xln[par][:], in_=x_in[b, t0:t0 + 128, :]),
                            wait=[f_xl[par]], inc=d_xl[par], inck=16)
                        ld = d_xl[par].n
                        P.step("dv", lambda v, par=par: v.reduce_sum(
                            musum[:], xln[par][:], axis=AX.X),
                            wait=[(d_xl[par], ld)], inc=dv_ln)
                        P.step("ac", lambda a, par=par: a.activation(
                            out=xn_bf2[tc % 2][:], in_=xln[par][:], func=AF.Square,
                            accum_out=sqsum[:]),
                            wait=[(d_xl[par], ld), dv_ln]
                            + r_xnbf2[tc % 2].waits(),
                            inc=ac_ln)
                        P.step("dv", lambda v: v.tensor_scalar(
                            out=negmu[:], in0=musum[:], scalar1=-1.0 / D,
                            scalar2=None, op0=ALU.mult), wait=[dv_ln],
                            inc=dv_ln)
                        P.step("dv", lambda v: v.tensor_mul(
                            mu2[:], negmu[:], negmu[:]), wait=[dv_ln],
                            inc=dv_ln)
                        P.step("dv", lambda v: v.tensor_scalar(
                            out=varb[:], in0=sqsum[:], scalar1=1.0 / D,
                            scalar2=None, op0=ALU.mult),
                            wait=[ac_ln, dv_ln], inc=dv_ln)
                        P.step("dv", lambda v: v.tensor_sub(
                            varb[:], varb[:], mu2[:]), wait=[dv_ln],
                            inc=dv_ln)
                        P.step("ac", lambda a: a.activation(
                            out=lnv[:], in_=varb[:], func=AF.Ln,
                            bias=c_lneps[:]),
                            wait=[dv_ln, ac_ln, (po_set, POSET)], inc=ac_ln)
                        P.step("ac", lambda a: a.activation(
                            out=rstd[:], in_=lnv[:], func=AF.Exp,
                            scale=c_nhalf[:]),
                            wait=[ac_ln, f_xl[0], f_xl[1]], inc=ac_ln)
                        P.step("dv", lambda v, par=par, tc=tc: v.tensor_scalar(
                            out=xn_bf2[tc % 2][:], in0=xln[par][:],
                            scalar1=negmu[:],
                            scalar2=rstd[:], op0=ALU.add, op1=ALU.mult),
                            wait=[ac_ln] + r_xnbf2[tc % 2].waits(),
                            inc=f_xl[par])
                    return chain

                def mk_tr(tc):
                    def tr():
                        par = tc % 2
                        xn_done = f_xl[par].n
                        for fc in range(NFC):
                            P.step("pe", lambda t, fc=fc: t.transpose(
                                pt_x[:, fc * 128:(fc + 1) * 128],
                                xn_bf2[tc % 2][:, fc * 128:(fc + 1) * 128],
                                id_bf[:]),
                                wait=([(f_xl[par], xn_done), (po_set, POSET)]
                                      + r_ptx.waits() + [(pe_pr, pe_pr.n)])
                                if fc == 0 else [],
                                inc=pe_ln if fc == NFC - 1 else None)
                        r_xnbf2[tc % 2].set(pe_ln)
                        P.step("dv", lambda v, tc=tc: v.tensor_copy(
                            Xn1T[:].rearrange("p (c f) -> p c f", f=1024)
                            [:, :, tc * 128:(tc + 1) * 128],
                            pt_x[0:64, :].rearrange("p (c f) -> p c f", f=128)
                            [:, 0:H // 2, :]),
                            wait=[pe_ln], inc=dv_ln)
                        P.step("dv", lambda v, tc=tc: v.tensor_copy(
                            Xn1T[:].rearrange("p (c f) -> p c f", f=1024)
                            [:, :, 512 + tc * 128:512 + (tc + 1) * 128],
                            pt_x[64:128, :].rearrange("p (c f) -> p c f",
                                                      f=128)
                            [:, 0:H // 2, :]),
                            inc=dv_ln)
                        r_ptx.set(dv_ln)
                        if tc == 3:
                            xnT_marks[(b_, qb_)] = dv_ln.n
                    return tr

                seq = []
                for tc in range(4):
                    seq.append(mk_chain(tc))
                    seq.append(mk_tr(tc))
                if thunks is None:
                    for f in seq:
                        f()
                else:
                    thunks.extend(seq)

            wo_q = []
            wo_mark = [0]
            for b in range(B):
                for qb in range(NB):
                    win0 = qb * 512

                    for tc in range(4):
                        t0 = win0 + tc * 128
                        P.step("sy", lambda s, b=b, t0=t0, tc=tc: s.dma_start(
                            out=x2_buf[:, tc * D:(tc + 1) * D],
                            in_=x_in[b, t0:t0 + 128, :]),
                            wait=[f_x2] if tc == 0 else [], inc=d_x2, inck=16)
                    x2_ready = d_x2.n

                    rot4 = [(ps_a, r_psa), (ps_b, r_psb),
                            (ps_c, r_psc), (ps_d, r_psd)]
                    uidx = [0]
                    pendu = deque()

                    def w2_stage2(tc, tcg, grp, kind, ps, r_ps, mm_done):
                        if kind == "v":
                            P.step("ac", lambda a, grp=grp, tcg=tcg, ps=ps: a.activation(
                                out=vstore[grp][:, tcg * GW:(tcg + 1) * GW],
                                in_=ps[:, 0:GW], func=AF.Copy),
                                wait=[(pe_pr, mm_done)] + r_kv.waits(),
                                inc=ac_pr)
                            r_ps.set(ac_pr)
                            return

                        if kind == "q":
                            P.step("dv", lambda v, ps=ps: v.tensor_copy(
                                qn_tok[:], ps[:, 0:GW]),
                                wait=[(pe_pr, mm_done)] + r_qntok.waits(),
                                inc=dv_pr)
                            ev_mark = dv_pr.n
                            for hh in range(HHALF):
                                P.step("pe", lambda t, hh=hh: t.transpose(
                                    pt_y[0:HD, hh * 128:(hh + 1) * 128],
                                    qn_tok[:, hh * HD:(hh + 1) * HD],
                                    id_bf[:]),
                                    wait=([(dv_pr, ev_mark)] + r_pty.waits())
                                    if hh == 0 else [],
                                    inc=pe_pr if hh == HHALF - 1 else None)
                            r_qntok.set(pe_pr)
                            tr_done = pe_pr.n
                            for hh in range(HHALF):
                                h = grp * HHALF + hh
                                dst = qstack[h][0:HD, tc * 128:(tc + 1) * 128]
                                P.step("dv", lambda v, hh=hh, dst=dst: v.tensor_scalar(
                                    out=dst,
                                    in0=pt_y[0:HD, hh * 128:(hh + 1) * 128],
                                    scalar1=qnw_col[:], scalar2=None,
                                    op0=ALU.mult),
                                    wait=([(pe_pr, tr_done)] + r_win.waits())
                                    if hh == 0 else [],
                                    inc=dv_pr if hh == HHALF - 1 else None)
                            r_pty.set(dv_pr)
                            P.step("ac", lambda a, ps=ps: a.activation(
                                out=sqp_bf[:], in_=ps[:, 0:GW],
                                func=AF.Square),
                                wait=[(pe_pr, mm_done), (dv_pr, ev_mark)]
                                + r_sqp.waits(),
                                inc=ac_pr)
                            sq_mark = ac_pr.n
                            P.step("dv", lambda v: v.tensor_reduce(
                                out=msq[:],
                                in_=sqp_bf[:].rearrange("p (h f) -> p h f",
                                                        f=HD),
                                axis=AX.X, op=ALU.add),
                                wait=[(ac_pr, sq_mark)] + r_msq.waits(),
                                inc=dv_pr)
                            r_sqp.set(dv_pr)
                            red_mark = dv_pr.n
                            r_ps.set(dv_pr)
                            P.step("ac", lambda a: a.activation(
                                out=ln8[:], in_=msq[:], func=AF.Ln,
                                bias=c_rmseps[:], scale=c_inv64[:]),
                                wait=[(dv_pr, red_mark), ac_pr], inc=ac_pr)
                            r_msq.set(ac_pr)
                            P.step("ac", lambda a: a.activation(
                                out=rstd8[:], in_=ln8[:], func=AF.Exp,
                                scale=c_nhalf[:]),
                                wait=[ac_pr] + r_rstd8.waits(), inc=ac_pr)
                            rstd_done = ac_pr.n
                            c0 = tc * H + grp * HHALF
                            P.step("dv", lambda v, c0=c0: v.tensor_scalar(
                                out=rstdq_win[:, c0:c0 + HHALF],
                                in0=rstd8[:], scalar1=0.125, scalar2=None,
                                op0=ALU.mult),
                                wait=[(ac_pr, rstd_done)], inc=dv_pr)
                            r_rstd8.set(dv_pr)
                            return

                        P.step("ac", lambda a, ps=ps: a.activation(
                            out=sqp_bf[:], in_=ps[:, 0:GW],
                            func=AF.Square),
                            wait=[(pe_pr, mm_done)] + r_sqp.waits(),
                            inc=ac_pr)
                        sq_mark = ac_pr.n
                        P.step("dv", lambda v: v.tensor_reduce(
                            out=msq[:],
                            in_=sqp_bf[:].rearrange(
                                "p (h f) -> p h f", f=HD),
                            axis=AX.X, op=ALU.add),
                            wait=[(ac_pr, sq_mark)] + r_msq.waits(),
                            inc=dv_pr)
                        r_sqp.set(dv_pr)
                        red_mark = dv_pr.n
                        P.step("ac", lambda a: a.activation(
                            out=ln8[:], in_=msq[:], func=AF.Ln,
                            bias=c_rmseps[:], scale=c_inv64[:]),
                            wait=[(dv_pr, red_mark), ac_pr], inc=ac_pr)
                        r_msq.set(ac_pr)
                        P.step("ac", lambda a: a.activation(
                            out=rstd8[:], in_=ln8[:], func=AF.Exp,
                            scale=c_nhalf[:]),
                            wait=[ac_pr] + r_rstd8.waits(), inc=ac_pr)
                        rstd_done = ac_pr.n
                        for hh in range(HHALF):
                            P.step("dv", lambda v, hh=hh, ps=ps: v.tensor_scalar(
                                out=qn_tok[:, hh * HD:(hh + 1) * HD],
                                in0=ps[:, hh * HD:(hh + 1) * HD],
                                scalar1=rstd8[:, hh:hh + 1],
                                scalar2=None, op0=ALU.mult),
                                wait=([(ac_pr, rstd_done)]
                                      + r_qntok.waits())
                                if hh == 0 else [],
                                inc=dv_pr if hh == HHALF - 1 else None)
                        r_rstd8.set(dv_pr)
                        r_ps.set(dv_pr)
                        r_ps.add(ac_pr)
                        appl = dv_pr.n
                        for hh in range(HHALF):
                            P.step("pe", lambda t, hh=hh: t.transpose(
                                pt_y[0:HD, hh * 128:(hh + 1) * 128],
                                qn_tok[:, hh * HD:(hh + 1) * HD],
                                id_bf[:]),
                                wait=([(dv_pr, appl)] + r_pty.waits())
                                if hh == 0 else [],
                                inc=pe_pr if hh == HHALF - 1 else None)
                        r_qntok.set(pe_pr)
                        tr_done = pe_pr.n
                        for hh in range(HHALF):
                            h = grp * HHALF + hh
                            dst = kstack[h][0:HD, tcg * 128:(tcg + 1) * 128]
                            P.step("dv", lambda v, hh=hh, dst=dst: v.tensor_scalar(
                                out=dst,
                                in0=pt_y[0:HD, hh * 128:(hh + 1) * 128],
                                scalar1=knw_col[:], scalar2=None,
                                op0=ALU.mult),
                                wait=([(pe_pr, tr_done)] + r_kv.waits())
                                if hh == 0 else [],
                                inc=dv_pr if hh == HHALF - 1 else None)
                        r_pty.set(dv_pr)
                    if (b, qb) not in xnT_marks:
                        emit_W1(b, qb)
                    xnT_done = xnT_marks[(b, qb)]
                    for tc in range(4):
                        t0 = win0 + tc * 128
                        tcg = t0 // 128
                        for grp in range(0 if skip_proj else NGRP):
                            for kind in ("q", "k", "v"):
                                ps, r_ps = rot4[uidx[0] % 4]
                                uidx[0] += 1
                                wT = {"q": wqT2, "k": wkT2, "v": wvT2}[kind]
                                for hh in range(HHALF):
                                    h = grp * HHALF + hh
                                    P.step("pe", lambda t, h=h, hh=hh, wT=wT, tc=tc, ps=ps: t.matmul(
                                        ps[:, hh * HD:(hh + 1) * HD],
                                        Xn1T[0:HD, h * 512 + tc * 128:
                                             h * 512 + (tc + 1) * 128],
                                        wT[0:HD, :], start=True,
                                        stop=True),
                                        wait=([(dv_ln, xnT_done),
                                               (dv_su, SETUP)] + r_ps.waits())
                                        if hh == 0 else [],
                                        inc=pe_pr if hh == HHALF - 1 else None)
                                pendu.append((tc, tcg, grp, kind, ps, r_ps,
                                              pe_pr.n))
                                if len(pendu) >= 3:
                                    w2_stage2(*pendu.popleft())

                    while pendu:
                        w2_stage2(*pendu.popleft())
                    stacks_done = dv_pr.n
                    pingpong = [(ps_a, r_psa), (ps_b, r_psb)]
                    pidx = 0
                    for kind in (() if skip_proj else ("qr", "kr")):
                        wT = wqrT if kind == "qr" else wkrT
                        for h in range(H):
                            ps, r_ps = pingpong[pidx]
                            pidx ^= 1
                            if kind == "qr":
                                rhs = qstack[h][0:HD, 0:512]
                            else:
                                rhs = Xn1T[0:HD, h * 512:(h + 1) * 512]
                            P.step("pe", lambda t, wT=wT, rhs=rhs, ps=ps: t.matmul(
                                ps[0:HD, :], wT[0:HD, :], rhs,
                                start=True, stop=True),
                                wait=[(dv_pr, stacks_done)] + r_ps.waits(),
                                inc=pe_pr)
                            mmn = pe_pr.n
                            if kind == "qr":
                                dst = qstack[h][HD:128, 0:512]
                            else:
                                dst = kstack[h][HD:128, win0:win0 + 512]
                            P.step("ac", lambda a, dst=dst, ps=ps: a.activation(
                                out=dst, in_=ps[0:HD, :], func=AF.Copy),
                                wait=[(pe_pr, mmn)], inc=ac_pr)
                            r_ps.set(ac_pr)
                    win_ready = dv_pr.n
                    win_ready_ac = ac_pr.n

                    if skip_attn:
                        P.step("dv", lambda v: v.tensor_copy(
                            mrow[:], musum[:]), wait=[(d_x2, x2_ready)],
                            inc=dv_res)
                        sp = dv_res.n
                        for tc in range(4):
                            row0 = b * T + win0 + tc * 128
                            P.step("po", lambda g, row0=row0, tc=tc: g.dma_start(
                                out=x2s[row0:row0 + 128, :],
                                in_=x2_buf[:, tc * D:(tc + 1) * D]),
                                wait=[(dv_res, sp)] if tc == 0 else [],
                                inc=f_x2, inck=16)
                        r_win.set(pe_s, pe_pr)
                        continue
                    poffs = [0]
                    for qt in range(NQT):
                        poffs.append(poffs[-1] + 128 * (4 * qb + qt + 1))

                    stats_mark = {}
                    negm_free = {}
                    sring = [0]

                    def emit_slot_stats(p, hi, qt):
                        h = 2 * p + hi
                        slot = hi * 4 + qt
                        spar = slot % 2
                        S_sb = S2[spar]
                        Tq = 4 * qb + qt
                        klen = 128 * (Tq + 1)
                        offs = list(range(0, klen, 512))
                        nchunks = 0
                        sping = [(ps_s0, r_ps0), (ps_s1, r_ps1),
                                 (ps_a, r_psa), (ps_b, r_psb)]
                        first_stat = True
                        for ci, off in enumerate(offs):
                            w = min(512, klen - off)
                            ps, r_ps = sping[(sring[0] + ci) % 4]
                            has_diag = (off + w == klen)
                            P.step("pe", lambda t, h=h, qt=qt, off=off, w=w, ps=ps: t.matmul(
                                ps[:, 0:w],
                                qstack[h][:, qt * 128:(qt + 1) * 128],
                                kstack[h][:, off:off + w],
                                start=True, stop=True),
                                wait=[(dv_pr, win_ready),
                                      (ac_pr, win_ready_ac)] + r_ps.waits(),
                                inc=pe_s)
                            mmn = pe_s.n
                            wmain = w - 128 if has_diag else w
                            sg = r_S[spar].waits()
                            first_stat = False
                            if wmain > 0:
                                P.step("dv", lambda v, off=off, wmain=wmain, ps=ps, nchunks=nchunks, S_sb=S_sb: v.tensor_scalar(
                                    out=S_sb[:, off:off + wmain],
                                    in0=ps[:, 0:wmain], scalar1=1.0,
                                    scalar2=None, op0=ALU.mult,
                                    op1=ALU.max,
                                    accum_out=mcol[:, nchunks:nchunks + 1]),
                                    wait=[(pe_s, mmn)] + sg, inc=dv_st)
                                r_ps.set(dv_st)
                                nchunks += 1
                            if has_diag:
                                P.step("dv", lambda v, off=off, w=w, wmain=wmain, ps=ps, S_sb=S_sb: v.scalar_tensor_tensor(
                                    out=S_sb[:, off + wmain:off + w],
                                    in0=ps[:, wmain:w], scalar=1.0,
                                    in1=negmask[:], op0=ALU.mult,
                                    op1=ALU.add),
                                    wait=[(pe_s, mmn), (po_set, POSET)] + sg,
                                    inc=dv_st)
                                P.step("dv", lambda v, off=off, w=w, wmain=wmain, nchunks=nchunks, S_sb=S_sb: v.reduce_max(
                                    out=mcol[:, nchunks:nchunks + 1],
                                    in_=S_sb[:, off + wmain:off + w],
                                    axis=AX.X),
                                    wait=[dv_st], inc=dv_st)
                                nchunks += 1
                                r_ps.set(dv_st)
                                r_ps.add(ac_st)
                        sring[0] = (sring[0] + len(offs)) % 4
                        P.step("dv", lambda v, nchunks=nchunks: v.reduce_max(
                            out=mrow[:], in_=mcol[:, 0:nchunks], axis=AX.X),
                            wait=[dv_st], inc=dv_st)
                        rc = qt * H + h
                        P.step("dv", lambda v, slot=slot, rc=rc: v.tensor_scalar(
                            out=negm8[:, slot:slot + 1], in0=mrow[:],
                            scalar1=rstdq_win[:, rc:rc + 1], scalar2=-1.0,
                            op0=ALU.mult, op1=ALU.mult),
                            wait=[dv_st, (dv_pr, win_ready)]
                            + negm_free.get(slot, []),
                            inc=dv_st)
                        stats_mark[(p, slot)] = dv_st.n

                    def emit_slot_exp(p, hi, qt):
                        slot = hi * 4 + qt
                        spar = slot % 2
                        P_sb = (P_A, P_B)[hi]
                        r_P = (r_PA, r_PB)[hi]
                        klen = 128 * (4 * qb + qt + 1)
                        po0 = poffs[qt]
                        rc = qt * H + 2 * p + hi
                        P.step("ac", lambda a, P_sb=P_sb, po0=po0, klen=klen, slot=slot, spar=spar, rc=rc: a.activation(
                            out=P_sb[:, po0:po0 + klen],
                            in_=S2[spar][:, 0:klen], func=AF.Exp,
                            scale=rstdq_win[:, rc:rc + 1],
                            bias=negm8[:, slot:slot + 1],
                            accum_out=l_buf[:, slot:slot + 1]),
                            wait=[(dv_st, stats_mark[(p, slot)])]
                            + r_P.waits()
                            + (r_lbuf.waits() if slot == 0 else []),
                            inc=ac_ex)
                        negm_free[slot] = [(ac_ex, ac_ex.n)]
                        r_S[spar].set(ac_ex)

                    def emit_wo(p, rl_done):
                        P.step("pe", lambda t: t.matmul(
                            ps_d[:, :], woT_pair[:], ctx_bf[:], start=True,
                            stop=True),
                            wait=[dv_ao, (dv_su, SETUP)] + r_psd.waits(),
                            inc=pe_ao)
                        r_ctxbf.set(pe_ao)
                        P.step("ac", lambda a: a.activation(
                            out=attn_bf[:], in_=ps_d[:, :], func=AF.Copy),
                            wait=[pe_ao] + r_attnbf.waits(), inc=ac_ex)
                        r_psd.set(ac_ex)
                        ab_done = ac_ex.n
                        for qt in range(NQT):
                            P.step("pe", lambda t, qt=qt: t.transpose(
                                pt_x[:, qt * 128:(qt + 1) * 128],
                                attn_bf[:, qt * 128:(qt + 1) * 128],
                                id_bf[:]),
                                wait=([(ac_ex, ab_done)] + r_ptx.waits())
                                if qt == 0 else [],
                                inc=pe_ao if qt == NQT - 1 else None)
                        r_attnbf.set(pe_ao)
                        trn = pe_ao.n
                        for qt in range(NQT):
                            for hi in range(2):
                                slot = hi * 4 + qt
                                col0 = qt * D + p * 128 + hi * HD
                                P.step("dv", lambda v, qt=qt, hi=hi, slot=slot, col0=col0: v.scalar_tensor_tensor(
                                    out=x2_buf[:, col0:col0 + HD],
                                    in0=pt_x[:, qt * 128 + hi * HD:
                                             qt * 128 + (hi + 1) * HD],
                                    scalar=rl_buf[:, slot:slot + 1],
                                    in1=x2_buf[:, col0:col0 + HD],
                                    op0=ALU.mult, op1=ALU.add),
                                    wait=([(pe_ao, trn), (dv_res, rl_done),
                                           (d_x2, x2_ready)]
                                          if (qt == 0 and hi == 0) else []),
                                    inc=(dv_res
                                         if (qt == NQT - 1 and hi == 1)
                                         else None))
                        r_ptx.set(dv_res)
                        return dv_res.n

                    def emit_pair_ctx(p):
                        if wo_q:
                            wo_mark[0] = wo_q.pop(0)()
                        if w1q:
                            w1q.pop(0)()
                        P.step("dv", lambda v: v.reciprocal(rl_buf[:],
                                                            l_buf[:]),
                               wait=[ac_ex, (dv_res, wo_mark[0])],
                               inc=dv_res)
                        r_lbuf.set(dv_res)
                        rl_done = dv_res.n
                        nj = 4 * qb + 4
                        ptb = (pt_x, pt_y)
                        rptb = (r_ptx, r_pty)
                        for j in range(nj):
                            par = j % 2
                            qt_min = max(0, j - 4 * qb)
                            for hi in range(2):
                                P_sb = (P_A, P_B)[hi]
                                for qt in range(qt_min, NQT):
                                    src0 = poffs[qt] + j * 128
                                    P.step("pe", lambda t, P_sb=P_sb, hi=hi, qt=qt, src0=src0, par=par: t.transpose(
                                        ptb[par][:, hi * 512 + qt * 128:
                                                 hi * 512 + (qt + 1) * 128],
                                        P_sb[:, src0:src0 + 128], id_bf[:]),
                                        wait=([ac_ex] + rptb[par].waits())
                                        if (hi == 0 and qt == qt_min) else [],
                                        inc=pe_pt
                                        if (hi == 1 and qt == NQT - 1)
                                        else None)
                            trn = pe_pt.n
                            P.step("dv", lambda v, par=par, qt_min=qt_min: v.tensor_copy(
                                PT2[par][:].rearrange("p (h f) -> p h f",
                                                      f=512)
                                [:, :, qt_min * 128:512],
                                ptb[par][:].rearrange("p (h f) -> p h f",
                                                      f=512)
                                [:, :, qt_min * 128:512]),
                                wait=[(pe_pt, trn)] + r_PT[par].waits(),
                                inc=dv_pt)
                            ev_w = (dv_pt, dv_pt.n)
                            rptb[par].set(dv_pt)
                            for hi in range(2):
                                h = 2 * p + hi
                                grp = h // HHALF
                                hh = h % HHALF
                                v0 = j * GW + hh * HD
                                P.step("pe", lambda t, hi=hi, grp=grp, v0=v0, par=par, qt_min=qt_min, j=j, nj=nj: t.matmul(
                                    ps_c[hi * HD:(hi + 1) * HD,
                                         qt_min * 128:512],
                                    vstore[grp][:, v0:v0 + HD],
                                    PT2[par][:, hi * 512 + qt_min * 128:
                                             hi * 512 + 512],
                                    start=(j == 0), stop=(j == nj - 1),
                                    tile_position=(0, hi * HD),
                                    skip_group_check=True),
                                    wait=([ev_w,
                                           (dv_pr, win_ready),
                                           (ac_pr, win_ready_ac)]
                                          + (r_psc.waits() if j == 0 else []))
                                    if hi == 0 else [],
                                    inc=pe_cx if hi == 1 else None)
                            r_PT[par].set(pe_cx)
                        r_PA.set(pe_pt)
                        r_PB.set(pe_pt)

                        P.step("dv", lambda v: v.tensor_copy(ctx_bf[:],
                                                             ps_c[:, :]),
                               wait=[pe_cx] + r_ctxbf.waits(), inc=dv_ao)
                        r_psc.set(dv_ao)
                        wo_q.append(lambda p=p, rl_done=rl_done:
                                    emit_wo(p, rl_done))

                    pend = deque()
                    w1q = []
                    slot_ctr = [0]
                    for p in range(H // 2):
                        if p == 1:
                            nb, nqb = (b, qb + 1) if qb + 1 < NB else \
                                      (b + 1, 0)
                            if nb < B:
                                emit_W1(nb, nqb, thunks=w1q)
                        for hi in range(2):
                            for qt in range(NQT):
                                slot_ctr[0] += 1
                                if len(pend) >= 2:
                                    g = pend.popleft()
                                    emit_slot_exp(*g)
                                    if g[1] == 1 and g[2] == NQT - 1:
                                        emit_pair_ctx(g[0])
                                emit_slot_stats(p, hi, qt)
                                pend.append((p, hi, qt))
                    while pend:
                        g = pend.popleft()
                        emit_slot_exp(*g)
                        if g[1] == 1 and g[2] == NQT - 1:
                            emit_pair_ctx(g[0])
                    while w1q:
                        w1q.pop(0)()
                    while wo_q:
                        wo_mark[0] = wo_q.pop(0)()
                    sp = dv_res.n
                    for tc in range(4):
                        row0 = b * T + win0 + tc * 128
                        P.step("po", lambda g, row0=row0, tc=tc: g.dma_start(
                            out=x2s[row0:row0 + 128, :],
                            in_=x2_buf[:, tc * D:(tc + 1) * D]),
                            wait=[(dv_res, sp)] if tc == 0 else [],
                            inc=f_x2, inck=16)
                    r_win.set(pe_s, pe_pr)
                r_kv.set(pe_s, pe_cx)

        with ExitStack() as ffn:
            def fsb(name, shape, dt=BF16):
                return ffn.enter_context(nc.sbuf_tensor(name, shape, dt))

            w1b = fsb("w1b", [128, NFC * FF])
            w2b = fsb("w2b", [128, NHC * D])
            xn2 = fsb("xn2", [128, D])
            nmr = ffn.enter_context(nc.sbuf_tensor("nmr", [128, 1], F32))
            Xn2T2 = [fsb(f"Xn2T{i}", [128, NFC * 512]) for i in range(2)]
            r_Xn2Tb = [Rel(), Rel()]
            hT = fsb("hT", [128, NHC * 512])
            y_bf = [fsb(f"ybf{i}", [128, 512]) for i in range(2)]
            r_xn2, r_Xn2T = Rel(), Rel()
            r_ybf = [Rel(), Rel()]

            SPILLS = f_x2.n
            LN1_DV = dv_ln.n

            prev_w2_end = 0
            ln2_marks = {}
            st_guard = [0]

            def emit_ln2(s_):
                b_, qb_ = divmod(s_, NB)
                win0_ = qb_ * 512
                lds = []

                def emit_load(tc):
                    row0 = b_ * T + win0_ + tc * 128
                    par = tc % 2
                    P.step("po", lambda g, row0=row0, par=par: g.dma_start(
                        out=xln[par][:], in_=x2s[row0:row0 + 128, :]),
                        wait=[f_xl[par], (f_x2, SPILLS)],
                        inc=d_xl[par], inck=16)
                    lds.append(d_xl[par].n)

                def emit_chain(tc):
                    par = tc % 2
                    ld = lds[tc]
                    P.step("ac", lambda a, par=par: a.activation(
                        out=xn2[:], in_=xln[par][:], func=AF.Copy,
                        accum_out=musum[:]),
                        wait=[(d_xl[par], ld), (po_f, st_guard[0]),
                              (dv_ln, LN1_DV)] + r_xn2.waits(),
                        inc=ac_f)
                    cp_mark = ac_f.n
                    P.step("ac", lambda a, par=par: a.activation(
                        out=xn2[:], in_=xln[par][:],
                        func=AF.Square, accum_out=sqsum[:]),
                        wait=[(ac_f, cp_mark)], inc=ac_f)
                    cs_mark = ac_f.n
                    P.step("po", lambda g: g.tensor_scalar(
                        out=negmu[:], in0=musum[:], scalar1=-1.0 / D,
                        scalar2=None, op0=ALU.mult),
                        wait=[(ac_f, cs_mark), (dv_ln, LN1_DV), po_f],
                        inc=po_f)
                    P.step("po", lambda g: g.tensor_tensor(
                        out=mu2[:], in0=negmu[:], in1=negmu[:],
                        op=ALU.mult), wait=[po_f], inc=po_f)
                    P.step("po", lambda g: g.tensor_scalar(
                        out=varb[:], in0=sqsum[:], scalar1=1.0 / D,
                        scalar2=None, op0=ALU.mult), wait=[po_f], inc=po_f)
                    P.step("po", lambda g: g.tensor_tensor(
                        out=varb[:], in0=varb[:], in1=mu2[:],
                        op=ALU.subtract), wait=[po_f], inc=po_f)
                    vb_mark = po_f.n
                    P.step("ac", lambda a: a.activation(
                        out=lnv[:], in_=varb[:], func=AF.Ln,
                        bias=c_lneps[:]),
                        wait=[(po_f, vb_mark), ac_f], inc=ac_f)
                    P.step("ac", lambda a: a.activation(
                        out=rstd[:], in_=lnv[:], func=AF.Exp,
                        scale=c_nhalf[:]), wait=[ac_f], inc=ac_f)
                    ex_mark = ac_f.n
                    P.step("po", lambda g: g.tensor_tensor(
                        out=nmr[:], in0=negmu[:], in1=rstd[:],
                        op=ALU.mult),
                        wait=[(ac_f, ex_mark), po_f], inc=po_f)
                    st_guard[0] = po_f.n
                    P.step("ac", lambda a, par=par: a.activation(
                        out=xn2[:], in_=xln[par][:], func=AF.Identity,
                        scale=rstd[:], bias=nmr[:]),
                        wait=[(po_f, st_guard[0]), ac_f] + r_xn2.waits(),
                        inc=f_xl[par])
                    xnd = f_xl[par].n
                    for fc in range(NFC):
                        P.step("pe", lambda t, fc=fc: t.transpose(
                            pt_x[:, fc * 128:(fc + 1) * 128],
                            xn2[:, fc * 128:(fc + 1) * 128], id_bf[:]),
                            wait=([(f_xl[par], xnd)] + r_ptx.waits())
                            if fc == 0 else [],
                            inc=pe_ft if fc == NFC - 1 else None)
                    r_xn2.set(pe_ft)
                    P.step("dv", lambda v, tc=tc, s_=s_: v.tensor_copy(
                        Xn2T2[s_ % 2][:].rearrange("p (c f) -> p c f", f=512)
                        [:, :, tc * 128:(tc + 1) * 128],
                        pt_x[:].rearrange("p (c f) -> p c f", f=128)),
                        wait=[pe_ft]
                        + (r_Xn2Tb[s_ % 2].waits() if tc == 0 else []),
                        inc=dv_f)
                    r_ptx.set(dv_f)

                for tc in range(4):
                    emit_load(tc)
                    if tc >= 1:
                        emit_chain(tc - 1)
                emit_chain(3)
                ln2_marks[s_] = dv_f.n

            emit_ln2(0)
            if B * NB > 1:
                emit_ln2(1)
            w1_marks = []
            HCG = FF // 4
            for hg in range(4):
                for fc in range(NFC):
                    P.step("po", lambda g, fc=fc, hg=hg: g.dma_start(
                        out=w1b[:, fc * FF + hg * HCG:
                                fc * FF + (hg + 1) * HCG],
                        in_=w1_d[fc * 128:(fc + 1) * 128,
                                 hg * HCG:(hg + 1) * HCG]),
                        wait=[(d_g, w1_marks[-1])]
                        if (fc == 0 and hg > 0) else [],
                        inc=d_g, inck=16)
                w1_marks.append(d_g.n)
            w2_marks = []
            for kg in range(4):
                for kc in range(kg * (NHC // 4), (kg + 1) * (NHC // 4)):
                    P.step("po", lambda g, kc=kc, kg=kg: g.dma_start(
                        out=w2b[:, kc * D:(kc + 1) * D],
                        in_=w2_d[kc * 128:(kc + 1) * 128, :]),
                        wait=[(d_g, (w2_marks[-1] if kg > 0
                                     else w1_marks[-1]))]
                        if kc == kg * (NHC // 4) else [],
                        inc=d_g, inck=16)
                w2_marks.append(d_g.n)

            for s in range(B * NB):
                b, qb = divmod(s, NB)
                win0 = qb * 512
                if s not in ln2_marks:
                    emit_ln2(s)
                if s + 1 < B * NB and s + 1 not in ln2_marks:
                    emit_ln2(s + 1)
                xn2T_ready = ln2_marks[s]
                for tc in range(4):
                    row0 = b * T + win0 + tc * 128
                    P.step("sy", lambda sy, row0=row0, tc=tc: sy.dma_start(
                        out=x2_buf[:, tc * D:(tc + 1) * D],
                        in_=x2s[row0:row0 + 128, :]),
                        wait=[f_x2, st_out] if tc == 0 else [], inc=d_x2,
                        inck=16)
                x2_here = d_x2.n

                pp1 = [(ps_c, r_psc), (ps_d, r_psd)]
                gelu_marks2 = {}
                for hc in range(0 if skip_ffn else NHC):
                    ps, r_ps = pp1[hc % 2]
                    for fc in range(NFC):
                        P.step("pe", lambda t, fc=fc, hc=hc, ps=ps, s=s: t.matmul(
                            ps[:, :],
                            w1b[:, fc * FF + hc * 128:
                                fc * FF + (hc + 1) * 128],
                            Xn2T2[s % 2][:, fc * 512:(fc + 1) * 512],
                            start=(fc == 0), stop=(fc == NFC - 1)),
                            wait=([(dv_f, xn2T_ready),
                                   (d_g, w1_marks[(hc * 128) // HCG])]
                                  + r_ps.waits()) if fc == 0 else [],
                            inc=pe_f if fc == NFC - 1 else None)
                    mmn = pe_f.n
                    P.step("ac", lambda a, hc=hc, ps=ps: a.activation(
                        out=hT[:, hc * 512:(hc + 1) * 512], in_=ps[:, :],
                        func=AF.Gelu, bias=b1col[:, hc:hc + 1]),
                        wait=[(pe_f, mmn), (pe_f, prev_w2_end),
                              (dv_su, SETUP)], inc=ac_f)
                    gelu_marks2[hc] = ac_f.n
                    r_ps.set(ac_f)
                r_Xn2Tb[s % 2].set(pe_f)
                pp2 = [(ps_a, r_psa), (ps_b, r_psb)]
                for oc in range(0 if skip_ffn else NOC):
                    ps, r_ps = pp2[oc % 2]
                    for kc in range(NHC):
                        P.step("pe", lambda t, oc=oc, kc=kc, ps=ps: t.matmul(
                            ps[:, :],
                            w2b[:, kc * D + oc * 128:
                                kc * D + (oc + 1) * 128],
                            hT[:, kc * 512:(kc + 1) * 512],
                            start=(kc == 0), stop=(kc == NHC - 1)),
                            wait=([(ac_f, gelu_marks2[kc]),
                                   (d_g, w2_marks[kc // (NHC // 4)])]
                                  + (r_ps.waits() if kc == 0 else [])),
                            inc=pe_f if kc == NHC - 1 else None)
                    mm2 = pe_f.n
                    ypar = oc % 2
                    P.step("dv", lambda v, oc=oc, ps=ps, ypar=ypar: v.tensor_scalar(
                        out=y_bf[ypar][:], in0=ps[:, :],
                        scalar1=b2col[:, oc:oc + 1], scalar2=None, op0=ALU.add),
                        wait=[(pe_f, mm2), (dv_su, SETUP)]
                        + r_ybf[ypar].waits(), inc=dv_f)
                    yn = dv_f.n
                    r_ps.set(dv_f)
                    for tc in range(4):
                        P.step("pe", lambda t, tc=tc, ypar=ypar: t.transpose(
                            pt_y[:, tc * 128:(tc + 1) * 128],
                            y_bf[ypar][:, tc * 128:(tc + 1) * 128], id_bf[:]),
                            wait=([(dv_f, yn)] + r_pty.waits())
                            if tc == 0 else [],
                            inc=pe_ft if tc == 3 else None)
                    r_ybf[ypar].set(pe_ft)
                    trn = pe_ft.n
                    for tc in range(4):
                        P.step("dv", lambda v, tc=tc, oc=oc: v.tensor_add(
                            x2_buf[:, tc * D + oc * 128:
                                   tc * D + (oc + 1) * 128],
                            x2_buf[:, tc * D + oc * 128:
                                   tc * D + (oc + 1) * 128],
                            pt_y[:, tc * 128:(tc + 1) * 128]),
                            wait=[(pe_ft, trn), (d_x2, x2_here)]
                            if tc == 0 else [],
                            inc=dv_f if tc == 3 else None)
                    r_pty.set(dv_f)
                prev_w2_end = pe_f.n
                add_done = dv_f.n
                for tc in range(4):
                    t0 = win0 + tc * 128
                    P.step("sy", lambda sy, b=b, t0=t0, tc=tc: sy.dma_start(
                        out=y_out[b, t0:t0 + 128, :],
                        in_=x2_buf[:, tc * D:(tc + 1) * D]),
                        wait=[(dv_f, add_done)] if tc == 0 else [],
                        inc=st_out, inck=16)

            P.step("po", lambda g: g.nop(), wait=[st_out])

        with nc.Block() as block:
            P.emit(block)

    return nc


_CACHE = {}


def _get_program():
    if "nc" not in _CACHE:
        _CACHE["nc"] = build_program()
    return _CACHE["nc"]


def kernel(L0_attn_x, L0_attn_W_DQ, L0_attn_W_UQ, L0_attn_W_DKV, L0_attn_W_UKV,
           L0_attn_W_QR, L0_attn_W_KR, L0_attn_W_O, L0_attn_q_norm_w,
           L0_attn_k_norm_w, L0_ffn_w1, L0_ffn_b1, L0_ffn_w2, L0_ffn_b2):
    x = np.ascontiguousarray(np.asarray(L0_attn_x, dtype=np.float32))
    n_cores = 8
    Bs = x.shape[0] // n_cores
    shards = [np.ascontiguousarray(x[i * Bs:(i + 1) * Bs])
              for i in range(n_cores)]
    w = {
        "WDQ": L0_attn_W_DQ, "WUQ": L0_attn_W_UQ, "WDKV": L0_attn_W_DKV,
        "WUKV": L0_attn_W_UKV, "WQR": L0_attn_W_QR, "WKR": L0_attn_W_KR,
        "WO": L0_attn_W_O, "QNW": L0_attn_q_norm_w, "KNW": L0_attn_k_norm_w,
        "W1": L0_ffn_w1, "B1": L0_ffn_b1, "W2": L0_ffn_w2, "B2": L0_ffn_b2,
    }
    w = {k: np.ascontiguousarray(np.asarray(v, dtype=np.float32))
         for k, v in w.items()}
    nc = _get_program()
    in_maps = [dict(x=shards[i], **w) for i in range(n_cores)]
    res = run_bass_kernel_spmd(nc, in_maps, core_ids=list(range(n_cores)))
    return np.concatenate([r["y"] for r in res.results], axis=0)
